# version 11
# baseline (speedup 1.0000x reference)
import numpy as np
import ml_dtypes
from contextlib import ExitStack

import concourse.bass as bass
import concourse.mybir as mybir
from concourse.bass_utils import run_bass_kernel_spmd
from concourse.bass_isa import ReduceOp
from concourse.library_config import attn as ATTN_LIB

F32 = mybir.dt.float32
BF16 = mybir.dt.bfloat16
FP16 = mybir.dt.float16
I32 = mybir.dt.int32
AF = mybir.ActivationFunctionType
ALU = mybir.AluOpType
AX = mybir.AxisListType
BF = ml_dtypes.bfloat16

D = 1024
ND = 8
NH = 16
HD = 64
NPAIR = 8
OFFSETS = sorted(set(range(0, 33)) | {48, 64, 96, 128, 192, 256, 384, 512,
                                      768, 1024, 1536})
NOFF = len(OFFSETS)
HALO = 1536
RELCH = [-12, -8, -6, -4, -3, -2, -1, 0]
NCH = len(RELCH)
EPS = 1e-5
QSC = HD ** (-0.5)
GAMMAS = [1.0 - 2.0 ** (-5.0 - h) for h in range(NH)]
LNG = [float(np.log(g)) for g in GAMMAS]
FF = 4096
NFC = 32
HID = 2730
NF2 = 22
HIDP = NF2 * 128
ELEN = 1792

_ATL_SPEC = [("bqkv", 24), ("bout", 8), ("bgate", 8),
             ("bq", 8), ("bk", 8), ("bv", 8), ("bg", 8), ("bo", 8),
             ("gnw", 8), ("gnb", 8), ("n1w", 8), ("n1b", 8),
             ("n2w", 8), ("n2b", 8), ("b_in", 44), ("b_out", 8),
             ("b1", 32), ("b2", 8)]
ATL = {}
_c = 0
for _n, _w in _ATL_SPEC:
    ATL[_n] = _c
    _c += _w
NB_COLS = _c


class Prog:
    ENGINES = ("sync", "scalar", "vector", "tensor", "gpsimd")
    CSEM = {"sync": "qS", "scalar": "s", "vector": "v", "tensor": "t",
            "gpsimd": "g"}

    def __init__(self, nc, es):
        self.nc = nc
        self.es = es
        self.ops = {e: [] for e in self.ENGINES}
        self.semh = {}
        self.count = {}
        self.wmark = {e: {} for e in self.ENGINES}

    def sem(self, name):
        if name not in self.semh:
            self.semh[name] = self.es.enter_context(self.nc.semaphore(name))
            self.count[name] = 0
        return self.semh[name]

    def wait(self, eng, ev):
        if ev is None:
            return
        s, v = ev
        if v <= 0 or self.wmark[eng].get(s, 0) >= v:
            return
        self.wmark[eng][s] = v
        h = self.sem(s)
        self.ops[eng].append(lambda e, h=h, v=v: e.wait_ge(h, v))

    def op(self, eng, fn, waits=(), sem=None, amt=1):
        for ev in waits:
            self.wait(eng, ev)
        if sem is None:
            sem = self.CSEM[eng]
        h = self.sem(sem)
        self.count[sem] += amt
        v = self.count[sem]

        def run(e, fn=fn, h=h, amt=amt):
            fn(e).then_inc(h, amt)

        self.ops[eng].append(run)
        return (sem, v)

    def full(self, sem):
        return (sem, self.count.get(sem, 0))

    def barrier(self):
        evs = [self.full(s) for s in list(self.count)]
        for e in self.ENGINES:
            for ev in evs:
                self.wait(e, ev)

    def flush(self):
        if not any(self.ops[e] for e in self.ENGINES):
            return
        with self.nc.Block() as block:
            for eng in self.ENGINES:
                ops = self.ops[eng]
                if not ops:
                    continue

                def body(e, ops=ops):
                    for f in ops:
                        f(e)

                getattr(block, eng)(body)
                self.ops[eng] = []

    def finalize(self):
        self.flush()


class DmaStream:
    def __init__(self, prog, eng, name, gens=6):
        self.p = prog
        self.eng = eng
        self.names = [f"{name}{i}" for i in range(gens)]
        self.i = 0

    def issue(self, out, in_, waits=(), **kw):
        p, g = self.p, self.names[self.i % len(self.names)]
        self.i += 1
        prev = p.count.get(g, 0)
        if prev:
            p.wait(self.eng, (g, prev))
        return p.op(self.eng, lambda e: e.dma_start(out=out, in_=in_, **kw),
                    waits, sem=g, amt=16)


def build_program(nc, S=2048, upto="ffn2", n_cores=8, npair=NPAIR):
    TT = HALO + S
    NT = S // 128
    NTT = TT // 128
    NTH = HALO // 128
    NQG = S // 512 if S >= 512 else 1
    QG = S // NQG
    NKG = TT // 512

    def din(name, shape, dt=F32):
        return nc.dram_tensor(name, list(shape), dt, kind="ExternalInput").ap()

    x_own = din("x_own", [D, S])
    x_halo = din("x_halo", [D, HALO])
    wqkvT = din("wqkvT", [D, 3 * D])
    wgateT = din("wgateT", [D, D])
    woutT = din("woutT", [D, D])
    posb = din("posb", [NOFF, NH])
    posb_rep = din("posb_rep", [128, NH * NOFF])
    ind_bnd = din("ind_bnd", [NOFF, S])
    bias_atlas = din("bias_atlas", [128, NB_COLS])
    w1T = din("w1T", [D, FF])
    w2T = din("w2T", [FF, D])
    wq_b = din("wq_b", [D, D])
    wk_b = din("wk_b", [D, D])
    wv_b = din("wv_b", [D, D])
    wg_b = din("wg_b", [D, D])
    wo_b = din("wo_b", [D, D])
    w_inT = din("w_inT", [D, 2 * HIDP])
    w_outT = din("w_outT", [HIDP, D])
    coef_row = din("coef_row", [1, 128])
    n2wb = din("n2wb", [2, D])
    n1wb = din("n1wb", [2, D])
    gnwm = din("gnwm", [1, 2 * D])
    gnbrow = din("gnbrow", [1, D])
    y_out = nc.dram_tensor("y_out", [D, S], F32, kind="ExternalOutput").ap()

    e2_dram = nc.dram_tensor("e2_dram", [NH, 128 * (ELEN + 1)], BF16,
                             kind="Internal").ap()
    fg_dram = nc.dram_tensor("fg_dram", [ND * 128, S], BF16,
                             kind="Internal").ap()
    g_dram = nc.dram_tensor("g_dram", [NFC * 128, S], FP16, kind="Internal").ap()
    glu_dram = nc.dram_tensor("glu_dram", [NF2 * 128, S], BF16,
                              kind="Internal").ap()
    t1_dram = nc.dram_tensor("t1_dram", [D, FF], BF16, kind="Internal").ap()
    t2_dram = nc.dram_tensor("t2_dram", [FF, D], BF16, kind="Internal").ap()
    cc_in = nc.dram_tensor("cc_in", [NH * HD, HD], BF16, kind="Internal").ap()
    cc_out = nc.dram_tensor("cc_out", [n_cores * NH * HD, HD], BF16,
                            kind="Internal").ap()

    es = ExitStack()
    P = Prog(nc, es)

    def sb(name, shape, dt=F32):
        return es.enter_context(nc.sbuf_tensor(name, list(shape), dt))

    dS = DmaStream(P, "sync", "dS", 8)
    dA = DmaStream(P, "scalar", "dA", 4)
    dG = DmaStream(P, "gpsimd", "dG", 8)

    X = sb("X", [128, ND * S])
    atlas = sb("atlas", [128, NB_COLS])
    ones_col_bf = sb("ones_col_bf", [128, 1], BF16)
    ones_col32 = sb("ones_col32", [128, 1])
    ones_row = sb("ones_row", [1, 128])
    neg1_bf = sb("neg1_bf", [1, 1], BF16)
    identbf = sb("identbf", [128, 128], BF16)
    ident32 = sb("ident32", [128, 128])
    iojp = sb("iojp", [128, 128])
    maskge = sb("maskge", [128, 128])
    iorow = sb("iorow", [1, 128])
    iocol = sb("iocol", [128, 1])
    indrow32 = sb("indrow32", [1, 256])
    iocol127 = sb("iocol127", [128, 1])
    iorow1 = sb("iorow1", [1, 128])

    ev_atlas = dS.issue(atlas[:], bias_atlas)
    ev_io = P.op("gpsimd", lambda e: e.iota(
        iojp[:], [[1, 128]], base=0, channel_multiplier=-1,
        allow_small_or_imprecise_dtypes=True))
    ev_iorow = P.op("gpsimd", lambda e: e.iota(
        iorow[:], [[1, 128]], base=0, channel_multiplier=0,
        allow_small_or_imprecise_dtypes=True))
    ev_iocol = P.op("gpsimd", lambda e: e.iota(
        iocol[:], [[0, 1]], base=0, channel_multiplier=1,
        allow_small_or_imprecise_dtypes=True))
    ev_id = P.op("vector", lambda e: e.tensor_scalar(
        ident32[:], iojp[:], 0.0, None, ALU.is_equal), [ev_io])
    ev_idb = P.op("vector", lambda e: e.tensor_copy(identbf[:], ident32[:]),
                  [ev_id])
    ev_mge = P.op("vector", lambda e: e.tensor_scalar(
        maskge[:], iojp[:], 0.0, None, ALU.is_ge), [ev_io])
    P.op("vector", lambda e: e.memset(ones_col_bf[:], 1.0))
    P.op("vector", lambda e: e.memset(ones_col32[:], 1.0))
    P.op("vector", lambda e: e.memset(ones_row[:], 1.0))
    P.op("vector", lambda e: e.memset(neg1_bf[:], -1.0))
    P.op("vector", lambda e: e.memset(indrow32[:, 0:64], 1.0))
    P.op("vector", lambda e: e.memset(indrow32[:, 64:128], 0.0))
    P.op("vector", lambda e: e.memset(indrow32[:, 128:192], 0.0))
    P.op("vector", lambda e: e.memset(indrow32[:, 192:256], 1.0))
    P.op("vector", lambda e: e.tensor_scalar(
        iocol127[:], iocol[:], -127.0, None, ALU.add), [ev_iocol])
    P.op("vector", lambda e: e.tensor_scalar(
        iorow1[:], iorow[:], 1.0, None, ALU.add), [ev_iorow])

    X3 = X[:].rearrange("p (c s) -> p c s", c=ND)
    ev_X = dS.issue(X3, x_own.rearrange("(c p) s -> p c s", p=128))

    def atl(name, j):
        return atlas[:, ATL[name] + j:ATL[name] + j + 1]

    def store_out():
        ev_y = dS.issue(y_out.rearrange("(c p) s -> p c s", p=128), X3,
                        waits=[P.full("v")])
        P.wait("sync", ev_y)
        P.finalize()
        return P

    P.barrier()
    if upto == "init":
        return store_out()

    with ExitStack() as esA:
        def sbA(name, shape, dt=F32):
            return esA.enter_context(nc.sbuf_tensor(name, list(shape), dt))

        u_bf = sbA("u_bf", [128, ND * TT], BF16)
        U3 = u_bf[:].rearrange("p (c t) -> p c t", c=ND)

        with ExitStack() as esL:
            def sbL(name, shape, dt=F32):
                return esL.enter_context(nc.sbuf_tensor(name, list(shape), dt))

            def psL(name, shape, dt=F32):
                return esL.enter_context(nc.psum_tensor(name, list(shape), dt))

            x2f = [sbL(f"a_x2f{i}", [128, D]) for i in range(2)]
            xh32 = [sbL(f"a_xh{i}", [128, D]) for i in range(2)]
            t32 = [sbL(f"a_t32{i}", [128, D]) for i in range(2)]
            rows = [sbL(f"a_rows{i}", [1, 768]) for i in range(2)]
            psS = [psL(f"a_psS{i}", [1, 256]) for i in range(2)]
            psAB = [psL(f"a_psAB{i}", [128, 128]) for i in range(2)]
            psBB = [psL(f"a_psBB{i}", [128, 128]) for i in range(2)]
            prev_u = [None, None]
            xhalo3 = x_halo.rearrange("(c p) t -> p c t", p=128)
            for tt in range(NTT):
                pi = tt % 2
                w0 = [prev_u[pi]] if prev_u[pi] else []
                if tt < NTH:
                    ev_ld = dS.issue(
                        xh32[pi][:].rearrange("p (c n) -> p c n", c=ND),
                        xhalo3[:, :, tt * 128:(tt + 1) * 128], waits=w0)
                    x_ap = xh32[pi][:].rearrange("p (c n) -> p c n", c=ND)
                    w0 = w0 + [ev_ld]
                else:
                    t0 = (tt - NTH) * 128
                    x_ap = X3[:, :, t0:t0 + 128]
                    w0 = w0 + [ev_X]
                evs = P.op("scalar", lambda e, x_ap=x_ap, pi=pi:
                           e.activation(
                    x2f[pi][:].rearrange("p (c n) -> p c n", c=ND), x_ap,
                    AF.Square, bias=0.0, scale=1.0), w0 + [P.full("v")])
                ev_m1 = ev_m2 = None
                for c in range(ND):
                    ev_m1 = P.op("tensor", lambda e, c=c, x_ap=x_ap, pi=pi:
                                 e.matmul(
                        psS[pi][0:1, 0:128], ones_col32[:],
                        x_ap[:, c], start=(c == 0), stop=(c == ND - 1)),
                        w0 + [P.full("v")])
                for c in range(ND):
                    ev_m2 = P.op("tensor", lambda e, c=c, pi=pi: e.matmul(
                        psS[pi][0:1, 128:256], ones_col32[:],
                        x2f[pi][:, c * 128:(c + 1) * 128],
                        start=(c == 0), stop=(c == ND - 1)), [evs])
                r = rows[pi]
                ev = P.op("vector", lambda e, r=r, pi=pi: e.tensor_scalar(
                    r[:, 0:128], psS[pi][0:1, 0:128], 1.0 / D, None, ALU.mult),
                    [ev_m1, ev_m2])
                ev = P.op("vector", lambda e, r=r: e.tensor_tensor(
                    r[:, 128:256], r[:, 0:128], r[:, 0:128], ALU.mult), [ev])
                ev = P.op("vector", lambda e, r=r, pi=pi:
                          e.scalar_tensor_tensor(
                    r[:, 256:384], psS[pi][0:1, 128:256], 1.0 / D,
                    r[:, 128:256], ALU.mult, ALU.subtract), [ev])
                ev = P.op("vector", lambda e, r=r: e.tensor_scalar(
                    r[:, 640:768], r[:, 256:384], EPS, None, ALU.add), [ev])
                ev = P.op("vector", lambda e, r=r: e.reciprocal(
                    r[:, 256:384], r[:, 640:768]), [ev])
                evA = P.op("scalar", lambda e, r=r: e.activation(
                    r[:, 384:512], r[:, 256:384], AF.Sqrt, bias=0.0,
                    scale=1.0), [ev])
                evN = P.op("vector", lambda e, r=r: e.scalar_tensor_tensor(
                    r[:, 512:640], r[:, 0:128], -1.0, r[:, 384:512],
                    ALU.mult, ALU.mult), [evA])
                evbA = P.op("tensor", lambda e, r=r, pi=pi: e.matmul(
                    psAB[pi][:], ones_row[:], r[:, 384:512],
                    start=True, stop=True), [evA, P.full("v")])
                evbB = P.op("tensor", lambda e, r=r, pi=pi: e.matmul(
                    psBB[pi][:], ones_row[:], r[:, 512:640],
                    start=True, stop=True), [evN])
                abc = psAB[pi][:].unsqueeze(1).broadcast_to([128, ND, 128])
                bbc = psBB[pi][:].unsqueeze(1).broadcast_to([128, ND, 128])
                ev_t = P.op("vector", lambda e, x_ap=x_ap, pi=pi, abc=abc:
                            e.tensor_tensor(
                    t32[pi][:].rearrange("p (c n) -> p c n", c=ND), x_ap, abc,
                    ALU.mult), [evbA] + w0)
                ev_u = P.op("vector", lambda e, tt=tt, pi=pi, bbc=bbc:
                            e.tensor_tensor(
                    U3[:, :, tt * 128:(tt + 1) * 128],
                    t32[pi][:].rearrange("p (c n) -> p c n", c=ND), bbc,
                    ALU.add), [evbB, ev_t])
                prev_u[pi] = ev_u
            P.barrier()
            P.flush()
        if upto == "ln0":
            return store_out()

        pb = sbA("pb", [NOFF, NH])
        expE44 = sbA("expE44", [NOFF, NH], BF16)
        indb = sbA("indb", [NOFF, S], BF16)
        with ExitStack() as esE:
            pbrep = esE.enter_context(nc.sbuf_tensor(
                "pbrep", [128, NH * NOFF], F32))
            erep = esE.enter_context(nc.sbuf_tensor(
                "erep", [128, ELEN], F32))
            ev_pb = dS.issue(pb[:], posb)
            ev_pbr = dS.issue(pbrep[:], posb_rep)
            ev_e44 = P.op("scalar", lambda e: e.activation(
                expE44[:], pb[:], AF.Exp, bias=0.0, scale=1.0), [ev_pb])
            ev_e128 = P.op("scalar", lambda e: e.activation(
                pbrep[:], pbrep[:], AF.Exp, bias=0.0, scale=1.0), [ev_pbr])
            ev_ed = None
            for h in range(NH):
                evz = P.op("vector", lambda e: e.memset(erep[:], 0.0),
                           [ev_ed] if ev_ed else [])
                evs_ = [P.op("vector", lambda e, h=h: e.tensor_copy(
                    erep[:, 127:127 + 33],
                    pbrep[:, h * NOFF:h * NOFF + 33]), [evz, ev_e128])]
                for oi in range(33, NOFF):
                    off = OFFSETS[oi]
                    evs_.append(P.op("vector", lambda e, h=h, oi=oi, off=off:
                                     e.tensor_copy(
                        erep[:, 127 + off:128 + off],
                        pbrep[:, h * NOFF + oi:h * NOFF + oi + 1]),
                        [evz, ev_e128]))
                ev_ed = dG.issue(
                    bass.AP(e2_dram.tensor, h * 128 * (ELEN + 1),
                            [[ELEN + 1, 128], [1, ELEN]]),
                    erep[:, 0:ELEN], waits=evs_)
            ev_indb = dG.issue(indb[:], ind_bnd)
            P.barrier()
            P.flush()
        if upto == "etab":
            return store_out()

        with ExitStack() as esP:
            def sbP(name, shape, dt=F32):
                return esP.enter_context(nc.sbuf_tensor(name, list(shape), dt))

            def psP(name, shape, dt=F32):
                return esP.enter_context(nc.psum_tensor(name, list(shape), dt))

            wsl = sbP("wsl", [128, 4 * ND * 128], BF16)
            q_bf = sbP("q_bf", [128, S], BF16)
            k_bf = sbP("k_bf", [128, TT], BF16)
            v_bf = sbP("v_bf", [128, TT], BF16)
            v_tm = sbP("v_tm", [128, NTT * 128], BF16)
            gate32 = sbP("gate32", [128, S])
            expM = sbP("expM", [128, 2048], BF16)
            ab = sbP("ab", [128, 2048], BF16)
            ftmp = sbP("ftmp", [1, 512])
            sufcb = sbP("sufcb", [1, 2 * S], BF16)
            negv = sbP("negv", [1, 128], BF16)
            zrec = sbP("zrec", [1, 256])
            t1b = sbP("t1b", [128, 128])
            fgp = sbP("fgp", [128, S], BF16)
            pp = psP("pp", [128, 512])
            pst = [psP(f"pst{i}", [128, 1024]) for i in range(2)]

            wqkvT3 = wqkvT.rearrange("(c p) n -> p c n", p=128)
            wgateT3 = wgateT.rearrange("(c p) n -> p c n", p=128)
            wsl4 = wsl[:].rearrange("p (w c n) -> p w c n", w=4, c=ND)
            fgd3 = fg_dram.rearrange("(c p) s -> p c s", p=128)

            eve_prev = None
            ev_exp_prev = [None, None]
            ev_fg_prev = None
            ev_zr_prev = [None, None]
            ev_t1_prev = None
            last_gate = None
            for p in range(npair):
                tguard = [P.full("t")]
                ev_w = []
                for wi, (src3, c0) in enumerate(
                        [(wqkvT3, 128 * p), (wqkvT3, D + 128 * p),
                         (wqkvT3, 2 * D + 128 * p), (wgateT3, 128 * p)]):
                    ev_w.append(dG.issue(wsl4[:, wi],
                                         src3[:, :, c0:c0 + 128],
                                         waits=tguard))
                if upto == "wsl":
                    P.barrier()
                    P.flush()
                    break

                def proj(dst_ap, wi, tok0, ntok, bias_ap, scale, ev_extra,
                         act=None):
                    evs_out = []
                    ng = ntok // 512 if ntok >= 512 else 1
                    gs = ntok // ng
                    evo = None
                    for g in range(ng):
                        evm = None
                        for c in range(ND):
                            evm = P.op("tensor",
                                       lambda e, c=c, g=g, wi=wi, tok0=tok0,
                                       gs=gs: e.matmul(
                                           pp[:, 0:gs],
                                           wsl4[:, wi, c],
                                           U3[:, c, tok0 + g * gs:
                                              tok0 + (g + 1) * gs],
                                           start=(c == 0), stop=(c == ND - 1)),
                                       [ev_w[wi], P.full("v")] + ev_extra +
                                       ([evo] if evo else []))
                        if act is None:
                            evo = P.op("vector",
                                       lambda e, g=g, gs=gs: e.tensor_scalar(
                                           dst_ap[:, g * gs:(g + 1) * gs],
                                           pp[:, 0:gs], bias_ap, scale,
                                           ALU.add, ALU.mult), [evm])
                        else:
                            evo = P.op("scalar",
                                       lambda e, g=g, gs=gs: e.activation(
                                           dst_ap[:, g * gs:(g + 1) * gs],
                                           pp[:, 0:gs], act, bias=bias_ap,
                                           scale=1.0), [evm])
                        evs_out.append(evo)
                    return evs_out

                ev_q = proj(q_bf[:], 0, HALO, S, atl("bqkv", p), QSC,
                            [last_gate] if last_gate else [])
                if upto == "projq":
                    P.barrier()
                    P.flush()
                    break
                ev_k = proj(k_bf[:], 1, 0, TT, atl("bqkv", 8 + p), 1.0, [])
                ev_v = proj(v_bf[:], 2, 0, TT, atl("bqkv", 16 + p), 1.0, [])
                if upto == "projv":
                    P.barrier()
                    P.flush()
                    break
                ev_g = proj(gate32[:], 3, HALO, S, atl("bgate", p), 1.0, [],
                            act=AF.Sigmoid)
                last_gate = ev_g[-1]
                if upto == "proj":
                    P.barrier()
                    P.flush()
                    break

                esPre = ExitStack()
                ptr = esPre.enter_context(
                    nc.psum_tensor(f"ptr{p}", [128, 128], BF16))
                psuf = esPre.enter_context(
                    nc.psum_tensor(f"psuf{p}", [1, 1024], F32))
                ev_vt = []
                for a in range(NTT):
                    evt = P.op("tensor", lambda e, a=a, ptr=ptr: e.transpose(
                        ptr[:], v_bf[:, a * 128:(a + 1) * 128], identbf[:]),
                        [ev_v[min(a * 128 // (TT // len(ev_v)),
                                  len(ev_v) - 1)], ev_idb, P.full("v")])
                    ev_vt.append(P.op("vector",
                                      lambda e, a=a, ptr=ptr: e.tensor_copy(
                        v_tm[:, a * 128:(a + 1) * 128], ptr[:]), [evt]))
                ev_vt_all = P.full("v")
                if upto == "vtr":
                    P.barrier()
                    P.flush()
                    esPre.close()
                    break

                ev_eM = []
                for hb in (0, 1):
                    h = 2 * p + hb
                    for ci, rel in enumerate(RELCH):
                        off0 = h * 128 * (ELEN + 1) + 127 - 128 * rel
                        ev_eM.append(dA.issue(
                            expM[:, hb * 1024 + ci * 128:
                                 hb * 1024 + (ci + 1) * 128],
                            bass.AP(e2_dram.tensor, off0,
                                    [[ELEN, 128], [1, 128]]),
                            waits=[P.full("v")]))
                if upto == "expm":
                    P.barrier()
                    P.flush()
                    esPre.close()
                    break

                ev_sf = []
                for hb in (0, 1):
                    h = 2 * p + hb
                    ho = hb * 64
                    for g in range(NQG):
                        gs = QG
                        evm1 = P.op("tensor", lambda e, h=h, g=g, gs=gs, psuf=psuf:
                                    e.matmul(
                            psuf[0:1, 0:gs], expE44[:, h:h + 1],
                            indb[:, g * gs:(g + 1) * gs],
                            start=True, stop=True),
                            [ev_e44, ev_indb, P.full("v")] +
                            ([eve_prev] if eve_prev else []))
                        evm2 = P.op("tensor", lambda e, ho=ho, g=g, gs=gs, psuf=psuf:
                                    e.matmul(
                            psuf[0:1, 512:512 + gs],
                            k_bf[ho:ho + 64, 0:1],
                            q_bf[ho:ho + 64, g * gs:(g + 1) * gs],
                            start=True, stop=True,
                            tile_position=(ho, 0)),
                            ev_k + ev_q +
                            ([eve_prev] if eve_prev else []))
                        eve = P.op("scalar", lambda e, gs=gs, psuf=psuf:
                                   e.activation(
                            ftmp[0:1, 0:gs], psuf[0:1, 512:512 + gs], AF.Exp,
                            bias=0.0, scale=1.0), [evm2, P.full("v")])
                        eve_prev = eve
                        ev_sf.append(P.op("vector", lambda e, g=g, gs=gs, psuf=psuf,
                                          hb=hb: e.tensor_tensor(
                            sufcb[0:1, hb * S + g * gs:hb * S + (g + 1) * gs],
                            ftmp[0:1, 0:gs], psuf[0:1, 0:gs], ALU.mult),
                            [eve, evm1]))
                ev_negv = P.op("vector", lambda e: e.tensor_scalar(
                    negv[0:1, :], v_tm[0:1, 0:128], -1.0, None, ALU.mult),
                    [ev_vt[0]])
                P.flush()
                esPre.close()
                if upto == "bnd":
                    break
                esTile = ExitStack()
                pav = esTile.enter_context(
                    nc.psum_tensor(f"pav{p}", [128, 128], F32))
                pzp = esTile.enter_context(
                    nc.psum_tensor(f"pzp{p}", [1, 256], F32))
                pzb = esTile.enter_context(
                    nc.psum_tensor(f"pzb{p}", [128, 128], F32))

                for t in range(NT):
                    ev_zmm2 = [None, None]
                    ev_av = [None, None]
                    for hb in (0, 1):
                        h = 2 * p + hb
                        ho = hb * 64
                        tp = (ho, 0)
                        evm = []
                        for ci, rel in enumerate(RELCH):
                            a = NTH + t + rel
                            evm.append(P.op("tensor",
                                            lambda e, ci=ci, a=a, hb=hb,
                                            ho=ho, t=t: e.matmul(
                                pst[hb][:, ci * 128:(ci + 1) * 128],
                                k_bf[ho:ho + 64, a * 128:(a + 1) * 128],
                                q_bf[ho:ho + 64, t * 128:(t + 1) * 128],
                                start=True, stop=True, tile_position=(ho, 0)),
                                ev_k + ev_q +
                                ([ev_exp_prev[hb]] if ev_exp_prev[hb]
                                 else [])))
                        ev_exp = P.op("scalar", lambda e, hb=hb: e.activation(
                            ab[:, hb * 1024:(hb + 1) * 1024], pst[hb][:],
                            AF.Exp, bias=0.0, scale=1.0),
                            evm + ([ev_fg_prev] if ev_fg_prev else []))
                        ev_exp_prev[hb] = ev_exp
                        ev_ab = P.op("vector", lambda e, hb=hb:
                                     e.tensor_tensor(
                            ab[:, hb * 1024:(hb + 1) * 1024],
                            ab[:, hb * 1024:(hb + 1) * 1024],
                            expM[:, hb * 1024:(hb + 1) * 1024], ALU.mult),
                            [ev_exp] + ev_eM)
                        for ci in range(NCH):
                            P.op("tensor", lambda e, ci=ci, hb=hb, pzp=pzp: e.matmul(
                                pzp[0:1, hb * 128:(hb + 1) * 128],
                                ones_col_bf[:],
                                ab[:, hb * 1024 + ci * 128:
                                   hb * 1024 + (ci + 1) * 128],
                                start=(ci == 0), stop=False),
                                [ev_ab] + ([ev_zr_prev[hb]]
                                           if ev_zr_prev[hb] else []))
                        ev_zmm = P.op("tensor", lambda e, hb=hb, t=t, pzp=pzp: e.matmul(
                            pzp[0:1, hb * 128:(hb + 1) * 128], neg1_bf[:],
                            sufcb[0:1, hb * S + t * 128:hb * S + (t + 1) * 128],
                            start=False, stop=True), ev_sf)
                        ev_zmm2[hb] = ev_zmm
                        for ci, rel in enumerate(RELCH):
                            a = NTH + t + rel
                            P.op("tensor", lambda e, ci=ci, a=a, ho=ho, hb=hb,
                                 pav=pav: e.matmul(
                                pav[ho:ho + 64, :],
                                v_tm[:, a * 128 + ho:a * 128 + ho + 64],
                                ab[:, hb * 1024 + ci * 128:
                                   hb * 1024 + (ci + 1) * 128],
                                start=(ci == 0), stop=False,
                                tile_position=(0, ho)),
                                [ev_ab, ev_vt_all] +
                                ([ev_t1_prev] if ev_t1_prev else []))
                        ev_av[hb] = P.op("tensor", lambda e, ho=ho, hb=hb, t=t,
                                         pav=pav: e.matmul(
                            pav[ho:ho + 64, :], negv[0:1, ho:ho + 64],
                            sufcb[0:1, hb * S + t * 128:
                                  hb * S + (t + 1) * 128],
                            start=False, stop=True, tile_position=(0, ho)),
                            [ev_negv] + ev_sf)
                    ev_zr_all = P.op("vector", lambda e, pzp=pzp: e.reciprocal(
                        zrec[0:1, 0:256], pzp[0:1, 0:256]),
                        [ev_zmm2[0], ev_zmm2[1]])
                    ev_zr_prev[0] = ev_zr_prev[1] = ev_zr_all
                    ev_zb = P.op("tensor", lambda e, pzb=pzb: e.matmul(
                        pzb[:], indrow32[0:1, 0:128], zrec[0:1, 0:128],
                        start=True, stop=False),
                        [ev_zr_all] +
                        ([ev_fg_prev] if ev_fg_prev else []))
                    ev_zb = P.op("tensor", lambda e, pzb=pzb: e.matmul(
                        pzb[:], indrow32[0:1, 128:256], zrec[0:1, 128:256],
                        start=False, stop=True), [])
                    ev_t1 = P.op("vector", lambda e, t=t, pav=pav: e.tensor_tensor(
                        t1b[:], pav[:], gate32[:, t * 128:(t + 1) * 128],
                        ALU.mult), [ev_av[0], ev_av[1]] + ev_g)
                    ev_t1_prev = ev_t1
                    ev_fg_prev = P.op("vector", lambda e, t=t,
                                      pzb=pzb: e.tensor_tensor(
                        fgp[:, t * 128:(t + 1) * 128], t1b[:], pzb[:],
                        ALU.mult), [ev_t1, ev_zb])
                    if upto == "tile1":
                        break
                ev_fgs = dS.issue(fgd3[:, p], fgp[:], waits=[P.full("v")])
                P.wait("vector", ev_fgs)
                P.flush()
                esTile.close()
            P.barrier()

            P.barrier()
            P.flush()
        if upto in ("pairs", "wsl", "proj", "projq", "projv", "vtr", "expm", "bnd", "tile1"):
            return store_out()

        with ExitStack() as esW:
            wob = esW.enter_context(nc.sbuf_tensor("wob", [128, ND * D],
                                                   BF16))
            fgall = [esW.enter_context(nc.sbuf_tensor(
                f"fgall{i}", [128, ND * 512], BF16)) for i in range(2)]
            pwoA = [esW.enter_context(nc.psum_tensor(f"pwoA{i}", [128, 512],
                                                     F32)) for i in range(2)]
            fgd3 = fg_dram.rearrange("(c p) s -> p c s", p=128)
            wob4 = wob[:].rearrange("p (c n) -> p c n", c=ND)
            ev_wo = dG.issue(wob4, woutT.rearrange("(c p) n -> p c n", p=128))
            prev_use = [None, None]
            rd_prev = [None, None]
            for g in range(NQG):
                pi = g % 2
                w0 = [prev_use[pi]] if prev_use[pi] else []
                evl = dS.issue(fgall[pi][:].rearrange("p (c n) -> p c n",
                                                      c=ND),
                               fgd3[:, :, g * QG:(g + 1) * QG], waits=w0)
                evm = None
                for dd in range(ND):
                    bi = dd % 2
                    for c in range(ND):
                        evm = P.op("tensor", lambda e, c=c, dd=dd, g=g, pi=pi,
                                   bi=bi: e.matmul(
                            pwoA[bi][:],
                            wob4[:, c, dd * 128:(dd + 1) * 128],
                            fgall[pi][:, c * 512:(c + 1) * 512],
                            start=(c == 0), stop=(c == ND - 1)),
                            [ev_wo, evl] +
                            ([rd_prev[bi]] if c == 0 and rd_prev[bi]
                             else []))
                    rd_prev[bi] = P.op("vector", lambda e, dd=dd, g=g, bi=bi:
                                       e.scalar_tensor_tensor(
                        X3[:, dd, g * QG:(g + 1) * QG], pwoA[bi][:],
                        atl("bout", dd), X3[:, dd, g * QG:(g + 1) * QG],
                        ALU.add, ALU.add), [evm, ev_atlas])
                prev_use[pi] = evm
            P.barrier()
            P.flush()

    if upto == "attn_a":
        return store_out()

    mwsb = sb("mwsb", [128, 2])

    def ternarize_weights():
        with ExitStack() as esT:
            def sbT(name, shape, dt=F32):
                return esT.enter_context(nc.sbuf_tensor(name, list(shape), dt))

            wch = [sbT(f"t_wch{i}", [128, FF]) for i in range(2)]
            gtb = sbT("t_gtb", [128, FF], BF16)
            ltb = sbT("t_ltb", [128, FF], BF16)
            ternb = sbT("t_ternb", [128, FF], BF16)
            asum = sbT("t_asum", [128, 40])
            wsi = sbT("t_wsi", [128, 1])
            pts1 = esT.enter_context(nc.psum_tensor("t_pts1", [1, 128], F32))
            pts2 = esT.enter_context(nc.psum_tensor("t_pts2", [128, 128],
                                                    F32))
            for wi, (wsrc, tdst, C, NR) in enumerate(
                    [(w1T, t1_dram, FF, ND), (w2T, t2_dram, D, NFC)]):
                src3 = wsrc.rearrange("(r p) c -> p r c", p=128)
                dst3 = tdst.rearrange("(r p) c -> p r c", p=128)
                prev = [None, None]
                for r in range(NR):
                    pi = r % 2
                    w0 = [prev[pi]] if prev[pi] else []
                    evl = dS.issue(wch[pi][:, 0:C], src3[:, r], waits=w0)
                    prev[pi] = P.op("vector", lambda e, pi=pi, r=r, C=C:
                                    e.tensor_reduce(
                        asum[:, r:r + 1], wch[pi][:, 0:C], AX.X, ALU.add,
                        apply_absolute_value=True), [evl])
                ev = P.op("vector", lambda e, NR=NR: e.tensor_reduce(
                    asum[:, 32:33], asum[:, 0:NR], AX.X, ALU.add),
                    [prev[0], prev[1]])
                ev = P.op("tensor", lambda e: e.matmul(
                    pts1[0:1, 0:1], ones_col32[:], asum[:, 32:33],
                    start=True, stop=True), [ev, P.full("v")])
                ev = P.op("vector", lambda e: e.tensor_copy(
                    asum[0:1, 33:34], pts1[0:1, 0:1]), [ev])
                ev = P.op("tensor", lambda e: e.matmul(
                    pts2[:, 0:1], ones_row[:], asum[0:1, 33:34],
                    start=True, stop=True), [ev])
                ev = P.op("vector", lambda e, C=C, NR=NR, wi=wi:
                          e.tensor_scalar(
                    mwsb[:, wi:wi + 1], pts2[:, 0:1], 1.0 / (128 * NR * C),
                    1e-5, ALU.mult, ALU.max), [ev])
                ev_wsi = P.op("vector", lambda e, wi=wi: e.reciprocal(
                    wsi[:], mwsb[:, wi:wi + 1]), [ev])
                prev_st = [None, None]
                ev_tst = None
                for r in range(NR):
                    pi = r % 2
                    w0 = [prev_st[pi]] if prev_st[pi] else []
                    evl = dS.issue(wch[pi][:, 0:C], src3[:, r],
                                   waits=w0 + [ev_wsi])
                    eg = P.op("vector", lambda e, pi=pi, C=C: e.tensor_scalar(
                        gtb[:, 0:C], wch[pi][:, 0:C], wsi[:], 0.5,
                        ALU.mult, ALU.is_gt), [evl])
                    el = P.op("vector", lambda e, pi=pi, C=C: e.tensor_scalar(
                        ltb[:, 0:C], wch[pi][:, 0:C], wsi[:], -0.5,
                        ALU.mult, ALU.is_lt), [evl])
                    prev_st[pi] = el
                    et = P.op("vector", lambda e, C=C: e.tensor_tensor(
                        ternb[:, 0:C], gtb[:, 0:C], ltb[:, 0:C],
                        ALU.subtract), [eg, el] +
                        ([ev_tst] if ev_tst else []))
                    ev_tst = dS.issue(dst3[:, r], ternb[:, 0:C], waits=[et])
            P.barrier()
            P.flush()

    def ffn_phase(tag):
        with ExitStack() as esF:
            def sbF(name, shape, dt=F32):
                return esF.enter_context(nc.sbuf_tensor(name, list(shape), dt))

            sgrow = sbF(f"{tag}_sg", [1, S])
            am2row = sbF(f"{tag}_am2", [1, S])
            rowB = sbF(f"{tag}_rowB", [1, S])
            gd3 = g_dram.rearrange("(f p) s -> p f s", p=128)

            with ExitStack() as esA1:
                def sbA1(name, shape, dt=F32):
                    return esA1.enter_context(nc.sbuf_tensor(name,
                                                             list(shape), dt))
                qg = sbA1(f"{tag}_qg", [128, ND * 512], BF16)
                Qg3 = qg[:].rearrange("p (c n) -> p c n", c=ND)
                c1bg = sbA1(f"{tag}_c1bg", [128, 512])
                rg = sbA1(f"{tag}_rg", [1, 3 * 512])
                rsc = sbA1(f"{tag}_rsc", [1, 768])
                x2f = [sbA1(f"{tag}_x2f{i}", [128, D]) for i in range(2)]
                racc = sbA1(f"{tag}_racc", [128, 128])
                rtmp = sbA1(f"{tag}_rtmp", [128, 128])
                h32 = sbA1(f"{tag}_h32", [128, 512])
                gbf = sbA1(f"{tag}_gbf", [128, 512], FP16)
                g2b = sbA1(f"{tag}_g2b", [128, 512], FP16)
                acc2 = sbA1(f"{tag}_acc2", [128, 512])
                acc2n = sbA1(f"{tag}_acc2n", [128, 512])
                tg = [sbA1(f"{tag}_tg{i}", [128, 512]) for i in range(2)]
                qi32 = [sbA1(f"{tag}_qi{i}", [128, 512], I32)
                        for i in range(2)]
                t1b = esA1.enter_context(
                    nc.sbuf_tensor(f"{tag}_t1b", [128, ND * FF], BF16))
                t13 = t1b[:].rearrange("p (c f) -> p c f", c=ND)
                ev_t1 = dS.issue(t13,
                                 t1_dram.rearrange("(c p) f -> p c f", p=128))
                psS = esA1.enter_context(nc.psum_tensor(f"{tag}_psS",
                                                        [1, 256], F32))
                pmB = esA1.enter_context(nc.psum_tensor(f"{tag}_pmB",
                                                        [128, 128], F32))
                pBB = esA1.enter_context(nc.psum_tensor(f"{tag}_pBB",
                                                        [128, 512], F32))
                psHr = [esA1.enter_context(nc.psum_tensor(
                    f"{tag}_psH{i}", [128, 512], F32)) for i in range(2)]
                pG = esA1.enter_context(nc.psum_tensor(f"{tag}_pG",
                                                       [1, 512], F32))
                pTT = esA1.enter_context(nc.psum_tensor(f"{tag}_pTT",
                                                        [128, 128], F32))
                pTr = esA1.enter_context(nc.psum_tensor(f"{tag}_pTr",
                                                        [1, 128], F32))
                ev_gst = None
                ev_pgrd = None
                ev_epi_bank = [None, None]
                ev_epi = None
                ev_gg = None
                ev_sq = None
                ev_sq2 = None
                ev_accT = None
                for g in range(NQG):
                    for it in range(4):
                        tt = g * 4 + it
                        pi = it % 2
                        x_ap = X3[:, :, tt * 128:(tt + 1) * 128]
                        evs = P.op("scalar", lambda e, x_ap=x_ap, pi=pi:
                                   e.activation(
                            x2f[pi][:].rearrange("p (c n) -> p c n", c=ND),
                            x_ap, AF.Square, bias=0.0, scale=1.0),
                            [P.full("v"), P.full("t")])
                        ev_m1 = ev_m2 = None
                        for c in range(ND):
                            ev_m1 = P.op("tensor", lambda e, c=c, x_ap=x_ap:
                                         e.matmul(
                                psS[0:1, 0:128], ones_col32[:],
                                x_ap[:, c], start=(c == 0),
                                stop=(c == ND - 1)), [P.full("v")])
                        for c in range(ND):
                            ev_m2 = P.op("tensor", lambda e, c=c, pi=pi:
                                         e.matmul(
                                psS[0:1, 128:256], ones_col32[:],
                                x2f[pi][:, c * 128:(c + 1) * 128],
                                start=(c == 0), stop=(c == ND - 1)), [evs])
                        ev = P.op("vector", lambda e: e.tensor_scalar(
                            rsc[:, 0:128], psS[0:1, 0:128], 1.0 / D, None,
                            ALU.mult), [ev_m1, ev_m2])
                        ev = P.op("vector", lambda e: e.tensor_tensor(
                            rsc[:, 128:256], rsc[:, 0:128], rsc[:, 0:128],
                            ALU.mult), [ev])
                        ev = P.op("vector", lambda e: e.scalar_tensor_tensor(
                            rsc[:, 256:384], psS[0:1, 128:256], 1.0 / D,
                            rsc[:, 128:256], ALU.mult, ALU.subtract), [ev])
                        ev = P.op("vector", lambda e: e.tensor_scalar(
                            rsc[:, 384:512], rsc[:, 256:384], EPS, None,
                            ALU.add), [ev])
                        ev = P.op("vector", lambda e: e.reciprocal(
                            rsc[:, 384:512], rsc[:, 384:512]), [ev])
                        evA1 = P.op("scalar", lambda e: e.activation(
                            rsc[:, 512:640], rsc[:, 384:512], AF.Sqrt,
                            bias=0.0, scale=1.0), [ev])
                        ev = P.op("vector", lambda e: e.tensor_tensor(
                            rsc[:, 128:256], rsc[:, 256:384], rsc[:, 384:512],
                            ALU.mult), [evA1])
                        ev = P.op("vector", lambda e: e.tensor_scalar(
                            rsc[:, 128:256], rsc[:, 128:256], EPS, None,
                            ALU.add), [ev])
                        ev = P.op("vector", lambda e: e.reciprocal(
                            rsc[:, 128:256], rsc[:, 128:256]), [ev])
                        evA2 = P.op("scalar", lambda e: e.activation(
                            rsc[:, 128:256], rsc[:, 128:256], AF.Sqrt,
                            bias=0.0, scale=1.0), [ev])
                        evCC = P.op("vector", lambda e: e.tensor_tensor(
                            rsc[:, 512:640], rsc[:, 512:640], rsc[:, 128:256],
                            ALU.mult), [evA2])
                        evMC = P.op("vector", lambda e: e.scalar_tensor_tensor(
                            rsc[:, 384:512], rsc[:, 0:128], -1.0,
                            rsc[:, 512:640], ALU.mult, ALU.mult), [evCC])
                        eva = evb_ = None
                        for c in range(ND):
                            if c == 0:
                                eva = P.op("vector", lambda e, x_ap=x_ap:
                                           e.tensor_copy(
                                    racc[:], x_ap[:, 0]), [P.full("t")])
                                evb_ = P.op("vector", lambda e, x_ap=x_ap:
                                            e.tensor_copy(
                                    rtmp[:], x_ap[:, 0]), [P.full("t")])
                            else:
                                eva = P.op("vector", lambda e, c=c,
                                           x_ap=x_ap: e.tensor_tensor(
                                    racc[:], racc[:], x_ap[:, c], ALU.max),
                                    [eva])
                                evb_ = P.op("vector", lambda e, c=c,
                                            x_ap=x_ap: e.tensor_tensor(
                                    rtmp[:], rtmp[:], x_ap[:, c], ALU.min),
                                    [evb_])
                        evT = P.op("tensor", lambda e: e.transpose(
                            pTT[:], racc[:], ident32[:]),
                            [eva, ev_id, P.full("v")])
                        evR = P.op("vector", lambda e: e.tensor_reduce(
                            racc[:, 0:1], pTT[:], AX.X, ALU.max), [evT])
                        evT2 = P.op("tensor", lambda e: e.transpose(
                            pTr[0:1, 0:128], racc[:, 0:1], ident32[:]),
                            [evR, P.full("v")])
                        evmx = P.op("vector", lambda e: e.scalar_tensor_tensor(
                            rsc[:, 640:768], rsc[:, 0:128], -1.0,
                            pTr[0:1, 0:128], ALU.mult, ALU.add), [evT2, ev])
                        evT = P.op("tensor", lambda e: e.transpose(
                            pTT[:], rtmp[:], ident32[:]),
                            [evb_, P.full("v")])
                        evR = P.op("vector", lambda e: e.tensor_reduce(
                            rtmp[:, 0:1], pTT[:], AX.X, ALU.min), [evT])
                        evT2 = P.op("tensor", lambda e: e.transpose(
                            pTr[0:1, 0:128], rtmp[:, 0:1], ident32[:]),
                            [evR, P.full("v")])
                        evmn = P.op("vector", lambda e: e.scalar_tensor_tensor(
                            rsc[:, 0:128], pTr[0:1, 0:128], -1.0,
                            rsc[:, 0:128], ALU.mult, ALU.add), [evT2, evmx])
                        ev = P.op("vector", lambda e: e.tensor_tensor(
                            rsc[:, 640:768], rsc[:, 640:768], rsc[:, 0:128],
                            ALU.max), [evmn])
                        ev = P.op("vector", lambda e: e.tensor_tensor(
                            rsc[:, 640:768], rsc[:, 640:768], rsc[:, 512:640],
                            ALU.mult), [ev, evCC])
                        ev = P.op("vector", lambda e: e.tensor_scalar(
                            rsc[:, 0:128], rsc[:, 640:768], 1e-5, 1.0 / 127.0,
                            ALU.max, ALU.mult), [ev])
                        ev_s = P.op("vector", lambda e: e.reciprocal(
                            rsc[:, 640:768], rsc[:, 0:128]), [ev])
                        sl = slice(it * 128, (it + 1) * 128)
                        P.op("vector", lambda e, sl=sl: e.tensor_tensor(
                            rg[:, 0:512][:, sl], rsc[:, 512:640],
                            rsc[:, 640:768], ALU.mult), [ev_s])
                        evsh = P.op("vector", lambda e, sl=sl: e.tensor_tensor(
                            rg[:, 512:1024][:, sl], rsc[:, 384:512],
                            rsc[:, 640:768], ALU.mult), [ev_s, evMC])
                        P.op("vector", lambda e, sl=sl: e.tensor_scalar(
                            rg[:, 1024:1536][:, sl], rsc[:, 0:128],
                            mwsb[0:1, 0:1], None, ALU.mult),
                            [ev_s, P.full("g")])
                    evb_sb = P.op("tensor", lambda e: e.matmul(
                        pBB[:], ones_row[:], rg[:, 0:512],
                        start=True, stop=True), [P.full("v")])
                    evb_sh = P.op("tensor", lambda e: e.matmul(
                        psHr[0][:], ones_row[:], rg[:, 512:1024],
                        start=True, stop=True), [P.full("v")])
                    evc1 = None
                    for q4 in range(4):
                        evbq = P.op("tensor", lambda e, q4=q4: e.matmul(
                            pmB[:], ones_row[:],
                            rg[:, 1024 + q4 * 128:1024 + (q4 + 1) * 128],
                            start=True, stop=True),
                            [P.full("v")] + ([evc1] if evc1 else []))
                        evc1 = P.op("vector", lambda e, q4=q4: e.tensor_copy(
                            c1bg[:, q4 * 128:(q4 + 1) * 128], pmB[:]), [evbq])
                    ev_qgw = []
                    for c in range(ND):
                        pi = c % 2
                        ev1 = P.op("vector", lambda e, c=c, g=g, pi=pi:
                                   e.tensor_tensor(
                            tg[pi][:], X3[:, c, g * 512:(g + 1) * 512],
                            pBB[:], ALU.mult), [evb_sb, P.full("v")])
                        ev2 = P.op("vector", lambda e, pi=pi: e.tensor_tensor(
                            tg[pi][:], tg[pi][:], psHr[0][:], ALU.add),
                            [ev1, evb_sh])
                        ev3 = P.op("vector", lambda e, pi=pi: e.tensor_copy(
                            qi32[pi][:], tg[pi][:]), [ev2])
                        ev_qgw.append(P.op("vector", lambda e, c=c, pi=pi:
                                           e.tensor_copy(
                            Qg3[:, c], qi32[pi][:]), [ev3]))
                    ev_ac = None
                    ev_epi_bank = [ev_qgw[ND - 1], ev_epi_bank[1]]
                    pend_pg = None
                    for f in range(NFC):
                        pb2 = f % 2
                        ph = psHr[pb2]
                        evm = None
                        for c in range(ND):
                            evm = P.op("tensor", lambda e, c=c, f=f, ph=ph:
                                       e.matmul(
                                ph[:], t13[:, c, f * 128:(f + 1) * 128],
                                Qg3[:, c], start=(c == 0),
                                stop=(c == ND - 1)),
                                [ev_t1, ev_qgw[c]] +
                                ([ev_epi_bank[pb2]]
                                 if c == 0 and ev_epi_bank[pb2] else []))
                        if pend_pg is not None:
                            ev_sq = P.op("tensor", *pend_pg)
                            pend_pg = None
                        ev_epi = P.op("vector", lambda e, ph=ph:
                                      e.tensor_tensor(
                            h32[:], ph[:], c1bg[:], ALU.mult),
                            [evm] + ([ev_epi] if ev_epi else []) +
                            ([ev_gg] if ev_gg else []))
                        ev_epi_bank[pb2] = ev_epi
                        ev_gg = P.op("scalar", lambda e, f=f: e.activation(
                            gbf[:], h32[:], AF.Gelu, bias=atl("b1", f),
                            scale=1.0), [ev_epi, ev_atlas] +
                            ([ev_gst] if ev_gst else []) +
                            ([ev_sq2] if ev_sq2 else []))
                        ev_gst = dS.issue(gd3[:, f, g * 512:(g + 1) * 512],
                                          gbf[:], waits=[ev_gg])
                        ev_sq2 = P.op("scalar", lambda e: e.activation(
                            g2b[:], gbf[:], AF.Square, bias=0.0, scale=1.0),
                            [ev_gg] + ([ev_sq] if ev_sq else []))
                        pend_pg = (
                            lambda e, f=f: e.matmul(
                                pG[0:1, :], ones_col_bf[:], g2b[:],
                                start=(f == 0), stop=(f == NFC - 1)),
                            [ev_sq2] + ([ev_pgrd] if f == 0 and ev_pgrd
                                        else []))
                        if f == 0:
                            ev_ac = P.op("vector", lambda e: e.tensor_copy(
                                acc2[:], gbf[:]), [ev_gg] +
                                ([ev_accT] if ev_accT else []))
                            ev_ac2 = P.op("vector", lambda e: e.tensor_copy(
                                acc2n[:], gbf[:]), [ev_gg] +
                                ([ev_accT] if ev_accT else []))
                        else:
                            ev_ac = P.op("vector", lambda e: e.tensor_tensor(
                                acc2[:], acc2[:], gbf[:], ALU.max),
                                [ev_gg, ev_ac])
                            ev_ac2 = P.op("vector", lambda e: e.tensor_tensor(
                                acc2n[:], acc2n[:], gbf[:], ALU.min),
                                [ev_gg, ev_ac2])
                    if pend_pg is not None:
                        ev_sq = P.op("tensor", *pend_pg)
                        pend_pg = None
                    ev_cmb = P.op("vector", lambda e: e.scalar_tensor_tensor(
                        acc2[:], acc2n[:], -1.0, acc2[:], ALU.mult, ALU.max),
                        [ev_ac, ev_ac2])
                    for b4 in range(4):
                        evT = P.op("tensor", lambda e, b4=b4: e.transpose(
                            pTT[:], acc2[:, b4 * 128:(b4 + 1) * 128],
                            ident32[:]), [ev_cmb, ev_id, P.full("v")])
                        ev_accT = evT
                        evR = P.op("vector", lambda e: e.tensor_reduce(
                            rtmp[:, 0:1], pTT[:], AX.X, ALU.max), [evT])
                        evT2 = P.op("tensor", lambda e: e.transpose(
                            pTr[0:1, 0:128], rtmp[:, 0:1], ident32[:]),
                            [evR, P.full("v")])
                        P.op("vector", lambda e, g=g, b4=b4: e.tensor_copy(
                            am2row[:, g * 512 + b4 * 128:
                                   g * 512 + (b4 + 1) * 128],
                            pTr[0:1, 0:128]), [evT2])
                    ev_pgrd = P.op("vector", lambda e, g=g: e.tensor_copy(
                        sgrow[:, g * 512:(g + 1) * 512], pG[0:1, :]), [ev_sq])
                P.barrier()
                P.flush()

            ev = P.op("vector", lambda e: e.tensor_scalar(
                sgrow[:], sgrow[:], 1.0 / FF, EPS, ALU.mult, ALU.add),
                [P.full("v"), P.full("g")])
            ev = P.op("vector", lambda e: e.reciprocal(sgrow[:], sgrow[:]),
                      [ev])
            evA3 = P.op("scalar", lambda e: e.activation(
                sgrow[:], sgrow[:], AF.Sqrt, bias=0.0, scale=1.0), [ev])
            ev = P.op("vector", lambda e: e.tensor_tensor(
                am2row[:], am2row[:], sgrow[:], ALU.mult), [evA3])
            ev = P.op("vector", lambda e: e.tensor_scalar(
                am2row[:], am2row[:], 1e-5, 1.0 / 127.0, ALU.max, ALU.mult),
                [ev])
            ev_s2 = P.op("vector", lambda e: e.reciprocal(
                rowB[:], am2row[:]), [ev])
            ev_sb2 = P.op("vector", lambda e: e.tensor_tensor(
                sgrow[:], sgrow[:], rowB[:], ALU.mult), [ev_s2])
            ev_c2 = P.op("vector", lambda e: e.tensor_scalar(
                am2row[:], am2row[:], mwsb[0:1, 1:2], None, ALU.mult),
                [ev_s2, P.full("g")])

            with ExitStack() as esB:
                t2b = esB.enter_context(
                    nc.sbuf_tensor(f"{tag}_t2b", [128, NFC * D], BF16))
                t23 = t2b[:].rearrange("p (f d) -> p f d", f=NFC)
                ev_t2 = dS.issue(t23,
                                 t2_dram.rearrange("(f p) d -> p f d", p=128),
                                 waits=[P.full("t")])
                sb2b = esB.enter_context(
                    nc.sbuf_tensor(f"{tag}_sb2b", [128, S], F32))
                c2b = esB.enter_context(
                    nc.sbuf_tensor(f"{tag}_c2b", [128, S], F32))
                with ExitStack() as esB1:
                    pBB2 = esB1.enter_context(
                        nc.psum_tensor(f"{tag}_pBB2", [128, 512], F32))
                    evp = None
                    for g in range(NQG):
                        evb = P.op("tensor", lambda e, g=g: e.matmul(
                            pBB2[:], ones_row[:],
                            sgrow[:, g * 512:(g + 1) * 512],
                            start=True, stop=True),
                            [ev_sb2] + ([evp] if evp else []))
                        evp = P.op("vector", lambda e, g=g: e.tensor_copy(
                            sb2b[:, g * 512:(g + 1) * 512], pBB2[:]), [evb])
                        evb = P.op("tensor", lambda e, g=g: e.matmul(
                            pBB2[:], ones_row[:],
                            am2row[:, g * 512:(g + 1) * 512],
                            start=True, stop=True), [ev_c2, evp])
                        evp = P.op("vector", lambda e, g=g: e.tensor_copy(
                            c2b[:, g * 512:(g + 1) * 512], pBB2[:]), [evb])
                    P.barrier()
                    P.flush()
                with ExitStack() as esB2:
                    pyB = [esB2.enter_context(nc.psum_tensor(
                        f"{tag}_pyB{i}", [128, 512], F32)) for i in range(ND)]
                    gblk = [esB2.enter_context(nc.sbuf_tensor(
                        f"{tag}_gblk{i}", [128, 512], FP16)) for i in range(2)]
                    tg2 = [esB2.enter_context(nc.sbuf_tensor(
                        f"{tag}_tg2{i}", [128, 512], F32)) for i in range(2)]
                    qi2 = [esB2.enter_context(nc.sbuf_tensor(
                        f"{tag}_qi2{i}", [128, 512], I32)) for i in range(2)]
                    q2b = [esB2.enter_context(nc.sbuf_tensor(
                        f"{tag}_q2b{i}", [128, 512], BF16)) for i in range(2)]
                    evq_prev = [None, None]
                    for g in range(NQG):
                        for fc in range(NFC):
                            pi = fc % 2
                            w0 = [evq_prev[pi]] if evq_prev[pi] else []
                            evl = dS.issue(gblk[pi][:],
                                           gd3[:, fc, g * 512:(g + 1) * 512],
                                           waits=w0)
                            ev1 = P.op("vector", lambda e, pi=pi, g=g:
                                       e.tensor_tensor(
                                tg2[pi][:], gblk[pi][:],
                                sb2b[:, g * 512:(g + 1) * 512], ALU.mult),
                                [evl] + w0 +
                                ([P.full("v")] if fc < 2 else []))
                            ev2 = P.op("vector", lambda e, pi=pi:
                                       e.tensor_copy(
                                qi2[pi][:], tg2[pi][:]), [ev1])
                            ev3 = P.op("vector", lambda e, pi=pi:
                                       e.tensor_copy(
                                q2b[pi][:], qi2[pi][:]), [ev2])
                            for dd in range(ND):
                                evq_prev[pi] = P.op(
                                    "tensor",
                                    lambda e, fc=fc, dd=dd, pi=pi: e.matmul(
                                        pyB[dd][:],
                                        t23[:, fc, dd * 128:(dd + 1) * 128],
                                        q2b[pi][:], start=(fc == 0),
                                        stop=(fc == NFC - 1)), [ev3, ev_t2])
                        for dd in range(ND):
                            ev1 = P.op("vector", lambda e, dd=dd, g=g:
                                       e.tensor_tensor(
                                tg2[0][:], pyB[dd][:],
                                c2b[:, g * 512:(g + 1) * 512], ALU.mult),
                                [evq_prev[0], evq_prev[1], P.full("v")])
                            P.op("vector", lambda e, dd=dd, g=g:
                                 e.scalar_tensor_tensor(
                                X3[:, dd, g * 512:(g + 1) * 512], tg2[0][:],
                                atl("b2", dd),
                                X3[:, dd, g * 512:(g + 1) * 512],
                                ALU.add, ALU.add), [ev1, ev_atlas])
                    P.barrier()
                    P.flush()

    ternarize_weights()
    ffn_phase("f1")
    if upto == "ffn1":
        return store_out()

    NC16 = S // 128
    k_dram = nc.dram_tensor("k_dram", [NPAIR * 128, S], BF16,
                            kind="Internal").ap()
    vt_dram = nc.dram_tensor("vt_dram", [NPAIR * 128, S], BF16,
                             kind="Internal").ap()
    gn_dram = nc.dram_tensor("gn_dram", [NPAIR * 128, S], BF16,
                             kind="Internal").ap()
    ab1_dram = nc.dram_tensor("ab1_dram", [2, S], F32, kind="Internal").ap()
    xpre_dram = nc.dram_tensor("xpre_dram", [D, S], F32,
                               kind="Internal").ap()
    kd3 = k_dram.rearrange("(r p) s -> p r s", r=NPAIR)
    vd3 = vt_dram.rearrange("(r p) s -> p r s", r=NPAIR)
    gd3b = gn_dram.rearrange("(r p) s -> p r s", r=NPAIR)

    with ExitStack() as esR:
        def sbR(name, shape, dt=F32):
            return esR.enter_context(nc.sbuf_tensor(name, list(shape), dt))

        ccrow = sbR("b_cc", [1, S])
        rows2 = sbR("b_rows2", [2, S])
        onesS = sbR("b_onesS", [1, 512])
        n1sb = sbR("b_n1", [2, D])
        gnwsb = sbR("b_gnw", [1, 2 * D])
        gnbsb = sbR("b_gnb", [1, D])
        coefsb = sbR("b_coef", [128, 128])
        indcol = sbR("b_indcol", [128, 2], BF16)
        snap = sbR("b_snap", [128, NPAIR * (NC16 + 1) * HD], BF16)
        SN4 = snap[:].rearrange("p (r c d) -> p r c d", r=NPAIR, c=NC16 + 1)
        sinall = sbR("b_sin", [128, NPAIR * HD])
        SI3 = sinall[:].rearrange("p (r d) -> p r d", r=NPAIR)
        xnp = sbR("b_xnp", [128, ND * S], BF16)
        XN3 = xnp[:].rearrange("p (c s) -> p c s", c=ND)
        xtmp_b = sbR("b_xtmp", [128, 512])
        wbuf = sbR("b_wbuf", [128, ND * 128], BF16)
        rsc = sbR("b_rsc", [1, 768])

        ev_ld = [dS.issue(n1sb[:], n1wb), dS.issue(gnwsb[:], gnwm),
                 dS.issue(gnbsb[:], gnbrow)]
        ev_xpre = dS.issue(xpre_dram.rearrange("(c p) s -> p c s", p=128),
                           X3, waits=[P.full("v")])
        P.wait("vector", ev_xpre)
        P.op("vector", lambda e: e.memset(onesS[:], 1.0))
        P.op("vector", lambda e: e.memset(rows2[0:2, :], 1.0))
        P.op("vector", lambda e: e.memset(indcol[0:64, 0:1], 1.0))
        P.op("vector", lambda e: e.memset(indcol[64:128, 0:1], 0.0))
        P.op("vector", lambda e: e.memset(indcol[0:64, 1:2], 0.0))
        P.op("vector", lambda e: e.memset(indcol[64:128, 1:2], 1.0))

        with ExitStack() as esC:
            pcf = esC.enter_context(nc.psum_tensor("b_pcf", [128, 128], F32))
            psS = esC.enter_context(nc.psum_tensor("b_psS", [1, 256], F32))
            x2f = [esC.enter_context(nc.sbuf_tensor(f"b_x2f{i}", [128, D],
                                                    F32)) for i in range(2)]
            a1row = esC.enter_context(nc.sbuf_tensor("b_a1", [1, S], F32))
            b1row = esC.enter_context(nc.sbuf_tensor("b_b1", [1, S], F32))
            ev_cf = dS.issue(coefsb[0:1, :], coef_row)
            evb = P.op("tensor", lambda e: e.matmul(
                pcf[:], ones_row[:], coefsb[0:1, :], start=True, stop=True),
                [ev_cf])
            P.op("vector", lambda e: e.tensor_copy(coefsb[:], pcf[:]), [evb])
            for tt in range(NT):
                pi = tt % 2
                x_ap = X3[:, :, tt * 128:(tt + 1) * 128]
                evs = P.op("scalar", lambda e, x_ap=x_ap, pi=pi: e.activation(
                    x2f[pi][:].rearrange("p (c n) -> p c n", c=ND), x_ap,
                    AF.Square, bias=0.0, scale=1.0),
                    [P.full("v"), P.full("t")])
                ev_m1 = ev_m2 = None
                for c in range(ND):
                    ev_m1 = P.op("tensor", lambda e, c=c, x_ap=x_ap: e.matmul(
                        psS[0:1, 0:128], ones_col32[:], x_ap[:, c],
                        start=(c == 0), stop=(c == ND - 1)), [P.full("v")])
                for c in range(ND):
                    ev_m2 = P.op("tensor", lambda e, c=c, pi=pi: e.matmul(
                        psS[0:1, 128:256], ones_col32[:],
                        x2f[pi][:, c * 128:(c + 1) * 128],
                        start=(c == 0), stop=(c == ND - 1)), [evs])
                sl = slice(tt * 128, (tt + 1) * 128)
                ev = P.op("vector", lambda e: e.tensor_scalar(
                    rsc[:, 0:128], psS[0:1, 0:128], 1.0 / D, None, ALU.mult),
                    [ev_m1, ev_m2, P.full("v")])
                ev = P.op("vector", lambda e: e.tensor_tensor(
                    rsc[:, 128:256], rsc[:, 0:128], rsc[:, 0:128], ALU.mult),
                    [ev])
                ev = P.op("vector", lambda e: e.scalar_tensor_tensor(
                    rsc[:, 256:384], psS[0:1, 128:256], 1.0 / D,
                    rsc[:, 128:256], ALU.mult, ALU.subtract), [ev])
                ev = P.op("vector", lambda e: e.tensor_scalar(
                    rsc[:, 384:512], rsc[:, 256:384], EPS, None, ALU.add),
                    [ev])
                ev = P.op("vector", lambda e: e.reciprocal(
                    rsc[:, 384:512], rsc[:, 384:512]), [ev])
                evA1 = P.op("scalar", lambda e: e.activation(
                    rsc[:, 512:640], rsc[:, 384:512], AF.Sqrt, bias=0.0,
                    scale=1.0), [ev, P.full("v")])
                P.op("vector", lambda e, sl=sl: e.tensor_copy(
                    a1row[:, sl], rsc[:, 512:640]), [evA1])
                P.op("vector", lambda e, sl=sl: e.scalar_tensor_tensor(
                    b1row[:, sl], rsc[:, 0:128], -1.0, rsc[:, 512:640],
                    ALU.mult, ALU.mult), [evA1])
                ev = P.op("vector", lambda e: e.tensor_tensor(
                    rsc[:, 128:256], rsc[:, 256:384], rsc[:, 384:512],
                    ALU.mult), [evA1])
                ev = P.op("vector", lambda e: e.tensor_scalar(
                    rsc[:, 128:256], rsc[:, 128:256], EPS, None, ALU.add),
                    [ev])
                ev = P.op("vector", lambda e: e.reciprocal(
                    rsc[:, 128:256], rsc[:, 128:256]), [ev])
                evA2 = P.op("scalar", lambda e: e.activation(
                    rsc[:, 128:256], rsc[:, 128:256], AF.Sqrt, bias=0.0,
                    scale=1.0), [ev])
                evCC = P.op("vector", lambda e, sl=sl: e.tensor_tensor(
                    ccrow[:, sl], rsc[:, 512:640], rsc[:, 128:256],
                    ALU.mult), [evA2])
                P.op("vector", lambda e, sl=sl: e.scalar_tensor_tensor(
                    rows2[0:1, sl], rsc[:, 0:128], -1.0, ccrow[:, sl],
                    ALU.mult, ALU.mult), [evCC])
            dS.issue(ab1_dram[0:1, :], a1row[:], waits=[P.full("v")])
            dS.issue(ab1_dram[1:2, :], b1row[:], waits=[P.full("v")])
            P.barrier()
            P.flush()

        def xn_all(pABx, pBBx, nrows):
            evs = []
            for c in range(ND):
                for g in range(NQG):
                    evb1 = P.op("tensor", lambda e, c=c, g=g, nrows=nrows:
                                e.matmul(
                        pABx[:], nrows[0:1, c * 128:(c + 1) * 128],
                        ccrow[:, g * 512:(g + 1) * 512], start=True,
                        stop=True), ev_ld + [P.full("v")])
                    evb2 = P.op("tensor", lambda e, c=c, g=g, nrows=nrows:
                                e.matmul(
                        pBBx[:], nrows[0:2, c * 128:(c + 1) * 128],
                        rows2[0:2, g * 512:(g + 1) * 512], start=True,
                        stop=True), [P.full("v")])
                    ev1 = P.op("vector", lambda e, c=c, g=g, pABx=pABx:
                               e.tensor_tensor(
                        xtmp_b[:], X3[:, c, g * 512:(g + 1) * 512], pABx[:],
                        ALU.mult), [evb1, P.full("v")])
                    evs.append(P.op("vector", lambda e, c=c, g=g, pBBx=pBBx:
                                    e.tensor_tensor(
                        XN3[:, c, g * 512:(g + 1) * 512], xtmp_b[:],
                        pBBx[:], ALU.add), [ev1, evb2]))
            return evs

        def projp(wsrc, col0, bias_ap, scale, dst_ap, ppb, wl=(), act=None,
                  eguard=(), bank_guard=()):
            w3 = wsrc.rearrange("(c p) n -> p c n", p=128)
            evw = dG.issue(wbuf[:].rearrange("p (c n) -> p c n", c=ND),
                           w3[:, :, col0:col0 + 128], waits=[P.full("t")])
            evs = []
            rot = ppb[0] is not ppb[1]
            nb = 2 if rot else 1
            for g in range(NQG):
                pp_ = ppb[g % 2]
                evm = None
                for c in range(ND):
                    evm = P.op("tensor", lambda e, c=c, g=g, pp_=pp_:
                               e.matmul(
                        pp_[:], wbuf[:, c * 128:(c + 1) * 128],
                        XN3[:, c, g * 512:(g + 1) * 512],
                        start=(c == 0), stop=(c == ND - 1)),
                        [evw, P.full("v")] + list(wl) +
                        ([evs[-nb]] if len(evs) >= nb else []) +
                        (list(bank_guard) if c == 0 and g < nb else []))
                if act is None:
                    evs.append(P.op("vector", lambda e, g=g, pp_=pp_:
                                    e.tensor_scalar(
                        dst_ap[:, g * 512:(g + 1) * 512], pp_[:], bias_ap,
                        scale, ALU.add, ALU.mult), [evm] + list(eguard)))
                else:
                    evs.append(P.op("scalar", lambda e, g=g, pp_=pp_:
                                    e.activation(
                        dst_ap[:, g * 512:(g + 1) * 512], pp_[:], act,
                        bias=bias_ap, scale=1.0), [evm] + list(eguard)))
            return evs

        with ExitStack() as esB1:
            def sbB1(name, shape, dt=F32):
                return esB1.enter_context(nc.sbuf_tensor(name, list(shape),
                                                         dt))
            kpair = sbB1("b_kpair", [128, S], BF16)
            vpair = sbB1("b_vpair", [128, S], BF16)
            vtm = sbB1("b_vtm", [128, S], BF16)
            ktm = sbB1("b_ktm", [128, S], BF16)
            kgt = sbB1("b_kgt", [128, 2 * HD], BF16)
            scur = sbB1("b_scur", [128, HD])
            gpcol = sbB1("b_gpcol", [128, 2])
            ppb = [esB1.enter_context(nc.psum_tensor(f"b_ppb{i}",
                                                      [128, 512], F32))
                   for i in range(2)]
            pABx = esB1.enter_context(nc.psum_tensor("b_pABx", [128, 512],
                                                     F32))
            pBBx = esB1.enter_context(nc.psum_tensor("b_pBBx", [128, 512],
                                                     F32))
            ptrb = esB1.enter_context(nc.psum_tensor("b_ptrb", [128, 128],
                                                     BF16))
            pU = esB1.enter_context(nc.psum_tensor("b_pU", [128, HD], F32))
            prev_store = []
            evxn = xn_all(pABx, pBBx, n1sb)
            for p in range(NPAIR):
                ev_k = projp(wk_b, 128 * p, atl("bk", p), 1.0, kpair[:], ppb,
                             eguard=prev_store)
                ev_v = projp(wv_b, 128 * p, atl("bv", p), 1.0, vpair[:], ppb,
                             wl=ev_k)
                ev_ks = dS.issue(kd3[:, p], kpair[:], waits=ev_k)
                ev_t = []
                for a in range(NC16):
                    evt = P.op("tensor", lambda e, a=a: e.transpose(
                        ptrb[:], vpair[:, a * 128:(a + 1) * 128], identbf[:]),
                        ev_v + [P.full("v")])
                    ev_t.append(P.op("vector", lambda e, a=a: e.tensor_copy(
                        vtm[:, a * 128:(a + 1) * 128], ptrb[:]),
                        [evt] + prev_store))
                for a in range(NC16):
                    evt = P.op("tensor", lambda e, a=a: e.transpose(
                        ptrb[:], kpair[:, a * 128:(a + 1) * 128], identbf[:]),
                        ev_k + [P.full("v")])
                    ev_t.append(P.op("vector", lambda e, a=a: e.tensor_copy(
                        ktm[:, a * 128:(a + 1) * 128], ptrb[:]), [evt]))
                ev_vs = dS.issue(vd3[:, p], vtm[:], waits=[P.full("v")])
                prev_store = [ev_ks, ev_vs]
                evg = []
                for hb in (0, 1):
                    h = 2 * p + hb
                    evg.append(P.op("scalar", lambda e, hb=hb, h=h:
                                    e.activation(
                        gpcol[:, hb:hb + 1], iocol127[:], AF.Exp,
                        bias=0.0, scale=-LNG[h]),
                        [ev_iocol, P.full("v")]))
                ev_sn0 = P.op("vector", lambda e, p=p: e.memset(
                    SN4[:, p, 0], 0.0), [P.full("t")])
                ev_sc0 = P.op("vector", lambda e: e.memset(scur[:], 0.0),
                              [P.full("t")])
                ev_prev = ev_sc0
                for c in range(NC16):
                    ev_kg = P.op("vector", lambda e, c=c: e.tensor_scalar(
                        kgt[:, 0:HD],
                        ktm[:, c * 128:(c + 1) * 128]
                        .rearrange("p (hb d) -> p hb d", hb=2)[:, 0],
                        gpcol[:, 0:1], None, ALU.mult),
                        ev_t + evg + [ev_prev])
                    ev_kg2 = P.op("vector", lambda e, c=c: e.tensor_scalar(
                        kgt[:, HD:2 * HD],
                        ktm[:, c * 128:(c + 1) * 128]
                        .rearrange("p (hb d) -> p hb d", hb=2)[:, 1],
                        gpcol[:, 1:2], None, ALU.mult), [ev_prev])
                    evu = []
                    for hb in (0, 1):
                        evu.append(P.op("tensor",
                                        lambda e, c=c, hb=hb: e.matmul(
                            pU[hb * HD:(hb + 1) * HD, :],
                            kgt[:, hb * HD:(hb + 1) * HD],
                            vtm[:, c * 128 + hb * HD:c * 128 + hb * HD + HD],
                            start=True, stop=True,
                            tile_position=(0, hb * HD)),
                            [ev_kg, ev_kg2, P.full("v")]))
                    ev_prev = None
                    for hb in (0, 1):
                        g128 = GAMMAS[2 * p + hb] ** 128
                        evs_ = P.op("vector",
                                    lambda e, hb=hb, g128=g128:
                                    e.scalar_tensor_tensor(
                            scur[hb * HD:(hb + 1) * HD, :],
                            scur[hb * HD:(hb + 1) * HD, :], g128,
                            pU[hb * HD:(hb + 1) * HD, :], ALU.mult, ALU.add),
                            evu + ([ev_prev] if ev_prev else []))
                        ev_prev = P.op("vector",
                                       lambda e, c=c, hb=hb, p=p:
                                       e.tensor_copy(
                            SN4[:, p, c + 1][hb * HD:(hb + 1) * HD, :],
                            scur[hb * HD:(hb + 1) * HD, :]), [evs_])
            P.barrier()
            P.flush()
        if upto == "bB1":
            return store_out()

        ev_ag_in = dS.issue(
            cc_in.rearrange("(pr hb d1) d2 -> (hb d1) pr d2", pr=NPAIR, hb=2),
            SN4[:, :, NC16], waits=[P.full("v")])
        ev_cc = P.op("gpsimd", lambda e: e.collective_compute(
            "AllGather", ALU.bypass, replica_groups=[list(range(n_cores))],
            ins=[cc_in], outs=[cc_out]), [ev_ag_in], sem="cc")
        with ExitStack() as esAG:
            agall = esAG.enter_context(nc.sbuf_tensor(
                "b_agall", [128, n_cores * NPAIR * HD], BF16))
            AG4 = agall[:].rearrange("p (r pr d) -> p r pr d", r=n_cores,
                                     pr=NPAIR)
            ev_agl = dS.issue(
                AG4, cc_out.rearrange("(r pr hb d1) d2 -> (hb d1) r pr d2",
                                      r=n_cores, pr=NPAIR, hb=2),
                waits=[ev_cc])
            for p in range(NPAIR):
                for hb in (0, 1):
                    h = 2 * p + hb
                    ev = None
                    for r in range(n_cores):
                        if r == 0:
                            ev = P.op("vector",
                                      lambda e, p=p, hb=hb, h=h, r=r:
                                      e.tensor_scalar(
                                SI3[:, p][hb * HD:(hb + 1) * HD, :],
                                AG4[:, r, p][hb * HD:(hb + 1) * HD, :],
                                coefsb[hb * HD:(hb + 1) * HD,
                                       r * NH + h:r * NH + h + 1],
                                None, ALU.mult), [ev_agl, P.full("t")])
                        else:
                            ev = P.op("vector",
                                      lambda e, p=p, hb=hb, h=h, r=r:
                                      e.scalar_tensor_tensor(
                                SI3[:, p][hb * HD:(hb + 1) * HD, :],
                                AG4[:, r, p][hb * HD:(hb + 1) * HD, :],
                                coefsb[hb * HD:(hb + 1) * HD,
                                       r * NH + h:r * NH + h + 1],
                                SI3[:, p][hb * HD:(hb + 1) * HD, :],
                                ALU.mult, ALU.add), [ev])
            P.barrier()
            P.flush()

        with ExitStack() as esB2:
            def sbB2(name, shape, dt=F32):
                return esB2.enter_context(nc.sbuf_tensor(name, list(shape),
                                                         dt))
            qp = sbB2("b_qp", [128, S], BF16)
            qgam = sbB2("b_qgam", [128, S], BF16)
            sg = sbB2("b_sg", [128, S])
            kpair2 = sbB2("b_kpair2", [128, S], BF16)
            vtm2 = sbB2("b_vtm2", [128, S], BF16)
            NHV = 2 if NQG >= 2 else 1
            rp = sbB2("b_rp", [128, S // NHV])
            rpb = sbB2("b_rpb", [128, 512], BF16)
            rp2b = sbB2("b_rp2b", [128, 512], BF16)
            gnp = sbB2("b_gnp", [128, S // NHV], BF16)
            stot = sbB2("b_stot", [128, HD], BF16)
            ab2 = sbB2("b_ab2", [128, 256], BF16)
            dm = sbB2("b_dm", [128, 256])
            grow = sbB2("b_grow", [1, 256])
            rgn = sbB2("b_rgn", [1, 4 * 512])
            gB = sbB2("b_gB", [128, 128])
            with ExitStack() as esPs:
                ppb2s = esPs.enter_context(nc.psum_tensor(
                    "b_ppb2", [128, 512], F32))
                ppb2 = [ppb2s, ppb2s]
                pABx2 = esPs.enter_context(nc.psum_tensor("b_pABx2",
                                                          [128, 512], F32))
                pBBx2 = esPs.enter_context(nc.psum_tensor("b_pBBx2",
                                                          [128, 512], F32))
                psCe = esPs.enter_context(nc.psum_tensor("b_psCe",
                                                          [128, 128], F32))
                psCo = esPs.enter_context(nc.psum_tensor("b_psCo",
                                                          [128, 128], F32))
                pret = esPs.enter_context(nc.psum_tensor("b_pret", [128, 128],
                                                         F32))
                pgs = esPs.enter_context(nc.psum_tensor("b_pgs", [1, 1024],
                                                        F32))
                ev_g_prev = []
                for p in range(NPAIR):
                    ev_q = projp(wq_b, 128 * p, atl("bq", p), QSC, qp[:],
                                 ppb2, bank_guard=ev_g_prev[-2:])
                    ev_g = projp(wg_b, 128 * p, atl("bg", p), 1.0, sg[:],
                                 ppb2, wl=ev_q, act=AF.Silu)
                    ev_g_prev = ev_g
                    ev_kl = dS.issue(kpair2[:], kd3[:, p],
                                     waits=[P.full("t")])
                    ev_vl = dS.issue(vtm2[:], vd3[:, p], waits=[P.full("t")])
                    evgr = []
                    for hb in (0, 1):
                        h = 2 * p + hb
                        evgr.append(P.op("scalar", lambda e, hb=hb, h=h:
                                         e.activation(
                            grow[0:1, hb * 128:(hb + 1) * 128], iorow1[:],
                            AF.Exp, bias=0.0, scale=LNG[h]),
                            [ev_iorow, P.full("v")]))
                    evgb = P.op("tensor", lambda e: e.matmul(
                        pABx2[0:128, 0:128], indrow32[0:1, 0:128],
                        grow[0:1, 0:128], start=True, stop=False),
                        evgr + [P.full("v")])
                    evgb = P.op("tensor", lambda e: e.matmul(
                        pABx2[0:128, 0:128], indrow32[0:1, 128:256],
                        grow[0:1, 128:256], start=False, stop=True), [])
                    evgbc = P.op("vector", lambda e: e.tensor_copy(
                        gB[:], pABx2[0:128, 0:128]), [evgb])
                    ev_qg = P.op("vector", lambda e: e.tensor_tensor(
                        qgam[:].rearrange("p (a n) -> p a n", n=128),
                        qp[:].rearrange("p (a n) -> p a n", n=128),
                        gB[:].unsqueeze(1).broadcast_to([128, NC16, 128]),
                        ALU.mult), ev_q + [evgbc])
                    evdm = []
                    for hb in (0, 1):
                        h = 2 * p + hb
                        eve = P.op("scalar", lambda e, hb=hb, h=h:
                                   e.activation(
                            dm[:, hb * 128:(hb + 1) * 128], iojp[:], AF.Exp,
                            bias=0.0, scale=LNG[h]), [ev_io, P.full("v")])
                        evdm.append(P.op("vector", lambda e, hb=hb:
                                         e.tensor_tensor(
                            dm[:, hb * 128:(hb + 1) * 128],
                            dm[:, hb * 128:(hb + 1) * 128], maskge[:],
                            ALU.mult), [eve, ev_mge]))
                    ev_rp_prev = None
                    ev_at_prev = None
                    ev_gn_st = None
                    for half in range(NHV):
                     for c in range(half * (NC16 // NHV),
                                    (half + 1) * (NC16 // NHV)):
                         evst = []
                         for hb in (0, 1):
                             g128c = GAMMAS[2 * p + hb] ** (128 * c)
                             evst.append(P.op("vector",
                                              lambda e, c=c, hb=hb, p=p,
                                              g128c=g128c:
                                              e.scalar_tensor_tensor(
                                 stot[hb * HD:(hb + 1) * HD, :],
                                 SI3[:, p][hb * HD:(hb + 1) * HD, :], g128c,
                                 SN4[:, p, c][hb * HD:(hb + 1) * HD, :],
                                 ALU.mult, ALU.add), [P.full("t")]))
                         evC = []
                         for hb, psC in ((0, psCe), (1, psCo)):
                             evC.append(P.op("tensor",
                                             lambda e, c=c, hb=hb, psC=psC,
                                             p=p: e.matmul(
                                 psC[:], kpair2[hb * HD:(hb + 1) * HD,
                                                c * 128:(c + 1) * 128],
                                 qp[hb * HD:(hb + 1) * HD,
                                    c * 128:(c + 1) * 128],
                                 start=True, stop=True,
                                 tile_position=(hb * HD, 0)),
                                 [ev_kl] + ev_q +
                                 ([ev_at_prev] if ev_at_prev else [])))
                         ev_at = []
                         for hb, psC in ((0, psCe), (1, psCo)):
                             ev_at.append(P.op("vector",
                                               lambda e, hb=hb, psC=psC:
                                               e.tensor_tensor(
                                 ab2[:, hb * 128:(hb + 1) * 128], psC[:],
                                 dm[:, hb * 128:(hb + 1) * 128], ALU.mult),
                                 [evC[hb]] + evdm))
                         ev_at_prev = ev_at[1]
                         evR = []
                         for hb in (0, 1):
                             evR.append(P.op("tensor",
                                             lambda e, c=c, hb=hb: e.matmul(
                                 pret[hb * HD:(hb + 1) * HD, :],
                                 vtm2[:, c * 128 + hb * HD:
                                      c * 128 + hb * HD + HD],
                                 ab2[:, hb * 128:(hb + 1) * 128],
                                 start=True, stop=False,
                                 tile_position=(0, hb * HD)),
                                 [ev_vl, ev_at[hb]] +
                                 ([ev_rp_prev] if ev_rp_prev else [])))
                             evR.append(P.op("tensor",
                                             lambda e, c=c, hb=hb: e.matmul(
                                 pret[hb * HD:(hb + 1) * HD, :],
                                 stot[hb * HD:(hb + 1) * HD, :],
                                 qgam[hb * HD:(hb + 1) * HD,
                                      c * 128:(c + 1) * 128],
                                 start=False, stop=True,
                                 tile_position=(hb * HD, hb * HD)),
                                 [evst[hb], ev_qg]))
                         ev_rp_prev = P.op("vector", lambda e, c=c:
                                           e.tensor_copy(
                             rp[:, (c % (NC16 // NHV)) * 128:(c % (NC16 // NHV) + 1) * 128], pret[:]), evR)
                     for g in range(half * (NQG // NHV),
                                    (half + 1) * (NQG // NHV)):
                         sl2 = slice(g * 512, (g + 1) * 512)
                         slh = slice((g % (NQG // NHV)) * 512,
                                     (g % (NQG // NHV) + 1) * 512)
                         evcp = P.op("vector", lambda e, slh=slh:
                                     e.tensor_copy(
                             rpb[:], rp[:, slh]), [ev_rp_prev, P.full("s")])
                         evsq = P.op("scalar", lambda e: e.activation(
                             rp2b[:], rpb[:], AF.Square, bias=0.0, scale=1.0),
                             [evcp])
                         evm = []
                         for hb in (0, 1):
                             evm.append(P.op("tensor", lambda e, hb=hb:
                                             e.matmul(
                                 pgs[0:1, hb * 512:(hb + 1) * 512],
                                 indcol[:, hb:hb + 1], rpb[:],
                                 start=True, stop=True),
                                 [evcp, P.full("v")]))
                         ev1 = P.op("vector", lambda e: e.tensor_scalar(
                             rgn[:, 0:1024], pgs[0:1, 0:1024], 1.0 / HD, None,
                             ALU.mult), evm)
                         evm2 = []
                         for hb in (0, 1):
                             evm2.append(P.op("tensor", lambda e, hb=hb:
                                              e.matmul(
                                 pgs[0:1, hb * 512:(hb + 1) * 512],
                                 indcol[:, hb:hb + 1], rp2b[:],
                                 start=True, stop=True), [evsq, ev1]))
                         ev2 = P.op("vector", lambda e: e.tensor_tensor(
                             rgn[:, 1024:2048], rgn[:, 0:1024],
                             rgn[:, 0:1024], ALU.mult), [ev1])
                         ev3 = P.op("vector", lambda e: e.scalar_tensor_tensor(
                             rgn[:, 1024:2048], pgs[0:1, 0:1024], 1.0 / HD,
                             rgn[:, 1024:2048], ALU.mult, ALU.subtract),
                             evm2 + [ev2])
                         ev4 = P.op("vector", lambda e: e.tensor_scalar(
                             rgn[:, 1024:2048], rgn[:, 1024:2048], EPS, None,
                             ALU.add), [ev3])
                         ev5 = P.op("vector", lambda e: e.reciprocal(
                             rgn[:, 1024:2048], rgn[:, 1024:2048]), [ev4])
                         ev6 = P.op("scalar", lambda e: e.activation(
                             rgn[:, 1024:2048], rgn[:, 1024:2048], AF.Sqrt,
                             bias=0.0, scale=1.0), [ev5])
                         ev7 = P.op("vector", lambda e: e.scalar_tensor_tensor(
                             rgn[:, 0:1024], rgn[:, 0:1024], -1.0,
                             rgn[:, 1024:2048], ALU.mult, ALU.mult), [ev6])
                         evb1 = P.op("tensor", lambda e, p=p: e.matmul(
                             pABx2[:], gnwsb[0:1, p * 128:(p + 1) * 128],
                             rgn[:, 1024:1536], start=True, stop=False),
                             [ev6, P.full("v")])
                         evb1 = P.op("tensor", lambda e, p=p: e.matmul(
                             pABx2[:], gnwsb[0:1, D + p * 128:
                                             D + (p + 1) * 128],
                             rgn[:, 1536:2048], start=False, stop=True), [])
                         evb2 = P.op("tensor", lambda e, p=p: e.matmul(
                             pBBx2[:], gnwsb[0:1, p * 128:(p + 1) * 128],
                             rgn[:, 0:512], start=True, stop=False),
                             [ev7, P.full("v")])
                         evb2 = P.op("tensor", lambda e, p=p: e.matmul(
                             pBBx2[:], gnwsb[0:1, D + p * 128:
                                             D + (p + 1) * 128],
                             rgn[:, 512:1024], start=False, stop=False), [])
                         evb2 = P.op("tensor", lambda e, p=p: e.matmul(
                             pBBx2[:], gnbsb[0:1, p * 128:(p + 1) * 128],
                             onesS[:], start=False, stop=True), [])
                         evap = P.op("vector", lambda e, slh=slh:
                                     e.tensor_tensor(
                             xtmp_b[:], rp[:, slh], pABx2[:], ALU.mult),
                             [evb1, P.full("v")])
                         evap = P.op("vector", lambda e: e.tensor_tensor(
                             xtmp_b[:], xtmp_b[:], pBBx2[:], ALU.add),
                             [evap, evb2])
                         evap = P.op("vector", lambda e, sl2=sl2, slh=slh:
                                     e.tensor_tensor(
                             gnp[:, slh], xtmp_b[:], sg[:, sl2], ALU.mult),
                             [evap] + ev_g)
                         ev_gn_st = dS.issue(gd3b[:, p, sl2], gnp[:, slh],
                                             waits=[evap] +
                                             ([ev_gn_st] if ev_gn_st else []))
                     P.wait("vector", ev_gn_st)
                P.barrier()
                P.flush()
        if upto == "bB2":
            return store_out()

        with ExitStack() as esWo:
            wob2 = esWo.enter_context(nc.sbuf_tensor("b_wob2",
                                                     [128, ND * D], BF16))
            gnall = [esWo.enter_context(nc.sbuf_tensor(
                f"b_gnall{i}", [128, ND * 512], BF16)) for i in range(2)]
            pwo = [esWo.enter_context(nc.psum_tensor(f"b_pwo{i}", [128, 512],
                                                     F32)) for i in range(2)]
            pA1 = esWo.enter_context(nc.psum_tensor("b_pA1", [128, 512], F32))
            pB1 = esWo.enter_context(nc.psum_tensor("b_pB1", [128, 512], F32))
            ab1sb = esWo.enter_context(nc.sbuf_tensor("b_ab1", [2, S], F32))
            sel2 = esWo.enter_context(nc.sbuf_tensor("b_sel2", [2, 128], F32))
            ev_ab1 = dS.issue(ab1sb[:], ab1_dram, waits=[P.full("v")])
            evs2 = P.op("vector", lambda e: e.memset(sel2[0:2, :], 1.0))
            evs2 = P.op("vector", lambda e: e.memset(sel2[0:1, :], 0.0),
                        [evs2])
            wo3 = wo_b.rearrange("(c p) n -> p c n", p=128)
            ev_wo = dG.issue(wob2[:].rearrange("p (c n) -> p c n", c=ND),
                             wo3, waits=[P.full("t")])
            ev3 = None
            prev_use = [None, None]
            rd_prev = [None, None]
            for g in range(NQG):
                pi = g % 2
                w0 = [prev_use[pi]] if prev_use[pi] else []
                evl = dS.issue(gnall[pi][:].rearrange("p (c n) -> p c n",
                                                      c=ND),
                               gd3b[:, :, g * 512:(g + 1) * 512], waits=w0)
                evbA = P.op("tensor", lambda e, g=g: e.matmul(
                    pA1[:], ones_row[:], ab1sb[0:1, g * 512:(g + 1) * 512],
                    start=True, stop=True), [ev_ab1, P.full("v")])
                evbB = P.op("tensor", lambda e, g=g: e.matmul(
                    pB1[:], sel2[0:2, :], ab1sb[0:2, g * 512:(g + 1) * 512],
                    start=True, stop=True), [ev_ab1, evs2, P.full("v")])
                evm = None
                for dd in range(ND):
                    bi = dd % 2
                    for c in range(ND):
                        evm = P.op("tensor", lambda e, c=c, dd=dd, bi=bi,
                                   pi=pi: e.matmul(
                            pwo[bi][:], wob2[:, c * D + dd * 128:
                                             c * D + (dd + 1) * 128],
                            gnall[pi][:, c * 512:(c + 1) * 512],
                            start=(c == 0), stop=(c == ND - 1)),
                            [ev_wo, evl] +
                            ([rd_prev[bi]] if c == 0 and rd_prev[bi]
                             else []))
                    ev1 = P.op("vector", lambda e, dd=dd, g=g:
                               e.tensor_tensor(
                        xtmp_b[:], X3[:, dd, g * 512:(g + 1) * 512], pA1[:],
                        ALU.mult), [evbA] + ([ev3] if ev3 else []))
                    ev2 = P.op("vector", lambda e: e.tensor_tensor(
                        xtmp_b[:], xtmp_b[:], pB1[:], ALU.add), [ev1, evbB])
                    ev3 = P.op("vector", lambda e, dd=dd, g=g, bi=bi:
                               e.scalar_tensor_tensor(
                        X3[:, dd, g * 512:(g + 1) * 512], pwo[bi][:],
                        atl("bo", dd), xtmp_b[:],
                        ALU.add, ALU.add), [ev2, evm, ev_atlas])
                    rd_prev[bi] = ev3
                prev_use[pi] = evm
            P.barrier()
            P.flush()
        if upto == "bwo":
            return store_out()

    with ExitStack() as esG:
        def sbG(name, shape, dt=F32):
            return esG.enter_context(nc.sbuf_tensor(name, list(shape), dt))

        ccr2 = sbG("g_cc", [1, S])
        row22 = sbG("g_rows2", [2, S])
        n2sb = sbG("g_n2", [2, D])
        x2ball = sbG("g_x2b", [128, ND * S], BF16)
        XB3 = x2ball[:].rearrange("p (c s) -> p c s", c=ND)
        xtmp_g = sbG("g_xtmp", [128, 512])
        rsc2 = sbG("g_rsc", [1, 768])
        ev_n2 = dS.issue(n2sb[:], n2wb)
        P.op("vector", lambda e: e.memset(row22[0:2, :], 1.0))
        with ExitStack() as esGs:
            psS = esGs.enter_context(nc.psum_tensor("g_psS", [1, 256], F32))
            pAB = esGs.enter_context(nc.psum_tensor("g_pAB", [128, 512], F32))
            pBB = esGs.enter_context(nc.psum_tensor("g_pBB", [128, 512], F32))
            x2f = [esGs.enter_context(nc.sbuf_tensor(f"g_x2f{i}", [128, D],
                                                     F32)) for i in range(2)]
            for tt in range(NT):
                pi = tt % 2
                x_ap = X3[:, :, tt * 128:(tt + 1) * 128]
                evs = P.op("scalar", lambda e, x_ap=x_ap, pi=pi: e.activation(
                    x2f[pi][:].rearrange("p (c n) -> p c n", c=ND), x_ap,
                    AF.Square, bias=0.0, scale=1.0),
                    [P.full("v"), P.full("t")])
                ev_m1 = ev_m2 = None
                for c in range(ND):
                    ev_m1 = P.op("tensor", lambda e, c=c, x_ap=x_ap: e.matmul(
                        psS[0:1, 0:128], ones_col32[:], x_ap[:, c],
                        start=(c == 0), stop=(c == ND - 1)), [P.full("v")])
                for c in range(ND):
                    ev_m2 = P.op("tensor", lambda e, c=c, pi=pi: e.matmul(
                        psS[0:1, 128:256], ones_col32[:],
                        x2f[pi][:, c * 128:(c + 1) * 128],
                        start=(c == 0), stop=(c == ND - 1)), [evs])
                sl = slice(tt * 128, (tt + 1) * 128)
                ev = P.op("vector", lambda e: e.tensor_scalar(
                    rsc2[:, 0:128], psS[0:1, 0:128], 1.0 / D, None,
                    ALU.mult), [ev_m1, ev_m2])
                ev = P.op("vector", lambda e: e.tensor_tensor(
                    rsc2[:, 128:256], rsc2[:, 0:128], rsc2[:, 0:128],
                    ALU.mult), [ev])
                ev = P.op("vector", lambda e: e.scalar_tensor_tensor(
                    rsc2[:, 256:384], psS[0:1, 128:256], 1.0 / D,
                    rsc2[:, 128:256], ALU.mult, ALU.subtract), [ev])
                ev = P.op("vector", lambda e: e.tensor_scalar(
                    rsc2[:, 384:512], rsc2[:, 256:384], EPS, None, ALU.add),
                    [ev])
                ev = P.op("vector", lambda e: e.reciprocal(
                    rsc2[:, 384:512], rsc2[:, 384:512]), [ev])
                evA1 = P.op("scalar", lambda e: e.activation(
                    rsc2[:, 512:640], rsc2[:, 384:512], AF.Sqrt, bias=0.0,
                    scale=1.0), [ev])
                evCC = P.op("vector", lambda e, sl=sl: e.tensor_copy(
                    ccr2[:, sl], rsc2[:, 512:640]), [evA1])
                P.op("vector", lambda e, sl=sl: e.scalar_tensor_tensor(
                    row22[0:1, sl], rsc2[:, 0:128], -1.0, ccr2[:, sl],
                    ALU.mult, ALU.mult), [evCC])
            for p in range(ND):
                for g in range(NQG):
                    evb1 = P.op("tensor", lambda e, p=p, g=g: e.matmul(
                        pAB[:], n2sb[0:1, p * 128:(p + 1) * 128],
                        ccr2[:, g * 512:(g + 1) * 512], start=True,
                        stop=True), [ev_n2, P.full("v")])
                    evb2 = P.op("tensor", lambda e, p=p, g=g: e.matmul(
                        pBB[:], n2sb[0:2, p * 128:(p + 1) * 128],
                        row22[0:2, g * 512:(g + 1) * 512], start=True,
                        stop=True), [P.full("v")])
                    ev1 = P.op("vector", lambda e, p=p, g=g: e.tensor_tensor(
                        xtmp_g[:], X3[:, p, g * 512:(g + 1) * 512], pAB[:],
                        ALU.mult), [evb1, P.full("v")])
                    P.op("vector", lambda e, p=p, g=g: e.tensor_tensor(
                        XB3[:, p, g * 512:(g + 1) * 512], xtmp_g[:], pBB[:],
                        ALU.add), [ev1, evb2])
            P.barrier()
            P.flush()
        gld3 = glu_dram.rearrange("(j p) s -> p j s", p=128)
        with ExitStack() as esGi:
            HHALF = HIDP
            wibuf = esGi.enter_context(nc.sbuf_tensor(
                "g_wibuf", [128, ND * HHALF], BF16))
            wi4 = wibuf[:].rearrange("p (c n) -> p c n", c=ND)
            su = esGi.enter_context(nc.sbuf_tensor("g_su", [128, 512], F32))
            g2v = esGi.enter_context(nc.sbuf_tensor("g_g2v", [128, 512],
                                                    F32))
            glub = esGi.enter_context(nc.sbuf_tensor("g_glub", [128, 512],
                                                     BF16))
            psUr = [esGi.enter_context(nc.psum_tensor(
                f"g_psU{i}", [128, 512], F32)) for i in range(2)]
            psG2r = [esGi.enter_context(nc.psum_tensor(
                f"g_psG2{i}", [128, 512], F32)) for i in range(2)]
            wiT3 = w_inT.rearrange("(c p) n -> p c n", p=128)
            ev_gs = None
            evgl = None
            it2 = 0
            evsu_bank = [None, None]
            evg2_bank = [None, None]
            for wh in range(2):
                ev_wi = dG.issue(wi4, wiT3[:, :, wh * HHALF:(wh + 1) * HHALF],
                                 waits=[P.full("t")])
                for j2 in range(wh * (NF2 // 2), (wh + 1) * (NF2 // 2)):
                  for g in range(NQG):
                    pi = it2 % 2
                    it2 += 1
                    cb = (2 * j2) * 128 - wh * HHALF
                    evmU = evmG = None
                    for c in range(ND):
                        evmU = P.op("tensor", lambda e, c=c, cb=cb, g=g,
                                    pi=pi: e.matmul(
                            psUr[pi][:], wi4[:, c, cb:cb + 128],
                            XB3[:, c, g * 512:(g + 1) * 512],
                            start=(c == 0), stop=(c == ND - 1)),
                            [ev_wi] + ([evsu_bank[pi]]
                                       if c == 0 and evsu_bank[pi] else []))
                    for c in range(ND):
                        evmG = P.op("tensor", lambda e, c=c, cb=cb, g=g,
                                    pi=pi: e.matmul(
                            psG2r[pi][:], wi4[:, c, cb + 128:cb + 256],
                            XB3[:, c, g * 512:(g + 1) * 512],
                            start=(c == 0), stop=(c == ND - 1)),
                            [ev_wi] + ([evg2_bank[pi]]
                                       if c == 0 and evg2_bank[pi] else []))
                    evsu = P.op("scalar", lambda e, j2=j2, pi=pi:
                                e.activation(
                        su[:], psUr[pi][:], AF.Silu, bias=atl("b_in", 2 * j2),
                        scale=1.0), [evmU, ev_atlas] +
                        ([evgl] if evgl else []))
                    evg2 = P.op("vector", lambda e, j2=j2, pi=pi:
                                e.tensor_scalar(
                        g2v[:], psG2r[pi][:], atl("b_in", 2 * j2 + 1), None,
                        ALU.add), [evmG, ev_atlas] +
                        ([evgl] if evgl else []))
                    evgl = P.op("vector", lambda e: e.tensor_tensor(
                        glub[:], su[:], g2v[:], ALU.mult),
                        [evsu, evg2] + ([ev_gs] if ev_gs else []))
                    ev_gs = dS.issue(gld3[:, j2, g * 512:(g + 1) * 512],
                                     glub[:], waits=[evgl])
                    evsu_bank[pi] = evsu
                    evg2_bank[pi] = evgl

            P.wait("vector", ev_gs)
            P.barrier()
            P.flush()
        with ExitStack() as esGo:
            wobuf = esGo.enter_context(nc.sbuf_tensor(
                "g_wobuf", [128, NF2 * D], BF16))
            wo4 = wobuf[:].rearrange("p (j n) -> p j n", j=NF2)
            glblk = [esGo.enter_context(nc.sbuf_tensor(
                f"g_glblk{i}", [128, 512], BF16)) for i in range(2)]
            pyO = [esGo.enter_context(nc.psum_tensor(
                f"g_pyO{i}", [128, 512], F32)) for i in range(ND)]
            ev_wog = dG.issue(wo4,
                              w_outT.rearrange("(j p) n -> p j n", p=128),
                              waits=[P.full("t")])
            evq_prev = [None, None]
            ev_xup = None
            for g in range(NQG):
                for j2 in range(NF2):
                    pi = j2 % 2
                    w0 = [evq_prev[pi]] if evq_prev[pi] else []
                    evl = dS.issue(glblk[pi][:],
                                   gld3[:, j2, g * 512:(g + 1) * 512],
                                   waits=w0)
                    for dd in range(ND):
                        evq_prev[pi] = P.op("tensor",
                                            lambda e, j2=j2, dd=dd, pi=pi:
                                            e.matmul(
                            pyO[dd][:], wo4[:, j2, dd * 128:(dd + 1) * 128],
                            glblk[pi][:], start=(j2 == 0),
                            stop=(j2 == NF2 - 1)), [evl, ev_wog] +
                            ([ev_xup] if j2 == 0 and ev_xup else []))
                for dd in range(ND):
                    ev_xup = P.op("vector", lambda e, dd=dd, g=g:
                                  e.scalar_tensor_tensor(
                        X3[:, dd, g * 512:(g + 1) * 512], pyO[dd][:],
                        atl("b_out", dd), X3[:, dd, g * 512:(g + 1) * 512],
                        ALU.add, ALU.add),
                        [evq_prev[0], evq_prev[1], P.full("v")])
            P.barrier()
            P.flush()

    with ExitStack() as esXa:
        xrb = [esXa.enter_context(nc.sbuf_tensor(f"xr{i}", [128, ND * 512],
                                                 F32)) for i in range(2)]
        xo3 = xpre_dram.rearrange("(c p) s -> p c s", p=128)
        prev_add = [None, None]
        for g in range(NQG):
            pi = g % 2
            w0 = [prev_add[pi]] if prev_add[pi] else []
            evl = dS.issue(xrb[pi][:].rearrange("p (c n) -> p c n", c=ND),
                           xo3[:, :, g * 512:(g + 1) * 512], waits=w0)
            prev_add[pi] = P.op("vector", lambda e, g=g, pi=pi:
                                e.tensor_tensor(
                X3[:, :, g * 512:(g + 1) * 512],
                X3[:, :, g * 512:(g + 1) * 512],
                xrb[pi][:].rearrange("p (c n) -> p c n", c=ND), ALU.add),
                [evl, P.full("v"), P.full("s")])
        P.barrier()
        P.flush()

    if upto == "attn_b":
        return store_out()
    ffn_phase("f2")
    return store_out()


def _stage_inputs(full):
    x = np.ascontiguousarray(full["L0_attn_a_x"])
    B, Sfull, _ = x.shape
    SQ = Sfull // 4
    wqkvT = np.ascontiguousarray(full["L0_attn_a_Wqkv"].T)
    wgateT = np.ascontiguousarray(full["L0_attn_a_Wgate"].T)
    woutT = np.ascontiguousarray(full["L0_attn_a_Wout"].T)
    posb = np.ascontiguousarray(full["L0_attn_a_pos_bias"])
    posb_rep = np.ascontiguousarray(
        np.tile(posb.T.reshape(1, -1), (128, 1)).astype(np.float32))
    w1T = np.ascontiguousarray(full["L0_ffn_w1"].T)
    w2T = np.ascontiguousarray(full["L0_ffn_w2"].T)
    wq_b = np.ascontiguousarray(full["L0_attn_b_wq"])
    wk_b = np.ascontiguousarray(full["L0_attn_b_wk"])
    wv_b = np.ascontiguousarray(full["L0_attn_b_wv"])
    wg_b = np.ascontiguousarray(full["L0_attn_b_wg"])
    wo_b = np.ascontiguousarray(full["L0_attn_b_wo"])
    w_in = full["L0_attn_b_w_in"]
    wu, wg2 = w_in[:, :HID], w_in[:, HID:]
    w_inT = np.zeros((D, 2 * HIDP), np.float32)
    b_in = full["L0_attn_b_b_in"]
    b_in_r = np.zeros(2 * HIDP, np.float32)
    for j in range(NF2):
        lo, hi = j * 128, min((j + 1) * 128, HID)
        w_inT[:, 2 * j * 128:2 * j * 128 + (hi - lo)] = wu[:, lo:hi]
        w_inT[:, (2 * j + 1) * 128:(2 * j + 1) * 128 + (hi - lo)] = wg2[:, lo:hi]
        b_in_r[2 * j * 128:2 * j * 128 + (hi - lo)] = b_in[lo:hi]
        b_in_r[(2 * j + 1) * 128:(2 * j + 1) * 128 + (hi - lo)] = b_in[HID + lo:HID + hi]
    w_out = full["L0_attn_b_w_out"]
    w_outT = np.zeros((HIDP, D), np.float32)
    w_outT[:HID] = w_out
    n2wb = np.stack([full["L0_attn_b_norm2_w"], full["L0_attn_b_norm2_b"]])
    n1wb = np.stack([full["L0_attn_b_norm1_w"], full["L0_attn_b_norm1_b"]])
    gnw = full["L0_attn_b_gn_w"]
    gnwm = np.zeros((1, 2 * D), np.float32)
    for p in range(NPAIR):
        gnwm[0, p * 128:p * 128 + 64] = gnw[p * 128:p * 128 + 64]
        gnwm[0, D + p * 128 + 64:D + (p + 1) * 128] = gnw[p * 128 + 64:
                                                          (p + 1) * 128]
    gnbrow = np.ascontiguousarray(full["L0_attn_b_gn_b"].reshape(1, D))

    def cols(v):
        return np.ascontiguousarray(v.reshape(-1, 128).T)

    atlasv = np.zeros((128, NB_COLS), np.float32)
    atlasv[:, ATL["bqkv"]:ATL["bqkv"] + 24] = np.concatenate(
        [cols(full["L0_attn_a_bqkv"][i * D:(i + 1) * D]) for i in range(3)],
        axis=1)
    atlasv[:, ATL["bout"]:ATL["bout"] + 8] = cols(full["L0_attn_a_bout"])
    atlasv[:, ATL["bgate"]:ATL["bgate"] + 8] = cols(full["L0_attn_a_bgate"])
    atlasv[:, ATL["bq"]:ATL["bq"] + 8] = cols(full["L0_attn_b_bq"])
    atlasv[:, ATL["bk"]:ATL["bk"] + 8] = cols(full["L0_attn_b_bk"])
    atlasv[:, ATL["bv"]:ATL["bv"] + 8] = cols(full["L0_attn_b_bv"])
    atlasv[:, ATL["bg"]:ATL["bg"] + 8] = cols(full["L0_attn_b_bg"])
    atlasv[:, ATL["bo"]:ATL["bo"] + 8] = cols(full["L0_attn_b_bo"])
    atlasv[:, ATL["gnw"]:ATL["gnw"] + 8] = cols(full["L0_attn_b_gn_w"])
    atlasv[:, ATL["gnb"]:ATL["gnb"] + 8] = cols(full["L0_attn_b_gn_b"])
    atlasv[:, ATL["n1w"]:ATL["n1w"] + 8] = cols(full["L0_attn_b_norm1_w"])
    atlasv[:, ATL["n1b"]:ATL["n1b"] + 8] = cols(full["L0_attn_b_norm1_b"])
    atlasv[:, ATL["n2w"]:ATL["n2w"] + 8] = cols(full["L0_attn_b_norm2_w"])
    atlasv[:, ATL["n2b"]:ATL["n2b"] + 8] = cols(full["L0_attn_b_norm2_b"])
    atlasv[:, ATL["b_in"]:ATL["b_in"] + 44] = cols(b_in_r)
    atlasv[:, ATL["b_out"]:ATL["b_out"] + 8] = cols(full["L0_attn_b_b_out"])
    atlasv[:, ATL["b1"]:ATL["b1"] + 32] = cols(full["L0_ffn_b1"])
    atlasv[:, ATL["b2"]:ATL["b2"] + 8] = cols(full["L0_ffn_b2"])

    in_maps = []
    for core in range(8):
        b, qt = divmod(core, 4)
        sl = x[b, qt * SQ:(qt + 1) * SQ]
        x_o = np.ascontiguousarray(sl.T)
        if qt == 0:
            x_h = np.zeros((D, HALO), np.float32)
        else:
            x_h = np.ascontiguousarray(
                x[b, qt * SQ - HALO:qt * SQ].T)
        ind = np.zeros((NOFF, SQ), np.float32)
        if qt == 0:
            n = np.arange(SQ)
            for oi, off in enumerate(OFFSETS):
                ind[oi] = (off > n).astype(np.float32)
        coef = np.zeros((1, 128), np.float32)
        for i in range(4):
            if i < qt:
                for h in range(NH):
                    coef[0, i * NH + h] = GAMMAS[h] ** (SQ * (qt - 1 - i))
        in_maps.append({
            "x_own": x_o, "x_halo": x_h, "wqkvT": wqkvT, "wgateT": wgateT,
            "woutT": woutT, "posb": posb, "posb_rep": posb_rep,
            "ind_bnd": ind,
            "bias_atlas": atlasv, "w1T": w1T, "w2T": w2T,
            "wq_b": wq_b, "wk_b": wk_b, "wv_b": wv_b, "wg_b": wg_b,
            "wo_b": wo_b, "w_inT": w_inT, "w_outT": w_outT,
            "coef_row": coef, "n2wb": np.ascontiguousarray(n2wb),
            "n1wb": np.ascontiguousarray(n1wb), "gnwm": gnwm,
            "gnbrow": gnbrow,
        })
    return in_maps


def kernel(**inputs):
    in_maps = _stage_inputs(inputs)
    nc = bass.Bass("TRN2", target_bir_lowering=False)
    build_program(nc, S=2048, upto="ffn2")
    res = run_bass_kernel_spmd(nc, in_maps, core_ids=list(range(8)))
    x = inputs["L0_attn_a_x"]
    B, Sfull, _ = x.shape
    SQ = Sfull // 4
    out = np.empty((B, Sfull, D), np.float32)
    for core in range(8):
        b, qt = divmod(core, 4)
        out[b, qt * SQ:(qt + 1) * SQ] = res.results[core]["y_out"].T
    return out


# version 12
# speedup vs baseline: 1.0294x; 1.0294x over previous
import numpy as np
import ml_dtypes
from contextlib import ExitStack

import concourse.bass as bass
import concourse.mybir as mybir
from concourse.bass_utils import run_bass_kernel_spmd
from concourse.bass_isa import ReduceOp
from concourse.library_config import attn as ATTN_LIB

F32 = mybir.dt.float32
BF16 = mybir.dt.bfloat16
FP16 = mybir.dt.float16
I32 = mybir.dt.int32
AF = mybir.ActivationFunctionType
ALU = mybir.AluOpType
AX = mybir.AxisListType
BF = ml_dtypes.bfloat16

D = 1024
ND = 8
NH = 16
HD = 64
NPAIR = 8
OFFSETS = sorted(set(range(0, 33)) | {48, 64, 96, 128, 192, 256, 384, 512,
                                      768, 1024, 1536})
NOFF = len(OFFSETS)
HALO = 1536
RELCH = [-12, -8, -6, -4, -3, -2, -1, 0]
NCH = len(RELCH)
EPS = 1e-5
QSC = HD ** (-0.5)
GAMMAS = [1.0 - 2.0 ** (-5.0 - h) for h in range(NH)]
LNG = [float(np.log(g)) for g in GAMMAS]
FF = 4096
NFC = 32
HID = 2730
NF2 = 22
HIDP = NF2 * 128
ELEN = 1792
ELEN2 = 2048
EBASE = 255
REL256 = [-12, -11, -8, -7, -6, -5, -4, -3, -2, -1, 0, 1]
NC256 = len(REL256)

_ATL_SPEC = [("bqkv", 24), ("bout", 8), ("bgate", 8),
             ("bq", 8), ("bk", 8), ("bv", 8), ("bg", 8), ("bo", 8),
             ("gnw", 8), ("gnb", 8), ("n1w", 8), ("n1b", 8),
             ("n2w", 8), ("n2b", 8), ("b_in", 44), ("b_out", 8),
             ("b1", 32), ("b2", 8)]
ATL = {}
_c = 0
for _n, _w in _ATL_SPEC:
    ATL[_n] = _c
    _c += _w
NB_COLS = _c


class Prog:
    ENGINES = ("sync", "scalar", "vector", "tensor", "gpsimd")
    CSEM = {"sync": "qS", "scalar": "s", "vector": "v", "tensor": "t",
            "gpsimd": "g"}

    def __init__(self, nc, es):
        self.nc = nc
        self.es = es
        self.ops = {e: [] for e in self.ENGINES}
        self.semh = {}
        self.count = {}
        self.wmark = {e: {} for e in self.ENGINES}

    def sem(self, name):
        if name not in self.semh:
            self.semh[name] = self.es.enter_context(self.nc.semaphore(name))
            self.count[name] = 0
        return self.semh[name]

    def wait(self, eng, ev):
        if ev is None:
            return
        s, v = ev
        if v <= 0 or self.wmark[eng].get(s, 0) >= v:
            return
        self.wmark[eng][s] = v
        h = self.sem(s)
        self.ops[eng].append(lambda e, h=h, v=v: e.wait_ge(h, v))

    def op(self, eng, fn, waits=(), sem=None, amt=1):
        for ev in waits:
            self.wait(eng, ev)
        if sem is None:
            sem = self.CSEM[eng]
        h = self.sem(sem)
        self.count[sem] += amt
        v = self.count[sem]

        def run(e, fn=fn, h=h, amt=amt):
            fn(e).then_inc(h, amt)

        self.ops[eng].append(run)
        return (sem, v)

    def full(self, sem):
        return (sem, self.count.get(sem, 0))

    def barrier(self):
        evs = [self.full(s) for s in list(self.count)]
        for e in self.ENGINES:
            for ev in evs:
                self.wait(e, ev)

    def flush(self):
        if not any(self.ops[e] for e in self.ENGINES):
            return
        with self.nc.Block() as block:
            for eng in self.ENGINES:
                ops = self.ops[eng]
                if not ops:
                    continue

                def body(e, ops=ops):
                    for f in ops:
                        f(e)

                getattr(block, eng)(body)
                self.ops[eng] = []

    def finalize(self):
        self.flush()


class DmaStream:
    def __init__(self, prog, eng, name, gens=6):
        self.p = prog
        self.eng = eng
        self.names = [f"{name}{i}" for i in range(gens)]
        self.i = 0

    def issue(self, out, in_, waits=(), **kw):
        p, g = self.p, self.names[self.i % len(self.names)]
        self.i += 1
        prev = p.count.get(g, 0)
        if prev:
            p.wait(self.eng, (g, prev))
        return p.op(self.eng, lambda e: e.dma_start(out=out, in_=in_, **kw),
                    waits, sem=g, amt=16)


def build_program(nc, S=2048, upto="ffn2", n_cores=8, npair=NPAIR):
    TT = HALO + S
    NT = S // 128
    NTT = TT // 128
    NTH = HALO // 128
    NQG = S // 512 if S >= 512 else 1
    QG = S // NQG
    NKG = TT // 512

    def din(name, shape, dt=F32):
        return nc.dram_tensor(name, list(shape), dt, kind="ExternalInput").ap()

    x_own = din("x_own", [D, S])
    x_halo = din("x_halo", [D, HALO])
    wqkvT = din("wqkvT", [D, 3 * D])
    wgateT = din("wgateT", [D, D])
    woutT = din("woutT", [D, D])
    posb = din("posb", [NOFF, NH])
    posb_rep = din("posb_rep", [128, NH * NOFF])
    ind_bnd = din("ind_bnd", [NOFF, S])
    bias_atlas = din("bias_atlas", [128, NB_COLS])
    w1T = din("w1T", [D, FF])
    w2T = din("w2T", [FF, D])
    wq_b = din("wq_b", [D, D])
    wk_b = din("wk_b", [D, D])
    wv_b = din("wv_b", [D, D])
    wg_b = din("wg_b", [D, D])
    wo_b = din("wo_b", [D, D])
    w_inT = din("w_inT", [D, 2 * HIDP])
    w_outT = din("w_outT", [HIDP, D])
    coef_row = din("coef_row", [1, 128])
    n2wb = din("n2wb", [2, D])
    n1wb = din("n1wb", [2, D])
    gnwm = din("gnwm", [1, 2 * D])
    gnbrow = din("gnbrow", [1, D])
    y_out = nc.dram_tensor("y_out", [D, S], F32, kind="ExternalOutput").ap()

    e2_dram = nc.dram_tensor("e2_dram", [NH, 128 * (ELEN2 + 1)], BF16,
                             kind="Internal").ap()
    fg_dram = nc.dram_tensor("fg_dram", [ND * 128, S], BF16,
                             kind="Internal").ap()
    g_dram = nc.dram_tensor("g_dram", [NFC * 128, S], FP16, kind="Internal").ap()
    glu_dram = nc.dram_tensor("glu_dram", [NF2 * 128, S], BF16,
                              kind="Internal").ap()
    t1_dram = nc.dram_tensor("t1_dram", [D, FF], BF16, kind="Internal").ap()
    t2_dram = nc.dram_tensor("t2_dram", [FF, D], BF16, kind="Internal").ap()
    cc_in = nc.dram_tensor("cc_in", [NH * HD, HD], BF16, kind="Internal").ap()
    cc_out = nc.dram_tensor("cc_out", [n_cores * NH * HD, HD], BF16,
                            kind="Internal").ap()

    es = ExitStack()
    P = Prog(nc, es)

    def sb(name, shape, dt=F32):
        return es.enter_context(nc.sbuf_tensor(name, list(shape), dt))

    dS = DmaStream(P, "sync", "dS", 8)
    dA = DmaStream(P, "scalar", "dA", 4)
    dG = DmaStream(P, "gpsimd", "dG", 8)

    X = sb("X", [128, ND * S])
    atlas = sb("atlas", [128, NB_COLS])
    ones_col_bf = sb("ones_col_bf", [128, 1], BF16)
    ones_col32 = sb("ones_col32", [128, 1])
    ones_row = sb("ones_row", [1, 128])
    neg1_bf = sb("neg1_bf", [1, 1], BF16)
    identbf = sb("identbf", [128, 128], BF16)
    ident32 = sb("ident32", [128, 128])
    iojp = sb("iojp", [128, 128])
    maskge = sb("maskge", [128, 128])
    iorow = sb("iorow", [1, 128])
    iocol = sb("iocol", [128, 1])
    indrow32 = sb("indrow32", [1, 256])
    iocol127 = sb("iocol127", [128, 1])
    iorow1 = sb("iorow1", [1, 128])

    ev_atlas = dS.issue(atlas[:], bias_atlas)
    ev_io = P.op("gpsimd", lambda e: e.iota(
        iojp[:], [[1, 128]], base=0, channel_multiplier=-1,
        allow_small_or_imprecise_dtypes=True))
    ev_iorow = P.op("gpsimd", lambda e: e.iota(
        iorow[:], [[1, 128]], base=0, channel_multiplier=0,
        allow_small_or_imprecise_dtypes=True))
    ev_iocol = P.op("gpsimd", lambda e: e.iota(
        iocol[:], [[0, 1]], base=0, channel_multiplier=1,
        allow_small_or_imprecise_dtypes=True))
    ev_id = P.op("vector", lambda e: e.tensor_scalar(
        ident32[:], iojp[:], 0.0, None, ALU.is_equal), [ev_io])
    ev_idb = P.op("vector", lambda e: e.tensor_copy(identbf[:], ident32[:]),
                  [ev_id])
    ev_mge = P.op("vector", lambda e: e.tensor_scalar(
        maskge[:], iojp[:], 0.0, None, ALU.is_ge), [ev_io])
    P.op("vector", lambda e: e.memset(ones_col_bf[:], 1.0))
    P.op("vector", lambda e: e.memset(ones_col32[:], 1.0))
    P.op("vector", lambda e: e.memset(ones_row[:], 1.0))
    P.op("vector", lambda e: e.memset(neg1_bf[:], -1.0))
    P.op("vector", lambda e: e.memset(indrow32[:, 0:64], 1.0))
    P.op("vector", lambda e: e.memset(indrow32[:, 64:128], 0.0))
    P.op("vector", lambda e: e.memset(indrow32[:, 128:192], 0.0))
    P.op("vector", lambda e: e.memset(indrow32[:, 192:256], 1.0))
    P.op("vector", lambda e: e.tensor_scalar(
        iocol127[:], iocol[:], -127.0, None, ALU.add), [ev_iocol])
    P.op("vector", lambda e: e.tensor_scalar(
        iorow1[:], iorow[:], 1.0, None, ALU.add), [ev_iorow])

    X3 = X[:].rearrange("p (c s) -> p c s", c=ND)
    ev_X = dS.issue(X3, x_own.rearrange("(c p) s -> p c s", p=128))

    def atl(name, j):
        return atlas[:, ATL[name] + j:ATL[name] + j + 1]

    def store_out():
        ev_y = dS.issue(y_out.rearrange("(c p) s -> p c s", p=128), X3,
                        waits=[P.full("v")])
        P.wait("sync", ev_y)
        P.finalize()
        return P

    P.barrier()
    if upto == "init":
        return store_out()

    with ExitStack() as esA:
        def sbA(name, shape, dt=F32):
            return esA.enter_context(nc.sbuf_tensor(name, list(shape), dt))

        u_bf = sbA("u_bf", [128, ND * TT], BF16)
        U3 = u_bf[:].rearrange("p (c t) -> p c t", c=ND)

        with ExitStack() as esL:
            def sbL(name, shape, dt=F32):
                return esL.enter_context(nc.sbuf_tensor(name, list(shape), dt))

            def psL(name, shape, dt=F32):
                return esL.enter_context(nc.psum_tensor(name, list(shape), dt))

            x2f = [sbL(f"a_x2f{i}", [128, D]) for i in range(2)]
            xh32 = [sbL(f"a_xh{i}", [128, D]) for i in range(2)]
            t32 = [sbL(f"a_t32{i}", [128, D]) for i in range(2)]
            rows = [sbL(f"a_rows{i}", [1, 768]) for i in range(2)]
            psS = [psL(f"a_psS{i}", [1, 256]) for i in range(2)]
            psAB = [psL(f"a_psAB{i}", [128, 128]) for i in range(2)]
            psBB = [psL(f"a_psBB{i}", [128, 128]) for i in range(2)]
            prev_u = [None, None]
            xhalo3 = x_halo.rearrange("(c p) t -> p c t", p=128)
            for tt in range(NTT):
                pi = tt % 2
                w0 = [prev_u[pi]] if prev_u[pi] else []
                if tt < NTH:
                    ev_ld = dS.issue(
                        xh32[pi][:].rearrange("p (c n) -> p c n", c=ND),
                        xhalo3[:, :, tt * 128:(tt + 1) * 128], waits=w0)
                    x_ap = xh32[pi][:].rearrange("p (c n) -> p c n", c=ND)
                    w0 = w0 + [ev_ld]
                else:
                    t0 = (tt - NTH) * 128
                    x_ap = X3[:, :, t0:t0 + 128]
                    w0 = w0 + [ev_X]
                evs = P.op("scalar", lambda e, x_ap=x_ap, pi=pi:
                           e.activation(
                    x2f[pi][:].rearrange("p (c n) -> p c n", c=ND), x_ap,
                    AF.Square, bias=0.0, scale=1.0), w0 + [P.full("v")])
                ev_m1 = ev_m2 = None
                for c in range(ND):
                    ev_m1 = P.op("tensor", lambda e, c=c, x_ap=x_ap, pi=pi:
                                 e.matmul(
                        psS[pi][0:1, 0:128], ones_col32[:],
                        x_ap[:, c], start=(c == 0), stop=(c == ND - 1)),
                        w0 + [P.full("v")])
                for c in range(ND):
                    ev_m2 = P.op("tensor", lambda e, c=c, pi=pi: e.matmul(
                        psS[pi][0:1, 128:256], ones_col32[:],
                        x2f[pi][:, c * 128:(c + 1) * 128],
                        start=(c == 0), stop=(c == ND - 1)), [evs])
                r = rows[pi]
                ev = P.op("vector", lambda e, r=r, pi=pi: e.tensor_scalar(
                    r[:, 0:128], psS[pi][0:1, 0:128], 1.0 / D, None, ALU.mult),
                    [ev_m1, ev_m2])
                ev = P.op("vector", lambda e, r=r: e.tensor_tensor(
                    r[:, 128:256], r[:, 0:128], r[:, 0:128], ALU.mult), [ev])
                ev = P.op("vector", lambda e, r=r, pi=pi:
                          e.scalar_tensor_tensor(
                    r[:, 256:384], psS[pi][0:1, 128:256], 1.0 / D,
                    r[:, 128:256], ALU.mult, ALU.subtract), [ev])
                ev = P.op("vector", lambda e, r=r: e.tensor_scalar(
                    r[:, 640:768], r[:, 256:384], EPS, None, ALU.add), [ev])
                ev = P.op("vector", lambda e, r=r: e.reciprocal(
                    r[:, 256:384], r[:, 640:768]), [ev])
                evA = P.op("scalar", lambda e, r=r: e.activation(
                    r[:, 384:512], r[:, 256:384], AF.Sqrt, bias=0.0,
                    scale=1.0), [ev])
                evN = P.op("vector", lambda e, r=r: e.scalar_tensor_tensor(
                    r[:, 512:640], r[:, 0:128], -1.0, r[:, 384:512],
                    ALU.mult, ALU.mult), [evA])
                evbA = P.op("tensor", lambda e, r=r, pi=pi: e.matmul(
                    psAB[pi][:], ones_row[:], r[:, 384:512],
                    start=True, stop=True), [evA, P.full("v")])
                evbB = P.op("tensor", lambda e, r=r, pi=pi: e.matmul(
                    psBB[pi][:], ones_row[:], r[:, 512:640],
                    start=True, stop=True), [evN])
                abc = psAB[pi][:].unsqueeze(1).broadcast_to([128, ND, 128])
                bbc = psBB[pi][:].unsqueeze(1).broadcast_to([128, ND, 128])
                ev_t = P.op("vector", lambda e, x_ap=x_ap, pi=pi, abc=abc:
                            e.tensor_tensor(
                    t32[pi][:].rearrange("p (c n) -> p c n", c=ND), x_ap, abc,
                    ALU.mult), [evbA] + w0)
                ev_u = P.op("vector", lambda e, tt=tt, pi=pi, bbc=bbc:
                            e.tensor_tensor(
                    U3[:, :, tt * 128:(tt + 1) * 128],
                    t32[pi][:].rearrange("p (c n) -> p c n", c=ND), bbc,
                    ALU.add), [evbB, ev_t])
                prev_u[pi] = ev_u
            P.barrier()
            P.flush()
        if upto == "ln0":
            return store_out()

        pb = sbA("pb", [NOFF, NH])
        expE44 = sbA("expE44", [NOFF, NH], BF16)
        indb = sbA("indb", [NOFF, S], BF16)
        with ExitStack() as esE:
            pbrep = esE.enter_context(nc.sbuf_tensor(
                "pbrep", [128, NH * NOFF], F32))
            erep = esE.enter_context(nc.sbuf_tensor(
                "erep", [128, ELEN2], F32))
            ev_pb = dS.issue(pb[:], posb)
            ev_pbr = dS.issue(pbrep[:], posb_rep)
            ev_e44 = P.op("scalar", lambda e: e.activation(
                expE44[:], pb[:], AF.Exp, bias=0.0, scale=1.0), [ev_pb])
            ev_e128 = P.op("scalar", lambda e: e.activation(
                pbrep[:], pbrep[:], AF.Exp, bias=0.0, scale=1.0), [ev_pbr])
            ev_ed = None
            for h in range(NH):
                evz = P.op("vector", lambda e: e.memset(erep[:], 0.0),
                           [ev_ed] if ev_ed else [])
                evs_ = [P.op("vector", lambda e, h=h: e.tensor_copy(
                    erep[:, EBASE:EBASE + 33],
                    pbrep[:, h * NOFF:h * NOFF + 33]), [evz, ev_e128])]
                for oi in range(33, NOFF):
                    off = OFFSETS[oi]
                    evs_.append(P.op("vector", lambda e, h=h, oi=oi, off=off:
                                     e.tensor_copy(
                        erep[:, EBASE + off:EBASE + 1 + off],
                        pbrep[:, h * NOFF + oi:h * NOFF + oi + 1]),
                        [evz, ev_e128]))
                ev_ed = dG.issue(
                    bass.AP(e2_dram.tensor, h * 128 * (ELEN2 + 1),
                            [[ELEN2 + 1, 128], [1, ELEN2]]),
                    erep[:, 0:ELEN2], waits=evs_)
            ev_indb = dG.issue(indb[:], ind_bnd)
            P.barrier()
            P.flush()
        if upto == "etab":
            return store_out()

        with ExitStack() as esP:
            def sbP(name, shape, dt=F32):
                return esP.enter_context(nc.sbuf_tensor(name, list(shape), dt))

            def psP(name, shape, dt=F32):
                return esP.enter_context(nc.psum_tensor(name, list(shape), dt))

            wsl = sbP("wsl", [128, 4 * ND * 128], BF16)
            q_bf = sbP("q_bf", [128, S], BF16)
            k_bf = sbP("k_bf", [128, TT], BF16)
            v_bf = sbP("v_bf", [128, TT], BF16)
            v_tm = sbP("v_tm", [128, NTT * 128], BF16)
            gate32 = sbP("gate32", [128, S], BF16)
            expM = sbP("expM", [128, 2 * NC256 * 256], BF16)
            ab = sbP("ab", [128, 2 * NC256 * 256], BF16)
            ftmp = sbP("ftmp", [1, 512])
            sufcb = sbP("sufcb", [1, 2 * S], BF16)
            negv = sbP("negv", [1, 128], BF16)
            zrec = sbP("zrec", [1, 512])
            t1b = sbP("t1b", [128, 256])
            fgp = sbP("fgp", [128, S], BF16)
            pp = psP("pp", [128, 512])
            psT = [psP(f"pst{i}", [128, 512]) for i in range(2)]

            wqkvT3 = wqkvT.rearrange("(c p) n -> p c n", p=128)
            wgateT3 = wgateT.rearrange("(c p) n -> p c n", p=128)
            wsl4 = wsl[:].rearrange("p (w c n) -> p w c n", w=4, c=ND)
            fgd3 = fg_dram.rearrange("(c p) s -> p c s", p=128)

            eve_prev = None
            ev_exp_prev = [None, None]
            ev_fg_prev = None
            ev_zr_prev = None
            ev_t1_prev = None
            last_gate = None
            for p in range(npair):
                tguard = [P.full("t")]
                ev_w = []
                for wi, (src3, c0) in enumerate(
                        [(wqkvT3, 128 * p), (wqkvT3, D + 128 * p),
                         (wqkvT3, 2 * D + 128 * p), (wgateT3, 128 * p)]):
                    ev_w.append(dG.issue(wsl4[:, wi],
                                         src3[:, :, c0:c0 + 128],
                                         waits=tguard))
                if upto == "wsl":
                    P.barrier()
                    P.flush()
                    break

                def proj(dst_ap, wi, tok0, ntok, bias_ap, scale, ev_extra,
                         act=None):
                    evs_out = []
                    ng = ntok // 512 if ntok >= 512 else 1
                    gs = ntok // ng
                    evo = None
                    for g in range(ng):
                        evm = None
                        for c in range(ND):
                            evm = P.op("tensor",
                                       lambda e, c=c, g=g, wi=wi, tok0=tok0,
                                       gs=gs: e.matmul(
                                           pp[:, 0:gs],
                                           wsl4[:, wi, c],
                                           U3[:, c, tok0 + g * gs:
                                              tok0 + (g + 1) * gs],
                                           start=(c == 0), stop=(c == ND - 1)),
                                       [ev_w[wi], P.full("v")] + ev_extra +
                                       ([evo] if evo else []))
                        if act is None:
                            evo = P.op("vector",
                                       lambda e, g=g, gs=gs: e.tensor_scalar(
                                           dst_ap[:, g * gs:(g + 1) * gs],
                                           pp[:, 0:gs], bias_ap, scale,
                                           ALU.add, ALU.mult), [evm])
                        else:
                            evo = P.op("scalar",
                                       lambda e, g=g, gs=gs: e.activation(
                                           dst_ap[:, g * gs:(g + 1) * gs],
                                           pp[:, 0:gs], act, bias=bias_ap,
                                           scale=1.0), [evm])
                        evs_out.append(evo)
                    return evs_out

                ev_q = proj(q_bf[:], 0, HALO, S, atl("bqkv", p), QSC,
                            [last_gate] if last_gate else [])
                if upto == "projq":
                    P.barrier()
                    P.flush()
                    break
                ev_k = proj(k_bf[:], 1, 0, TT, atl("bqkv", 8 + p), 1.0, [])
                ev_v = proj(v_bf[:], 2, 0, TT, atl("bqkv", 16 + p), 1.0, [])
                if upto == "projv":
                    P.barrier()
                    P.flush()
                    break
                ev_g = proj(gate32[:], 3, HALO, S, atl("bgate", p), 1.0, [],
                            act=AF.Sigmoid)
                last_gate = ev_g[-1]
                if upto == "proj":
                    P.barrier()
                    P.flush()
                    break

                esPre = ExitStack()
                ptr = esPre.enter_context(
                    nc.psum_tensor(f"ptr{p}", [128, 128], BF16))
                psuf = esPre.enter_context(
                    nc.psum_tensor(f"psuf{p}", [1, 1024], F32))
                ev_vt = []
                for a in range(NTT):
                    evt = P.op("tensor", lambda e, a=a, ptr=ptr: e.transpose(
                        ptr[:], v_bf[:, a * 128:(a + 1) * 128], identbf[:]),
                        [ev_v[min(a * 128 // (TT // len(ev_v)),
                                  len(ev_v) - 1)], ev_idb, P.full("v")])
                    ev_vt.append(P.op("vector",
                                      lambda e, a=a, ptr=ptr: e.tensor_copy(
                        v_tm[:, a * 128:(a + 1) * 128], ptr[:]), [evt]))
                ev_vt_all = P.full("v")
                if upto == "vtr":
                    P.barrier()
                    P.flush()
                    esPre.close()
                    break

                ev_eM = []
                for hb in (0, 1):
                    h = 2 * p + hb
                    for ci, rel in enumerate(REL256):
                        off0 = h * 128 * (ELEN2 + 1) + EBASE - 128 * rel
                        sl_m = (hb * NC256 + ci) * 256
                        ev_eM.append(dA.issue(
                            expM[:, sl_m:sl_m + 256],
                            bass.AP(e2_dram.tensor, off0,
                                    [[ELEN2, 128], [1, 256]]),
                            waits=[P.full("v")]))
                if upto == "expm":
                    P.barrier()
                    P.flush()
                    esPre.close()
                    break

                ev_sf = []
                for hb in (0, 1):
                    h = 2 * p + hb
                    ho = hb * 64
                    for g in range(NQG):
                        gs = QG
                        evm1 = P.op("tensor", lambda e, h=h, g=g, gs=gs, psuf=psuf:
                                    e.matmul(
                            psuf[0:1, 0:gs], expE44[:, h:h + 1],
                            indb[:, g * gs:(g + 1) * gs],
                            start=True, stop=True),
                            [ev_e44, ev_indb, P.full("v")] +
                            ([eve_prev] if eve_prev else []))
                        evm2 = P.op("tensor", lambda e, ho=ho, g=g, gs=gs, psuf=psuf:
                                    e.matmul(
                            psuf[0:1, 512:512 + gs],
                            k_bf[ho:ho + 64, 0:1],
                            q_bf[ho:ho + 64, g * gs:(g + 1) * gs],
                            start=True, stop=True,
                            tile_position=(ho, 0)),
                            ev_k + ev_q +
                            ([eve_prev] if eve_prev else []))
                        eve = P.op("scalar", lambda e, gs=gs, psuf=psuf:
                                   e.activation(
                            ftmp[0:1, 0:gs], psuf[0:1, 512:512 + gs], AF.Exp,
                            bias=0.0, scale=1.0), [evm2, P.full("v")])
                        eve_prev = eve
                        ev_sf.append(P.op("vector", lambda e, g=g, gs=gs, psuf=psuf,
                                          hb=hb: e.tensor_tensor(
                            sufcb[0:1, hb * S + g * gs:hb * S + (g + 1) * gs],
                            ftmp[0:1, 0:gs], psuf[0:1, 0:gs], ALU.mult),
                            [eve, evm1]))
                ev_negv = P.op("vector", lambda e: e.tensor_scalar(
                    negv[0:1, :], v_tm[0:1, 0:128], -1.0, None, ALU.mult),
                    [ev_vt[0]])
                P.flush()
                esPre.close()
                if upto == "bnd":
                    break
                esTile = ExitStack()
                pav = esTile.enter_context(
                    nc.psum_tensor(f"pav{p}", [128, 256], F32))
                zps = esTile.enter_context(
                    nc.psum_tensor(f"zps{p}", [1, 512], F32))
                pzb = esTile.enter_context(
                    nc.psum_tensor(f"pzb{p}", [128, 256], F32))

                for tb in range(NT // 2):
                    q0 = tb * 256
                    ev_zmm2 = [None, None]
                    ev_av = [None, None]
                    for hb in (0, 1):
                        ho = hb * 64
                        ev_ab_c = [None] * NC256
                        for cp in range(NC256 // 2):
                            bi = cp % 2
                            pb_ = psT[bi]
                            evm = []
                            for k2 in (0, 1):
                                ci = 2 * cp + k2
                                a = NTH + 2 * tb + REL256[ci]
                                evm.append(P.op("tensor",
                                                lambda e, ci=ci, a=a, hb=hb,
                                                ho=ho, q0=q0, k2=k2, pb_=pb_:
                                                e.matmul(
                                    pb_[:, k2 * 256:(k2 + 1) * 256],
                                    k_bf[ho:ho + 64, a * 128:(a + 1) * 128],
                                    q_bf[ho:ho + 64, q0:q0 + 256],
                                    start=True, stop=True,
                                    tile_position=(ho, 0)),
                                    ev_k + ev_q +
                                    ([ev_exp_prev[bi]] if k2 == 0 and
                                     ev_exp_prev[bi] else [])))
                            sl_a = (hb * NC256 + 2 * cp) * 256
                            ev_exp = P.op("scalar", lambda e, sl_a=sl_a,
                                          pb_=pb_: e.activation(
                                ab[:, sl_a:sl_a + 512], pb_[:],
                                AF.Exp, bias=0.0, scale=1.0), evm +
                                ([ev_fg_prev] if ev_fg_prev else []))
                            ev_exp_prev[bi] = ev_exp
                            ev_ab = P.op("vector", lambda e, sl_a=sl_a:
                                         e.tensor_tensor(
                                ab[:, sl_a:sl_a + 512],
                                ab[:, sl_a:sl_a + 512],
                                expM[:, sl_a:sl_a + 512], ALU.mult),
                                [ev_exp] + ev_eM)
                            ev_ab_c[2 * cp] = ev_ab
                            ev_ab_c[2 * cp + 1] = ev_ab
                        for ci in range(NC256):
                            sl_c = (hb * NC256 + ci) * 256
                            P.op("tensor", lambda e, ci=ci, hb=hb, sl_c=sl_c,
                                 zps=zps: e.matmul(
                                zps[0:1, hb * 256:(hb + 1) * 256],
                                ones_col_bf[:], ab[:, sl_c:sl_c + 256],
                                start=(ci == 0), stop=False),
                                [ev_ab_c[ci]] +
                                ([ev_zr_prev] if ci == 0 and ev_zr_prev
                                 else []))
                        ev_zmm = P.op("tensor", lambda e, hb=hb, q0=q0, zps=zps:
                                      e.matmul(
                            zps[0:1, hb * 256:(hb + 1) * 256], neg1_bf[:],
                            sufcb[0:1, hb * S + q0:hb * S + q0 + 256],
                            start=False, stop=True), ev_sf)
                        ev_zmm2[hb] = ev_zmm
                        for ci in range(NC256):
                            a = NTH + 2 * tb + REL256[ci]
                            sl_c = (hb * NC256 + ci) * 256
                            P.op("tensor", lambda e, ci=ci, a=a, ho=ho,
                                 sl_c=sl_c, pav=pav: e.matmul(
                                pav[ho:ho + 64, :],
                                v_tm[:, a * 128 + ho:a * 128 + ho + 64],
                                ab[:, sl_c:sl_c + 256],
                                start=(ci == 0), stop=False,
                                tile_position=(0, ho)),
                                [ev_ab_c[ci], ev_vt_all] +
                                ([ev_t1_prev] if ci == 0 and ev_t1_prev
                                 else []))
                        ev_av[hb] = P.op("tensor", lambda e, ho=ho, hb=hb,
                                         q0=q0, pav=pav: e.matmul(
                            pav[ho:ho + 64, :], negv[0:1, ho:ho + 64],
                            sufcb[0:1, hb * S + q0:hb * S + q0 + 256],
                            start=False, stop=True, tile_position=(0, ho)),
                            [ev_negv] + ev_sf)
                    ev_zr_all = P.op("vector", lambda e, zps=zps: e.reciprocal(
                        zrec[0:1, 0:512], zps[0:1, 0:512]),
                        [ev_zmm2[0], ev_zmm2[1]])
                    ev_zr_prev = ev_zr_all
                    ev_zb = P.op("tensor", lambda e, pzb=pzb: e.matmul(
                        pzb[:], indrow32[0:1, 0:128], zrec[0:1, 0:256],
                        start=True, stop=False),
                        [ev_zr_all] +
                        ([ev_fg_prev] if ev_fg_prev else []))
                    ev_zb = P.op("tensor", lambda e, pzb=pzb: e.matmul(
                        pzb[:], indrow32[0:1, 128:256], zrec[0:1, 256:512],
                        start=False, stop=True), [])
                    ev_t1 = P.op("vector", lambda e, q0=q0, pav=pav: e.tensor_tensor(
                        t1b[:], pav[:], gate32[:, q0:q0 + 256],
                        ALU.mult), [ev_av[0], ev_av[1]] + ev_g)
                    ev_t1_prev = ev_t1
                    ev_fg_prev = P.op("vector", lambda e, q0=q0, pzb=pzb:
                                      e.tensor_tensor(
                        fgp[:, q0:q0 + 256], t1b[:], pzb[:],
                        ALU.mult), [ev_t1, ev_zb])
                    if upto == "tile1":
                        break
                ev_fgs = dS.issue(fgd3[:, p], fgp[:], waits=[P.full("v")])
                P.wait("vector", ev_fgs)
                P.flush()
                esTile.close()
            P.barrier()

            P.barrier()
            P.flush()
        if upto in ("pairs", "wsl", "proj", "projq", "projv", "vtr", "expm", "bnd", "tile1"):
            return store_out()

        with ExitStack() as esW:
            wob = esW.enter_context(nc.sbuf_tensor("wob", [128, ND * D],
                                                   BF16))
            fgall = [esW.enter_context(nc.sbuf_tensor(
                f"fgall{i}", [128, ND * 512], BF16)) for i in range(2)]
            pwoA = [esW.enter_context(nc.psum_tensor(f"pwoA{i}", [128, 512],
                                                     F32)) for i in range(2)]
            fgd3 = fg_dram.rearrange("(c p) s -> p c s", p=128)
            wob4 = wob[:].rearrange("p (c n) -> p c n", c=ND)
            ev_wo = dG.issue(wob4, woutT.rearrange("(c p) n -> p c n", p=128))
            prev_use = [None, None]
            rd_prev = [None, None]
            for g in range(NQG):
                pi = g % 2
                w0 = [prev_use[pi]] if prev_use[pi] else []
                evl = dS.issue(fgall[pi][:].rearrange("p (c n) -> p c n",
                                                      c=ND),
                               fgd3[:, :, g * QG:(g + 1) * QG], waits=w0)
                evm = None
                for dd in range(ND):
                    bi = dd % 2
                    for c in range(ND):
                        evm = P.op("tensor", lambda e, c=c, dd=dd, g=g, pi=pi,
                                   bi=bi: e.matmul(
                            pwoA[bi][:],
                            wob4[:, c, dd * 128:(dd + 1) * 128],
                            fgall[pi][:, c * 512:(c + 1) * 512],
                            start=(c == 0), stop=(c == ND - 1)),
                            [ev_wo, evl] +
                            ([rd_prev[bi]] if c == 0 and rd_prev[bi]
                             else []))
                    rd_prev[bi] = P.op("vector", lambda e, dd=dd, g=g, bi=bi:
                                       e.scalar_tensor_tensor(
                        X3[:, dd, g * QG:(g + 1) * QG], pwoA[bi][:],
                        atl("bout", dd), X3[:, dd, g * QG:(g + 1) * QG],
                        ALU.add, ALU.add), [evm, ev_atlas])
                prev_use[pi] = evm
            P.barrier()
            P.flush()

    if upto == "attn_a":
        return store_out()

    mwsb = sb("mwsb", [128, 2])

    def ternarize_weights():
        with ExitStack() as esT:
            def sbT(name, shape, dt=F32):
                return esT.enter_context(nc.sbuf_tensor(name, list(shape), dt))

            wch = [sbT(f"t_wch{i}", [128, FF]) for i in range(2)]
            gtb = sbT("t_gtb", [128, FF], BF16)
            ltb = sbT("t_ltb", [128, FF], BF16)
            ternb = sbT("t_ternb", [128, FF], BF16)
            asum = sbT("t_asum", [128, 40])
            wsi = sbT("t_wsi", [128, 1])
            pts1 = esT.enter_context(nc.psum_tensor("t_pts1", [1, 128], F32))
            pts2 = esT.enter_context(nc.psum_tensor("t_pts2", [128, 128],
                                                    F32))
            for wi, (wsrc, tdst, C, NR) in enumerate(
                    [(w1T, t1_dram, FF, ND), (w2T, t2_dram, D, NFC)]):
                src3 = wsrc.rearrange("(r p) c -> p r c", p=128)
                dst3 = tdst.rearrange("(r p) c -> p r c", p=128)
                prev = [None, None]
                for r in range(NR):
                    pi = r % 2
                    w0 = [prev[pi]] if prev[pi] else []
                    evl = dS.issue(wch[pi][:, 0:C], src3[:, r], waits=w0)
                    prev[pi] = P.op("vector", lambda e, pi=pi, r=r, C=C:
                                    e.tensor_reduce(
                        asum[:, r:r + 1], wch[pi][:, 0:C], AX.X, ALU.add,
                        apply_absolute_value=True), [evl])
                ev = P.op("vector", lambda e, NR=NR: e.tensor_reduce(
                    asum[:, 32:33], asum[:, 0:NR], AX.X, ALU.add),
                    [prev[0], prev[1]])
                ev = P.op("tensor", lambda e: e.matmul(
                    pts1[0:1, 0:1], ones_col32[:], asum[:, 32:33],
                    start=True, stop=True), [ev, P.full("v")])
                ev = P.op("vector", lambda e: e.tensor_copy(
                    asum[0:1, 33:34], pts1[0:1, 0:1]), [ev])
                ev = P.op("tensor", lambda e: e.matmul(
                    pts2[:, 0:1], ones_row[:], asum[0:1, 33:34],
                    start=True, stop=True), [ev])
                ev = P.op("vector", lambda e, C=C, NR=NR, wi=wi:
                          e.tensor_scalar(
                    mwsb[:, wi:wi + 1], pts2[:, 0:1], 1.0 / (128 * NR * C),
                    1e-5, ALU.mult, ALU.max), [ev])
                ev_wsi = P.op("vector", lambda e, wi=wi: e.reciprocal(
                    wsi[:], mwsb[:, wi:wi + 1]), [ev])
                prev_st = [None, None]
                ev_tst = None
                for r in range(NR):
                    pi = r % 2
                    w0 = [prev_st[pi]] if prev_st[pi] else []
                    evl = dS.issue(wch[pi][:, 0:C], src3[:, r],
                                   waits=w0 + [ev_wsi])
                    eg = P.op("vector", lambda e, pi=pi, C=C: e.tensor_scalar(
                        gtb[:, 0:C], wch[pi][:, 0:C], wsi[:], 0.5,
                        ALU.mult, ALU.is_gt), [evl])
                    el = P.op("vector", lambda e, pi=pi, C=C: e.tensor_scalar(
                        ltb[:, 0:C], wch[pi][:, 0:C], wsi[:], -0.5,
                        ALU.mult, ALU.is_lt), [evl])
                    prev_st[pi] = el
                    et = P.op("vector", lambda e, C=C: e.tensor_tensor(
                        ternb[:, 0:C], gtb[:, 0:C], ltb[:, 0:C],
                        ALU.subtract), [eg, el] +
                        ([ev_tst] if ev_tst else []))
                    ev_tst = dS.issue(dst3[:, r], ternb[:, 0:C], waits=[et])
            P.barrier()
            P.flush()

    def ffn_phase(tag):
        with ExitStack() as esF:
            def sbF(name, shape, dt=F32):
                return esF.enter_context(nc.sbuf_tensor(name, list(shape), dt))

            sgrow = sbF(f"{tag}_sg", [1, S])
            am2row = sbF(f"{tag}_am2", [1, S])
            rowB = sbF(f"{tag}_rowB", [1, S])
            gd3 = g_dram.rearrange("(f p) s -> p f s", p=128)

            with ExitStack() as esA1:
                def sbA1(name, shape, dt=F32):
                    return esA1.enter_context(nc.sbuf_tensor(name,
                                                             list(shape), dt))
                qg = sbA1(f"{tag}_qg", [128, ND * 512], BF16)
                Qg3 = qg[:].rearrange("p (c n) -> p c n", c=ND)
                c1bg = sbA1(f"{tag}_c1bg", [128, 512])
                rg = sbA1(f"{tag}_rg", [1, 3 * 512])
                rsc = sbA1(f"{tag}_rsc", [1, 768])
                x2f = [sbA1(f"{tag}_x2f{i}", [128, D]) for i in range(2)]
                racc = sbA1(f"{tag}_racc", [128, 128])
                rtmp = sbA1(f"{tag}_rtmp", [128, 128])
                h32 = sbA1(f"{tag}_h32", [128, 512])
                gbf = sbA1(f"{tag}_gbf", [128, 512], FP16)
                g2b = sbA1(f"{tag}_g2b", [128, 512], FP16)
                acc2 = sbA1(f"{tag}_acc2", [128, 512])
                acc2n = sbA1(f"{tag}_acc2n", [128, 512])
                tg = [sbA1(f"{tag}_tg{i}", [128, 512]) for i in range(2)]
                qi32 = [sbA1(f"{tag}_qi{i}", [128, 512], I32)
                        for i in range(2)]
                t1b = esA1.enter_context(
                    nc.sbuf_tensor(f"{tag}_t1b", [128, ND * FF], BF16))
                t13 = t1b[:].rearrange("p (c f) -> p c f", c=ND)
                ev_t1 = dS.issue(t13,
                                 t1_dram.rearrange("(c p) f -> p c f", p=128))
                psS = esA1.enter_context(nc.psum_tensor(f"{tag}_psS",
                                                        [1, 256], F32))
                pmB = esA1.enter_context(nc.psum_tensor(f"{tag}_pmB",
                                                        [128, 128], F32))
                pBB = esA1.enter_context(nc.psum_tensor(f"{tag}_pBB",
                                                        [128, 512], F32))
                psHr = [esA1.enter_context(nc.psum_tensor(
                    f"{tag}_psH{i}", [128, 512], F32)) for i in range(2)]
                pG = esA1.enter_context(nc.psum_tensor(f"{tag}_pG",
                                                       [1, 512], F32))
                pTT = esA1.enter_context(nc.psum_tensor(f"{tag}_pTT",
                                                        [128, 128], F32))
                pTr = esA1.enter_context(nc.psum_tensor(f"{tag}_pTr",
                                                        [1, 128], F32))
                ev_gst = None
                ev_pgrd = None
                ev_epi_bank = [None, None]
                ev_epi = None
                ev_gg = None
                ev_sq = None
                ev_sq2 = None
                ev_accT = None
                for g in range(NQG):
                    for it in range(4):
                        tt = g * 4 + it
                        pi = it % 2
                        x_ap = X3[:, :, tt * 128:(tt + 1) * 128]
                        evs = P.op("scalar", lambda e, x_ap=x_ap, pi=pi:
                                   e.activation(
                            x2f[pi][:].rearrange("p (c n) -> p c n", c=ND),
                            x_ap, AF.Square, bias=0.0, scale=1.0),
                            [P.full("v"), P.full("t")])
                        ev_m1 = ev_m2 = None
                        for c in range(ND):
                            ev_m1 = P.op("tensor", lambda e, c=c, x_ap=x_ap:
                                         e.matmul(
                                psS[0:1, 0:128], ones_col32[:],
                                x_ap[:, c], start=(c == 0),
                                stop=(c == ND - 1)), [P.full("v")])
                        for c in range(ND):
                            ev_m2 = P.op("tensor", lambda e, c=c, pi=pi:
                                         e.matmul(
                                psS[0:1, 128:256], ones_col32[:],
                                x2f[pi][:, c * 128:(c + 1) * 128],
                                start=(c == 0), stop=(c == ND - 1)), [evs])
                        ev = P.op("vector", lambda e: e.tensor_scalar(
                            rsc[:, 0:128], psS[0:1, 0:128], 1.0 / D, None,
                            ALU.mult), [ev_m1, ev_m2])
                        ev = P.op("vector", lambda e: e.tensor_tensor(
                            rsc[:, 128:256], rsc[:, 0:128], rsc[:, 0:128],
                            ALU.mult), [ev])
                        ev = P.op("vector", lambda e: e.scalar_tensor_tensor(
                            rsc[:, 256:384], psS[0:1, 128:256], 1.0 / D,
                            rsc[:, 128:256], ALU.mult, ALU.subtract), [ev])
                        ev = P.op("vector", lambda e: e.tensor_scalar(
                            rsc[:, 384:512], rsc[:, 256:384], EPS, None,
                            ALU.add), [ev])
                        ev = P.op("vector", lambda e: e.reciprocal(
                            rsc[:, 384:512], rsc[:, 384:512]), [ev])
                        evA1 = P.op("scalar", lambda e: e.activation(
                            rsc[:, 512:640], rsc[:, 384:512], AF.Sqrt,
                            bias=0.0, scale=1.0), [ev])
                        ev = P.op("vector", lambda e: e.tensor_tensor(
                            rsc[:, 128:256], rsc[:, 256:384], rsc[:, 384:512],
                            ALU.mult), [evA1])
                        ev = P.op("vector", lambda e: e.tensor_scalar(
                            rsc[:, 128:256], rsc[:, 128:256], EPS, None,
                            ALU.add), [ev])
                        ev = P.op("vector", lambda e: e.reciprocal(
                            rsc[:, 128:256], rsc[:, 128:256]), [ev])
                        evA2 = P.op("scalar", lambda e: e.activation(
                            rsc[:, 128:256], rsc[:, 128:256], AF.Sqrt,
                            bias=0.0, scale=1.0), [ev])
                        evCC = P.op("vector", lambda e: e.tensor_tensor(
                            rsc[:, 512:640], rsc[:, 512:640], rsc[:, 128:256],
                            ALU.mult), [evA2])
                        evMC = P.op("vector", lambda e: e.scalar_tensor_tensor(
                            rsc[:, 384:512], rsc[:, 0:128], -1.0,
                            rsc[:, 512:640], ALU.mult, ALU.mult), [evCC])
                        eva = evb_ = None
                        for c in range(ND):
                            if c == 0:
                                eva = P.op("vector", lambda e, x_ap=x_ap:
                                           e.tensor_copy(
                                    racc[:], x_ap[:, 0]), [P.full("t")])
                                evb_ = P.op("vector", lambda e, x_ap=x_ap:
                                            e.tensor_copy(
                                    rtmp[:], x_ap[:, 0]), [P.full("t")])
                            else:
                                eva = P.op("vector", lambda e, c=c,
                                           x_ap=x_ap: e.tensor_tensor(
                                    racc[:], racc[:], x_ap[:, c], ALU.max),
                                    [eva])
                                evb_ = P.op("vector", lambda e, c=c,
                                            x_ap=x_ap: e.tensor_tensor(
                                    rtmp[:], rtmp[:], x_ap[:, c], ALU.min),
                                    [evb_])
                        evT = P.op("tensor", lambda e: e.transpose(
                            pTT[:], racc[:], ident32[:]),
                            [eva, ev_id, P.full("v")])
                        evR = P.op("vector", lambda e: e.tensor_reduce(
                            racc[:, 0:1], pTT[:], AX.X, ALU.max), [evT])
                        evT2 = P.op("tensor", lambda e: e.transpose(
                            pTr[0:1, 0:128], racc[:, 0:1], ident32[:]),
                            [evR, P.full("v")])
                        evmx = P.op("vector", lambda e: e.scalar_tensor_tensor(
                            rsc[:, 640:768], rsc[:, 0:128], -1.0,
                            pTr[0:1, 0:128], ALU.mult, ALU.add), [evT2, ev])
                        evT = P.op("tensor", lambda e: e.transpose(
                            pTT[:], rtmp[:], ident32[:]),
                            [evb_, P.full("v")])
                        evR = P.op("vector", lambda e: e.tensor_reduce(
                            rtmp[:, 0:1], pTT[:], AX.X, ALU.min), [evT])
                        evT2 = P.op("tensor", lambda e: e.transpose(
                            pTr[0:1, 0:128], rtmp[:, 0:1], ident32[:]),
                            [evR, P.full("v")])
                        evmn = P.op("vector", lambda e: e.scalar_tensor_tensor(
                            rsc[:, 0:128], pTr[0:1, 0:128], -1.0,
                            rsc[:, 0:128], ALU.mult, ALU.add), [evT2, evmx])
                        ev = P.op("vector", lambda e: e.tensor_tensor(
                            rsc[:, 640:768], rsc[:, 640:768], rsc[:, 0:128],
                            ALU.max), [evmn])
                        ev = P.op("vector", lambda e: e.tensor_tensor(
                            rsc[:, 640:768], rsc[:, 640:768], rsc[:, 512:640],
                            ALU.mult), [ev, evCC])
                        ev = P.op("vector", lambda e: e.tensor_scalar(
                            rsc[:, 0:128], rsc[:, 640:768], 1e-5, 1.0 / 127.0,
                            ALU.max, ALU.mult), [ev])
                        ev_s = P.op("vector", lambda e: e.reciprocal(
                            rsc[:, 640:768], rsc[:, 0:128]), [ev])
                        sl = slice(it * 128, (it + 1) * 128)
                        P.op("vector", lambda e, sl=sl: e.tensor_tensor(
                            rg[:, 0:512][:, sl], rsc[:, 512:640],
                            rsc[:, 640:768], ALU.mult), [ev_s])
                        evsh = P.op("vector", lambda e, sl=sl: e.tensor_tensor(
                            rg[:, 512:1024][:, sl], rsc[:, 384:512],
                            rsc[:, 640:768], ALU.mult), [ev_s, evMC])
                        P.op("vector", lambda e, sl=sl: e.tensor_scalar(
                            rg[:, 1024:1536][:, sl], rsc[:, 0:128],
                            mwsb[0:1, 0:1], None, ALU.mult),
                            [ev_s, P.full("g")])
                    evb_sb = P.op("tensor", lambda e: e.matmul(
                        pBB[:], ones_row[:], rg[:, 0:512],
                        start=True, stop=True), [P.full("v")])
                    evb_sh = P.op("tensor", lambda e: e.matmul(
                        psHr[0][:], ones_row[:], rg[:, 512:1024],
                        start=True, stop=True), [P.full("v")])
                    evc1 = None
                    for q4 in range(4):
                        evbq = P.op("tensor", lambda e, q4=q4: e.matmul(
                            pmB[:], ones_row[:],
                            rg[:, 1024 + q4 * 128:1024 + (q4 + 1) * 128],
                            start=True, stop=True),
                            [P.full("v")] + ([evc1] if evc1 else []))
                        evc1 = P.op("vector", lambda e, q4=q4: e.tensor_copy(
                            c1bg[:, q4 * 128:(q4 + 1) * 128], pmB[:]), [evbq])
                    ev_qgw = []
                    for c in range(ND):
                        pi = c % 2
                        ev1 = P.op("vector", lambda e, c=c, g=g, pi=pi:
                                   e.tensor_tensor(
                            tg[pi][:], X3[:, c, g * 512:(g + 1) * 512],
                            pBB[:], ALU.mult), [evb_sb, P.full("v")])
                        ev2 = P.op("vector", lambda e, pi=pi: e.tensor_tensor(
                            tg[pi][:], tg[pi][:], psHr[0][:], ALU.add),
                            [ev1, evb_sh])
                        ev3 = P.op("vector", lambda e, pi=pi: e.tensor_copy(
                            qi32[pi][:], tg[pi][:]), [ev2])
                        ev_qgw.append(P.op("vector", lambda e, c=c, pi=pi:
                                           e.tensor_copy(
                            Qg3[:, c], qi32[pi][:]), [ev3]))
                    ev_ac = None
                    ev_epi_bank = [ev_qgw[ND - 1], ev_epi_bank[1]]
                    pend_pg = None
                    for f in range(NFC):
                        pb2 = f % 2
                        ph = psHr[pb2]
                        evm = None
                        for c in range(ND):
                            evm = P.op("tensor", lambda e, c=c, f=f, ph=ph:
                                       e.matmul(
                                ph[:], t13[:, c, f * 128:(f + 1) * 128],
                                Qg3[:, c], start=(c == 0),
                                stop=(c == ND - 1)),
                                [ev_t1, ev_qgw[c]] +
                                ([ev_epi_bank[pb2]]
                                 if c == 0 and ev_epi_bank[pb2] else []))
                        if pend_pg is not None:
                            ev_sq = P.op("tensor", *pend_pg)
                            pend_pg = None
                        ev_epi = P.op("vector", lambda e, ph=ph:
                                      e.tensor_tensor(
                            h32[:], ph[:], c1bg[:], ALU.mult),
                            [evm] + ([ev_epi] if ev_epi else []) +
                            ([ev_gg] if ev_gg else []))
                        ev_epi_bank[pb2] = ev_epi
                        ev_gg = P.op("scalar", lambda e, f=f: e.activation(
                            gbf[:], h32[:], AF.Gelu, bias=atl("b1", f),
                            scale=1.0), [ev_epi, ev_atlas] +
                            ([ev_gst] if ev_gst else []) +
                            ([ev_sq2] if ev_sq2 else []))
                        ev_gst = dS.issue(gd3[:, f, g * 512:(g + 1) * 512],
                                          gbf[:], waits=[ev_gg])
                        ev_sq2 = P.op("scalar", lambda e: e.activation(
                            g2b[:], gbf[:], AF.Square, bias=0.0, scale=1.0),
                            [ev_gg] + ([ev_sq] if ev_sq else []))
                        pend_pg = (
                            lambda e, f=f: e.matmul(
                                pG[0:1, :], ones_col_bf[:], g2b[:],
                                start=(f == 0), stop=(f == NFC - 1)),
                            [ev_sq2] + ([ev_pgrd] if f == 0 and ev_pgrd
                                        else []))
                        if f == 0:
                            ev_ac = P.op("vector", lambda e: e.tensor_copy(
                                acc2[:], gbf[:]), [ev_gg] +
                                ([ev_accT] if ev_accT else []))
                            ev_ac2 = P.op("vector", lambda e: e.tensor_copy(
                                acc2n[:], gbf[:]), [ev_gg] +
                                ([ev_accT] if ev_accT else []))
                        else:
                            ev_ac = P.op("vector", lambda e: e.tensor_tensor(
                                acc2[:], acc2[:], gbf[:], ALU.max),
                                [ev_gg, ev_ac])
                            ev_ac2 = P.op("vector", lambda e: e.tensor_tensor(
                                acc2n[:], acc2n[:], gbf[:], ALU.min),
                                [ev_gg, ev_ac2])
                    if pend_pg is not None:
                        ev_sq = P.op("tensor", *pend_pg)
                        pend_pg = None
                    ev_cmb = P.op("vector", lambda e: e.scalar_tensor_tensor(
                        acc2[:], acc2n[:], -1.0, acc2[:], ALU.mult, ALU.max),
                        [ev_ac, ev_ac2])
                    for b4 in range(4):
                        evT = P.op("tensor", lambda e, b4=b4: e.transpose(
                            pTT[:], acc2[:, b4 * 128:(b4 + 1) * 128],
                            ident32[:]), [ev_cmb, ev_id, P.full("v")])
                        ev_accT = evT
                        evR = P.op("vector", lambda e: e.tensor_reduce(
                            rtmp[:, 0:1], pTT[:], AX.X, ALU.max), [evT])
                        evT2 = P.op("tensor", lambda e: e.transpose(
                            pTr[0:1, 0:128], rtmp[:, 0:1], ident32[:]),
                            [evR, P.full("v")])
                        P.op("vector", lambda e, g=g, b4=b4: e.tensor_copy(
                            am2row[:, g * 512 + b4 * 128:
                                   g * 512 + (b4 + 1) * 128],
                            pTr[0:1, 0:128]), [evT2])
                    ev_pgrd = P.op("vector", lambda e, g=g: e.tensor_copy(
                        sgrow[:, g * 512:(g + 1) * 512], pG[0:1, :]), [ev_sq])
                P.barrier()
                P.flush()

            ev = P.op("vector", lambda e: e.tensor_scalar(
                sgrow[:], sgrow[:], 1.0 / FF, EPS, ALU.mult, ALU.add),
                [P.full("v"), P.full("g")])
            ev = P.op("vector", lambda e: e.reciprocal(sgrow[:], sgrow[:]),
                      [ev])
            evA3 = P.op("scalar", lambda e: e.activation(
                sgrow[:], sgrow[:], AF.Sqrt, bias=0.0, scale=1.0), [ev])
            ev = P.op("vector", lambda e: e.tensor_tensor(
                am2row[:], am2row[:], sgrow[:], ALU.mult), [evA3])
            ev = P.op("vector", lambda e: e.tensor_scalar(
                am2row[:], am2row[:], 1e-5, 1.0 / 127.0, ALU.max, ALU.mult),
                [ev])
            ev_s2 = P.op("vector", lambda e: e.reciprocal(
                rowB[:], am2row[:]), [ev])
            ev_sb2 = P.op("vector", lambda e: e.tensor_tensor(
                sgrow[:], sgrow[:], rowB[:], ALU.mult), [ev_s2])
            ev_c2 = P.op("vector", lambda e: e.tensor_scalar(
                am2row[:], am2row[:], mwsb[0:1, 1:2], None, ALU.mult),
                [ev_s2, P.full("g")])

            with ExitStack() as esB:
                t2b = esB.enter_context(
                    nc.sbuf_tensor(f"{tag}_t2b", [128, NFC * D], BF16))
                t23 = t2b[:].rearrange("p (f d) -> p f d", f=NFC)
                ev_t2 = dS.issue(t23,
                                 t2_dram.rearrange("(f p) d -> p f d", p=128),
                                 waits=[P.full("t")])
                sb2b = esB.enter_context(
                    nc.sbuf_tensor(f"{tag}_sb2b", [128, S], F32))
                c2b = esB.enter_context(
                    nc.sbuf_tensor(f"{tag}_c2b", [128, S], F32))
                with ExitStack() as esB1:
                    pBB2 = esB1.enter_context(
                        nc.psum_tensor(f"{tag}_pBB2", [128, 512], F32))
                    evp = None
                    for g in range(NQG):
                        evb = P.op("tensor", lambda e, g=g: e.matmul(
                            pBB2[:], ones_row[:],
                            sgrow[:, g * 512:(g + 1) * 512],
                            start=True, stop=True),
                            [ev_sb2] + ([evp] if evp else []))
                        evp = P.op("vector", lambda e, g=g: e.tensor_copy(
                            sb2b[:, g * 512:(g + 1) * 512], pBB2[:]), [evb])
                        evb = P.op("tensor", lambda e, g=g: e.matmul(
                            pBB2[:], ones_row[:],
                            am2row[:, g * 512:(g + 1) * 512],
                            start=True, stop=True), [ev_c2, evp])
                        evp = P.op("vector", lambda e, g=g: e.tensor_copy(
                            c2b[:, g * 512:(g + 1) * 512], pBB2[:]), [evb])
                    P.barrier()
                    P.flush()
                with ExitStack() as esB2:
                    pyB = [esB2.enter_context(nc.psum_tensor(
                        f"{tag}_pyB{i}", [128, 512], F32)) for i in range(ND)]
                    gblk = [esB2.enter_context(nc.sbuf_tensor(
                        f"{tag}_gblk{i}", [128, 512], FP16)) for i in range(2)]
                    tg2 = [esB2.enter_context(nc.sbuf_tensor(
                        f"{tag}_tg2{i}", [128, 512], F32)) for i in range(2)]
                    qi2 = [esB2.enter_context(nc.sbuf_tensor(
                        f"{tag}_qi2{i}", [128, 512], I32)) for i in range(2)]
                    q2b = [esB2.enter_context(nc.sbuf_tensor(
                        f"{tag}_q2b{i}", [128, 512], BF16)) for i in range(2)]
                    evq_prev = [None, None]
                    for g in range(NQG):
                        for fc in range(NFC):
                            pi = fc % 2
                            w0 = [evq_prev[pi]] if evq_prev[pi] else []
                            evl = dS.issue(gblk[pi][:],
                                           gd3[:, fc, g * 512:(g + 1) * 512],
                                           waits=w0)
                            ev1 = P.op("vector", lambda e, pi=pi, g=g:
                                       e.tensor_tensor(
                                tg2[pi][:], gblk[pi][:],
                                sb2b[:, g * 512:(g + 1) * 512], ALU.mult),
                                [evl] + w0 +
                                ([P.full("v")] if fc < 2 else []))
                            ev2 = P.op("vector", lambda e, pi=pi:
                                       e.tensor_copy(
                                qi2[pi][:], tg2[pi][:]), [ev1])
                            ev3 = P.op("vector", lambda e, pi=pi:
                                       e.tensor_copy(
                                q2b[pi][:], qi2[pi][:]), [ev2])
                            for dd in range(ND):
                                evq_prev[pi] = P.op(
                                    "tensor",
                                    lambda e, fc=fc, dd=dd, pi=pi: e.matmul(
                                        pyB[dd][:],
                                        t23[:, fc, dd * 128:(dd + 1) * 128],
                                        q2b[pi][:], start=(fc == 0),
                                        stop=(fc == NFC - 1)), [ev3, ev_t2])
                        for dd in range(ND):
                            ev1 = P.op("vector", lambda e, dd=dd, g=g:
                                       e.tensor_tensor(
                                tg2[0][:], pyB[dd][:],
                                c2b[:, g * 512:(g + 1) * 512], ALU.mult),
                                [evq_prev[0], evq_prev[1], P.full("v")])
                            P.op("vector", lambda e, dd=dd, g=g:
                                 e.scalar_tensor_tensor(
                                X3[:, dd, g * 512:(g + 1) * 512], tg2[0][:],
                                atl("b2", dd),
                                X3[:, dd, g * 512:(g + 1) * 512],
                                ALU.add, ALU.add), [ev1, ev_atlas])
                    P.barrier()
                    P.flush()

    ternarize_weights()
    ffn_phase("f1")
    if upto == "ffn1":
        return store_out()

    NC16 = S // 128
    k_dram = nc.dram_tensor("k_dram", [NPAIR * 128, S], BF16,
                            kind="Internal").ap()
    vt_dram = nc.dram_tensor("vt_dram", [NPAIR * 128, S], BF16,
                             kind="Internal").ap()
    gn_dram = nc.dram_tensor("gn_dram", [NPAIR * 128, S], BF16,
                             kind="Internal").ap()
    ab1_dram = nc.dram_tensor("ab1_dram", [2, S], F32, kind="Internal").ap()
    xpre_dram = nc.dram_tensor("xpre_dram", [D, S], F32,
                               kind="Internal").ap()
    kd3 = k_dram.rearrange("(r p) s -> p r s", r=NPAIR)
    vd3 = vt_dram.rearrange("(r p) s -> p r s", r=NPAIR)
    gd3b = gn_dram.rearrange("(r p) s -> p r s", r=NPAIR)

    with ExitStack() as esR:
        def sbR(name, shape, dt=F32):
            return esR.enter_context(nc.sbuf_tensor(name, list(shape), dt))

        ccrow = sbR("b_cc", [1, S])
        rows2 = sbR("b_rows2", [2, S])
        onesS = sbR("b_onesS", [1, 512])
        n1sb = sbR("b_n1", [2, D])
        gnwsb = sbR("b_gnw", [1, 2 * D])
        gnbsb = sbR("b_gnb", [1, D])
        coefsb = sbR("b_coef", [128, 128])
        indcol = sbR("b_indcol", [128, 2], BF16)
        snap = sbR("b_snap", [128, NPAIR * (NC16 + 1) * HD], BF16)
        SN4 = snap[:].rearrange("p (r c d) -> p r c d", r=NPAIR, c=NC16 + 1)
        sinall = sbR("b_sin", [128, NPAIR * HD])
        SI3 = sinall[:].rearrange("p (r d) -> p r d", r=NPAIR)
        xnp = sbR("b_xnp", [128, ND * S], BF16)
        XN3 = xnp[:].rearrange("p (c s) -> p c s", c=ND)
        xtmp_b = sbR("b_xtmp", [128, 512])
        wbuf = sbR("b_wbuf", [128, ND * 128], BF16)
        rsc = sbR("b_rsc", [1, 768])

        ev_ld = [dS.issue(n1sb[:], n1wb), dS.issue(gnwsb[:], gnwm),
                 dS.issue(gnbsb[:], gnbrow)]
        ev_xpre = dS.issue(xpre_dram.rearrange("(c p) s -> p c s", p=128),
                           X3, waits=[P.full("v")])
        P.wait("vector", ev_xpre)
        P.op("vector", lambda e: e.memset(onesS[:], 1.0))
        P.op("vector", lambda e: e.memset(rows2[0:2, :], 1.0))
        P.op("vector", lambda e: e.memset(indcol[0:64, 0:1], 1.0))
        P.op("vector", lambda e: e.memset(indcol[64:128, 0:1], 0.0))
        P.op("vector", lambda e: e.memset(indcol[0:64, 1:2], 0.0))
        P.op("vector", lambda e: e.memset(indcol[64:128, 1:2], 1.0))

        with ExitStack() as esC:
            pcf = esC.enter_context(nc.psum_tensor("b_pcf", [128, 128], F32))
            psS = esC.enter_context(nc.psum_tensor("b_psS", [1, 256], F32))
            x2f = [esC.enter_context(nc.sbuf_tensor(f"b_x2f{i}", [128, D],
                                                    F32)) for i in range(2)]
            a1row = esC.enter_context(nc.sbuf_tensor("b_a1", [1, S], F32))
            b1row = esC.enter_context(nc.sbuf_tensor("b_b1", [1, S], F32))
            ev_cf = dS.issue(coefsb[0:1, :], coef_row)
            evb = P.op("tensor", lambda e: e.matmul(
                pcf[:], ones_row[:], coefsb[0:1, :], start=True, stop=True),
                [ev_cf])
            P.op("vector", lambda e: e.tensor_copy(coefsb[:], pcf[:]), [evb])
            for tt in range(NT):
                pi = tt % 2
                x_ap = X3[:, :, tt * 128:(tt + 1) * 128]
                evs = P.op("scalar", lambda e, x_ap=x_ap, pi=pi: e.activation(
                    x2f[pi][:].rearrange("p (c n) -> p c n", c=ND), x_ap,
                    AF.Square, bias=0.0, scale=1.0),
                    [P.full("v"), P.full("t")])
                ev_m1 = ev_m2 = None
                for c in range(ND):
                    ev_m1 = P.op("tensor", lambda e, c=c, x_ap=x_ap: e.matmul(
                        psS[0:1, 0:128], ones_col32[:], x_ap[:, c],
                        start=(c == 0), stop=(c == ND - 1)), [P.full("v")])
                for c in range(ND):
                    ev_m2 = P.op("tensor", lambda e, c=c, pi=pi: e.matmul(
                        psS[0:1, 128:256], ones_col32[:],
                        x2f[pi][:, c * 128:(c + 1) * 128],
                        start=(c == 0), stop=(c == ND - 1)), [evs])
                sl = slice(tt * 128, (tt + 1) * 128)
                ev = P.op("vector", lambda e: e.tensor_scalar(
                    rsc[:, 0:128], psS[0:1, 0:128], 1.0 / D, None, ALU.mult),
                    [ev_m1, ev_m2, P.full("v")])
                ev = P.op("vector", lambda e: e.tensor_tensor(
                    rsc[:, 128:256], rsc[:, 0:128], rsc[:, 0:128], ALU.mult),
                    [ev])
                ev = P.op("vector", lambda e: e.scalar_tensor_tensor(
                    rsc[:, 256:384], psS[0:1, 128:256], 1.0 / D,
                    rsc[:, 128:256], ALU.mult, ALU.subtract), [ev])
                ev = P.op("vector", lambda e: e.tensor_scalar(
                    rsc[:, 384:512], rsc[:, 256:384], EPS, None, ALU.add),
                    [ev])
                ev = P.op("vector", lambda e: e.reciprocal(
                    rsc[:, 384:512], rsc[:, 384:512]), [ev])
                evA1 = P.op("scalar", lambda e: e.activation(
                    rsc[:, 512:640], rsc[:, 384:512], AF.Sqrt, bias=0.0,
                    scale=1.0), [ev, P.full("v")])
                P.op("vector", lambda e, sl=sl: e.tensor_copy(
                    a1row[:, sl], rsc[:, 512:640]), [evA1])
                P.op("vector", lambda e, sl=sl: e.scalar_tensor_tensor(
                    b1row[:, sl], rsc[:, 0:128], -1.0, rsc[:, 512:640],
                    ALU.mult, ALU.mult), [evA1])
                ev = P.op("vector", lambda e: e.tensor_tensor(
                    rsc[:, 128:256], rsc[:, 256:384], rsc[:, 384:512],
                    ALU.mult), [evA1])
                ev = P.op("vector", lambda e: e.tensor_scalar(
                    rsc[:, 128:256], rsc[:, 128:256], EPS, None, ALU.add),
                    [ev])
                ev = P.op("vector", lambda e: e.reciprocal(
                    rsc[:, 128:256], rsc[:, 128:256]), [ev])
                evA2 = P.op("scalar", lambda e: e.activation(
                    rsc[:, 128:256], rsc[:, 128:256], AF.Sqrt, bias=0.0,
                    scale=1.0), [ev])
                evCC = P.op("vector", lambda e, sl=sl: e.tensor_tensor(
                    ccrow[:, sl], rsc[:, 512:640], rsc[:, 128:256],
                    ALU.mult), [evA2])
                P.op("vector", lambda e, sl=sl: e.scalar_tensor_tensor(
                    rows2[0:1, sl], rsc[:, 0:128], -1.0, ccrow[:, sl],
                    ALU.mult, ALU.mult), [evCC])
            dS.issue(ab1_dram[0:1, :], a1row[:], waits=[P.full("v")])
            dS.issue(ab1_dram[1:2, :], b1row[:], waits=[P.full("v")])
            P.barrier()
            P.flush()

        def xn_all(pABx, pBBx, nrows):
            evs = []
            for c in range(ND):
                for g in range(NQG):
                    evb1 = P.op("tensor", lambda e, c=c, g=g, nrows=nrows:
                                e.matmul(
                        pABx[:], nrows[0:1, c * 128:(c + 1) * 128],
                        ccrow[:, g * 512:(g + 1) * 512], start=True,
                        stop=True), ev_ld + [P.full("v")])
                    evb2 = P.op("tensor", lambda e, c=c, g=g, nrows=nrows:
                                e.matmul(
                        pBBx[:], nrows[0:2, c * 128:(c + 1) * 128],
                        rows2[0:2, g * 512:(g + 1) * 512], start=True,
                        stop=True), [P.full("v")])
                    ev1 = P.op("vector", lambda e, c=c, g=g, pABx=pABx:
                               e.tensor_tensor(
                        xtmp_b[:], X3[:, c, g * 512:(g + 1) * 512], pABx[:],
                        ALU.mult), [evb1, P.full("v")])
                    evs.append(P.op("vector", lambda e, c=c, g=g, pBBx=pBBx:
                                    e.tensor_tensor(
                        XN3[:, c, g * 512:(g + 1) * 512], xtmp_b[:],
                        pBBx[:], ALU.add), [ev1, evb2]))
            return evs

        def projp(wsrc, col0, bias_ap, scale, dst_ap, ppb, wl=(), act=None,
                  eguard=(), bank_guard=()):
            w3 = wsrc.rearrange("(c p) n -> p c n", p=128)
            evw = dG.issue(wbuf[:].rearrange("p (c n) -> p c n", c=ND),
                           w3[:, :, col0:col0 + 128], waits=[P.full("t")])
            evs = []
            rot = ppb[0] is not ppb[1]
            nb = 2 if rot else 1
            for g in range(NQG):
                pp_ = ppb[g % 2]
                evm = None
                for c in range(ND):
                    evm = P.op("tensor", lambda e, c=c, g=g, pp_=pp_:
                               e.matmul(
                        pp_[:], wbuf[:, c * 128:(c + 1) * 128],
                        XN3[:, c, g * 512:(g + 1) * 512],
                        start=(c == 0), stop=(c == ND - 1)),
                        [evw, P.full("v")] + list(wl) +
                        ([evs[-nb]] if len(evs) >= nb else []) +
                        (list(bank_guard) if c == 0 and g < nb else []))
                if act is None:
                    evs.append(P.op("vector", lambda e, g=g, pp_=pp_:
                                    e.tensor_scalar(
                        dst_ap[:, g * 512:(g + 1) * 512], pp_[:], bias_ap,
                        scale, ALU.add, ALU.mult), [evm] + list(eguard)))
                else:
                    evs.append(P.op("scalar", lambda e, g=g, pp_=pp_:
                                    e.activation(
                        dst_ap[:, g * 512:(g + 1) * 512], pp_[:], act,
                        bias=bias_ap, scale=1.0), [evm] + list(eguard)))
            return evs

        with ExitStack() as esB1:
            def sbB1(name, shape, dt=F32):
                return esB1.enter_context(nc.sbuf_tensor(name, list(shape),
                                                         dt))
            kpair = sbB1("b_kpair", [128, S], BF16)
            vpair = sbB1("b_vpair", [128, S], BF16)
            vtm = sbB1("b_vtm", [128, S], BF16)
            ktm = sbB1("b_ktm", [128, S], BF16)
            kgt = sbB1("b_kgt", [128, 2 * HD], BF16)
            scur = sbB1("b_scur", [128, HD])
            gpcol = sbB1("b_gpcol", [128, 2])
            ppb = [esB1.enter_context(nc.psum_tensor(f"b_ppb{i}",
                                                      [128, 512], F32))
                   for i in range(2)]
            pABx = esB1.enter_context(nc.psum_tensor("b_pABx", [128, 512],
                                                     F32))
            pBBx = esB1.enter_context(nc.psum_tensor("b_pBBx", [128, 512],
                                                     F32))
            ptrb = esB1.enter_context(nc.psum_tensor("b_ptrb", [128, 128],
                                                     BF16))
            pU = esB1.enter_context(nc.psum_tensor("b_pU", [128, HD], F32))
            prev_store = []
            evxn = xn_all(pABx, pBBx, n1sb)
            for p in range(NPAIR):
                ev_k = projp(wk_b, 128 * p, atl("bk", p), 1.0, kpair[:], ppb,
                             eguard=prev_store)
                ev_v = projp(wv_b, 128 * p, atl("bv", p), 1.0, vpair[:], ppb,
                             wl=ev_k)
                ev_ks = dS.issue(kd3[:, p], kpair[:], waits=ev_k)
                ev_t = []
                for a in range(NC16):
                    evt = P.op("tensor", lambda e, a=a: e.transpose(
                        ptrb[:], vpair[:, a * 128:(a + 1) * 128], identbf[:]),
                        ev_v + [P.full("v")])
                    ev_t.append(P.op("vector", lambda e, a=a: e.tensor_copy(
                        vtm[:, a * 128:(a + 1) * 128], ptrb[:]),
                        [evt] + prev_store))
                for a in range(NC16):
                    evt = P.op("tensor", lambda e, a=a: e.transpose(
                        ptrb[:], kpair[:, a * 128:(a + 1) * 128], identbf[:]),
                        ev_k + [P.full("v")])
                    ev_t.append(P.op("vector", lambda e, a=a: e.tensor_copy(
                        ktm[:, a * 128:(a + 1) * 128], ptrb[:]), [evt]))
                ev_vs = dS.issue(vd3[:, p], vtm[:], waits=[P.full("v")])
                prev_store = [ev_ks, ev_vs]
                evg = []
                for hb in (0, 1):
                    h = 2 * p + hb
                    evg.append(P.op("scalar", lambda e, hb=hb, h=h:
                                    e.activation(
                        gpcol[:, hb:hb + 1], iocol127[:], AF.Exp,
                        bias=0.0, scale=-LNG[h]),
                        [ev_iocol, P.full("v")]))
                ev_sn0 = P.op("vector", lambda e, p=p: e.memset(
                    SN4[:, p, 0], 0.0), [P.full("t")])
                ev_sc0 = P.op("vector", lambda e: e.memset(scur[:], 0.0),
                              [P.full("t")])
                ev_prev = ev_sc0
                for c in range(NC16):
                    ev_kg = P.op("vector", lambda e, c=c: e.tensor_scalar(
                        kgt[:, 0:HD],
                        ktm[:, c * 128:(c + 1) * 128]
                        .rearrange("p (hb d) -> p hb d", hb=2)[:, 0],
                        gpcol[:, 0:1], None, ALU.mult),
                        ev_t + evg + [ev_prev])
                    ev_kg2 = P.op("vector", lambda e, c=c: e.tensor_scalar(
                        kgt[:, HD:2 * HD],
                        ktm[:, c * 128:(c + 1) * 128]
                        .rearrange("p (hb d) -> p hb d", hb=2)[:, 1],
                        gpcol[:, 1:2], None, ALU.mult), [ev_prev])
                    evu = []
                    for hb in (0, 1):
                        evu.append(P.op("tensor",
                                        lambda e, c=c, hb=hb: e.matmul(
                            pU[hb * HD:(hb + 1) * HD, :],
                            kgt[:, hb * HD:(hb + 1) * HD],
                            vtm[:, c * 128 + hb * HD:c * 128 + hb * HD + HD],
                            start=True, stop=True,
                            tile_position=(0, hb * HD)),
                            [ev_kg, ev_kg2, P.full("v")]))
                    ev_prev = None
                    for hb in (0, 1):
                        g128 = GAMMAS[2 * p + hb] ** 128
                        evs_ = P.op("vector",
                                    lambda e, hb=hb, g128=g128:
                                    e.scalar_tensor_tensor(
                            scur[hb * HD:(hb + 1) * HD, :],
                            scur[hb * HD:(hb + 1) * HD, :], g128,
                            pU[hb * HD:(hb + 1) * HD, :], ALU.mult, ALU.add),
                            evu + ([ev_prev] if ev_prev else []))
                        ev_prev = P.op("vector",
                                       lambda e, c=c, hb=hb, p=p:
                                       e.tensor_copy(
                            SN4[:, p, c + 1][hb * HD:(hb + 1) * HD, :],
                            scur[hb * HD:(hb + 1) * HD, :]), [evs_])
            P.barrier()
            P.flush()
        if upto == "bB1":
            return store_out()

        ev_ag_in = dS.issue(
            cc_in.rearrange("(pr hb d1) d2 -> (hb d1) pr d2", pr=NPAIR, hb=2),
            SN4[:, :, NC16], waits=[P.full("v")])
        ev_cc = P.op("gpsimd", lambda e: e.collective_compute(
            "AllGather", ALU.bypass, replica_groups=[list(range(n_cores))],
            ins=[cc_in], outs=[cc_out]), [ev_ag_in], sem="cc")
        with ExitStack() as esAG:
            agall = esAG.enter_context(nc.sbuf_tensor(
                "b_agall", [128, n_cores * NPAIR * HD], BF16))
            AG4 = agall[:].rearrange("p (r pr d) -> p r pr d", r=n_cores,
                                     pr=NPAIR)
            ev_agl = dS.issue(
                AG4, cc_out.rearrange("(r pr hb d1) d2 -> (hb d1) r pr d2",
                                      r=n_cores, pr=NPAIR, hb=2),
                waits=[ev_cc])
            for p in range(NPAIR):
                for hb in (0, 1):
                    h = 2 * p + hb
                    ev = None
                    for r in range(n_cores):
                        if r == 0:
                            ev = P.op("vector",
                                      lambda e, p=p, hb=hb, h=h, r=r:
                                      e.tensor_scalar(
                                SI3[:, p][hb * HD:(hb + 1) * HD, :],
                                AG4[:, r, p][hb * HD:(hb + 1) * HD, :],
                                coefsb[hb * HD:(hb + 1) * HD,
                                       r * NH + h:r * NH + h + 1],
                                None, ALU.mult), [ev_agl, P.full("t")])
                        else:
                            ev = P.op("vector",
                                      lambda e, p=p, hb=hb, h=h, r=r:
                                      e.scalar_tensor_tensor(
                                SI3[:, p][hb * HD:(hb + 1) * HD, :],
                                AG4[:, r, p][hb * HD:(hb + 1) * HD, :],
                                coefsb[hb * HD:(hb + 1) * HD,
                                       r * NH + h:r * NH + h + 1],
                                SI3[:, p][hb * HD:(hb + 1) * HD, :],
                                ALU.mult, ALU.add), [ev])
            P.barrier()
            P.flush()

        with ExitStack() as esB2:
            def sbB2(name, shape, dt=F32):
                return esB2.enter_context(nc.sbuf_tensor(name, list(shape),
                                                         dt))
            qp = sbB2("b_qp", [128, S], BF16)
            qgam = sbB2("b_qgam", [128, S], BF16)
            sg = sbB2("b_sg", [128, S])
            kpair2 = sbB2("b_kpair2", [128, S], BF16)
            vtm2 = sbB2("b_vtm2", [128, S], BF16)
            NHV = 2 if NQG >= 2 else 1
            rp = sbB2("b_rp", [128, S // NHV])
            rpb = sbB2("b_rpb", [128, 512], BF16)
            rp2b = sbB2("b_rp2b", [128, 512], BF16)
            gnp = sbB2("b_gnp", [128, S // NHV], BF16)
            stot = sbB2("b_stot", [128, HD], BF16)
            ab2 = sbB2("b_ab2", [128, 256], BF16)
            dm = sbB2("b_dm", [128, 256])
            grow = sbB2("b_grow", [1, 256])
            rgn = sbB2("b_rgn", [1, 4 * 512])
            gB = sbB2("b_gB", [128, 128])
            with ExitStack() as esPs:
                ppb2s = esPs.enter_context(nc.psum_tensor(
                    "b_ppb2", [128, 512], F32))
                ppb2 = [ppb2s, ppb2s]
                pABx2 = esPs.enter_context(nc.psum_tensor("b_pABx2",
                                                          [128, 512], F32))
                pBBx2 = esPs.enter_context(nc.psum_tensor("b_pBBx2",
                                                          [128, 512], F32))
                psCe = esPs.enter_context(nc.psum_tensor("b_psCe",
                                                          [128, 128], F32))
                psCo = esPs.enter_context(nc.psum_tensor("b_psCo",
                                                          [128, 128], F32))
                pret = esPs.enter_context(nc.psum_tensor("b_pret", [128, 128],
                                                         F32))
                pgs = esPs.enter_context(nc.psum_tensor("b_pgs", [1, 1024],
                                                        F32))
                ev_g_prev = []
                for p in range(NPAIR):
                    ev_q = projp(wq_b, 128 * p, atl("bq", p), QSC, qp[:],
                                 ppb2, bank_guard=ev_g_prev[-2:])
                    ev_g = projp(wg_b, 128 * p, atl("bg", p), 1.0, sg[:],
                                 ppb2, wl=ev_q, act=AF.Silu)
                    ev_g_prev = ev_g
                    ev_kl = dS.issue(kpair2[:], kd3[:, p],
                                     waits=[P.full("t")])
                    ev_vl = dS.issue(vtm2[:], vd3[:, p], waits=[P.full("t")])
                    evgr = []
                    for hb in (0, 1):
                        h = 2 * p + hb
                        evgr.append(P.op("scalar", lambda e, hb=hb, h=h:
                                         e.activation(
                            grow[0:1, hb * 128:(hb + 1) * 128], iorow1[:],
                            AF.Exp, bias=0.0, scale=LNG[h]),
                            [ev_iorow, P.full("v")]))
                    evgb = P.op("tensor", lambda e: e.matmul(
                        pABx2[0:128, 0:128], indrow32[0:1, 0:128],
                        grow[0:1, 0:128], start=True, stop=False),
                        evgr + [P.full("v")])
                    evgb = P.op("tensor", lambda e: e.matmul(
                        pABx2[0:128, 0:128], indrow32[0:1, 128:256],
                        grow[0:1, 128:256], start=False, stop=True), [])
                    evgbc = P.op("vector", lambda e: e.tensor_copy(
                        gB[:], pABx2[0:128, 0:128]), [evgb])
                    ev_qg = P.op("vector", lambda e: e.tensor_tensor(
                        qgam[:].rearrange("p (a n) -> p a n", n=128),
                        qp[:].rearrange("p (a n) -> p a n", n=128),
                        gB[:].unsqueeze(1).broadcast_to([128, NC16, 128]),
                        ALU.mult), ev_q + [evgbc])
                    evdm = []
                    for hb in (0, 1):
                        h = 2 * p + hb
                        eve = P.op("scalar", lambda e, hb=hb, h=h:
                                   e.activation(
                            dm[:, hb * 128:(hb + 1) * 128], iojp[:], AF.Exp,
                            bias=0.0, scale=LNG[h]), [ev_io, P.full("v")])
                        evdm.append(P.op("vector", lambda e, hb=hb:
                                         e.tensor_tensor(
                            dm[:, hb * 128:(hb + 1) * 128],
                            dm[:, hb * 128:(hb + 1) * 128], maskge[:],
                            ALU.mult), [eve, ev_mge]))
                    ev_rp_prev = None
                    ev_at_prev = None
                    ev_gn_st = None
                    for half in range(NHV):
                     for c in range(half * (NC16 // NHV),
                                    (half + 1) * (NC16 // NHV)):
                         evst = []
                         for hb in (0, 1):
                             g128c = GAMMAS[2 * p + hb] ** (128 * c)
                             evst.append(P.op("vector",
                                              lambda e, c=c, hb=hb, p=p,
                                              g128c=g128c:
                                              e.scalar_tensor_tensor(
                                 stot[hb * HD:(hb + 1) * HD, :],
                                 SI3[:, p][hb * HD:(hb + 1) * HD, :], g128c,
                                 SN4[:, p, c][hb * HD:(hb + 1) * HD, :],
                                 ALU.mult, ALU.add), [P.full("t")]))
                         evC = []
                         for hb, psC in ((0, psCe), (1, psCo)):
                             evC.append(P.op("tensor",
                                             lambda e, c=c, hb=hb, psC=psC,
                                             p=p: e.matmul(
                                 psC[:], kpair2[hb * HD:(hb + 1) * HD,
                                                c * 128:(c + 1) * 128],
                                 qp[hb * HD:(hb + 1) * HD,
                                    c * 128:(c + 1) * 128],
                                 start=True, stop=True,
                                 tile_position=(hb * HD, 0)),
                                 [ev_kl] + ev_q +
                                 ([ev_at_prev] if ev_at_prev else [])))
                         ev_at = []
                         for hb, psC in ((0, psCe), (1, psCo)):
                             ev_at.append(P.op("vector",
                                               lambda e, hb=hb, psC=psC:
                                               e.tensor_tensor(
                                 ab2[:, hb * 128:(hb + 1) * 128], psC[:],
                                 dm[:, hb * 128:(hb + 1) * 128], ALU.mult),
                                 [evC[hb]] + evdm))
                         ev_at_prev = ev_at[1]
                         evR = []
                         for hb in (0, 1):
                             evR.append(P.op("tensor",
                                             lambda e, c=c, hb=hb: e.matmul(
                                 pret[hb * HD:(hb + 1) * HD, :],
                                 vtm2[:, c * 128 + hb * HD:
                                      c * 128 + hb * HD + HD],
                                 ab2[:, hb * 128:(hb + 1) * 128],
                                 start=True, stop=False,
                                 tile_position=(0, hb * HD)),
                                 [ev_vl, ev_at[hb]] +
                                 ([ev_rp_prev] if ev_rp_prev else [])))
                             evR.append(P.op("tensor",
                                             lambda e, c=c, hb=hb: e.matmul(
                                 pret[hb * HD:(hb + 1) * HD, :],
                                 stot[hb * HD:(hb + 1) * HD, :],
                                 qgam[hb * HD:(hb + 1) * HD,
                                      c * 128:(c + 1) * 128],
                                 start=False, stop=True,
                                 tile_position=(hb * HD, hb * HD)),
                                 [evst[hb], ev_qg]))
                         ev_rp_prev = P.op("vector", lambda e, c=c:
                                           e.tensor_copy(
                             rp[:, (c % (NC16 // NHV)) * 128:(c % (NC16 // NHV) + 1) * 128], pret[:]), evR)
                     for g in range(half * (NQG // NHV),
                                    (half + 1) * (NQG // NHV)):
                         sl2 = slice(g * 512, (g + 1) * 512)
                         slh = slice((g % (NQG // NHV)) * 512,
                                     (g % (NQG // NHV) + 1) * 512)
                         evcp = P.op("vector", lambda e, slh=slh:
                                     e.tensor_copy(
                             rpb[:], rp[:, slh]), [ev_rp_prev, P.full("s")])
                         evsq = P.op("scalar", lambda e: e.activation(
                             rp2b[:], rpb[:], AF.Square, bias=0.0, scale=1.0),
                             [evcp])
                         evm = []
                         for hb in (0, 1):
                             evm.append(P.op("tensor", lambda e, hb=hb:
                                             e.matmul(
                                 pgs[0:1, hb * 512:(hb + 1) * 512],
                                 indcol[:, hb:hb + 1], rpb[:],
                                 start=True, stop=True),
                                 [evcp, P.full("v")]))
                         ev1 = P.op("vector", lambda e: e.tensor_scalar(
                             rgn[:, 0:1024], pgs[0:1, 0:1024], 1.0 / HD, None,
                             ALU.mult), evm)
                         evm2 = []
                         for hb in (0, 1):
                             evm2.append(P.op("tensor", lambda e, hb=hb:
                                              e.matmul(
                                 pgs[0:1, hb * 512:(hb + 1) * 512],
                                 indcol[:, hb:hb + 1], rp2b[:],
                                 start=True, stop=True), [evsq, ev1]))
                         ev2 = P.op("vector", lambda e: e.tensor_tensor(
                             rgn[:, 1024:2048], rgn[:, 0:1024],
                             rgn[:, 0:1024], ALU.mult), [ev1])
                         ev3 = P.op("vector", lambda e: e.scalar_tensor_tensor(
                             rgn[:, 1024:2048], pgs[0:1, 0:1024], 1.0 / HD,
                             rgn[:, 1024:2048], ALU.mult, ALU.subtract),
                             evm2 + [ev2])
                         ev4 = P.op("vector", lambda e: e.tensor_scalar(
                             rgn[:, 1024:2048], rgn[:, 1024:2048], EPS, None,
                             ALU.add), [ev3])
                         ev5 = P.op("vector", lambda e: e.reciprocal(
                             rgn[:, 1024:2048], rgn[:, 1024:2048]), [ev4])
                         ev6 = P.op("scalar", lambda e: e.activation(
                             rgn[:, 1024:2048], rgn[:, 1024:2048], AF.Sqrt,
                             bias=0.0, scale=1.0), [ev5])
                         ev7 = P.op("vector", lambda e: e.scalar_tensor_tensor(
                             rgn[:, 0:1024], rgn[:, 0:1024], -1.0,
                             rgn[:, 1024:2048], ALU.mult, ALU.mult), [ev6])
                         evb1 = P.op("tensor", lambda e, p=p: e.matmul(
                             pABx2[:], gnwsb[0:1, p * 128:(p + 1) * 128],
                             rgn[:, 1024:1536], start=True, stop=False),
                             [ev6, P.full("v")])
                         evb1 = P.op("tensor", lambda e, p=p: e.matmul(
                             pABx2[:], gnwsb[0:1, D + p * 128:
                                             D + (p + 1) * 128],
                             rgn[:, 1536:2048], start=False, stop=True), [])
                         evb2 = P.op("tensor", lambda e, p=p: e.matmul(
                             pBBx2[:], gnwsb[0:1, p * 128:(p + 1) * 128],
                             rgn[:, 0:512], start=True, stop=False),
                             [ev7, P.full("v")])
                         evb2 = P.op("tensor", lambda e, p=p: e.matmul(
                             pBBx2[:], gnwsb[0:1, D + p * 128:
                                             D + (p + 1) * 128],
                             rgn[:, 512:1024], start=False, stop=False), [])
                         evb2 = P.op("tensor", lambda e, p=p: e.matmul(
                             pBBx2[:], gnbsb[0:1, p * 128:(p + 1) * 128],
                             onesS[:], start=False, stop=True), [])
                         evap = P.op("vector", lambda e, slh=slh:
                                     e.tensor_tensor(
                             xtmp_b[:], rp[:, slh], pABx2[:], ALU.mult),
                             [evb1, P.full("v")])
                         evap = P.op("vector", lambda e: e.tensor_tensor(
                             xtmp_b[:], xtmp_b[:], pBBx2[:], ALU.add),
                             [evap, evb2])
                         evap = P.op("vector", lambda e, sl2=sl2, slh=slh:
                                     e.tensor_tensor(
                             gnp[:, slh], xtmp_b[:], sg[:, sl2], ALU.mult),
                             [evap] + ev_g)
                         ev_gn_st = dS.issue(gd3b[:, p, sl2], gnp[:, slh],
                                             waits=[evap] +
                                             ([ev_gn_st] if ev_gn_st else []))
                     P.wait("vector", ev_gn_st)
                P.barrier()
                P.flush()
        if upto == "bB2":
            return store_out()

        with ExitStack() as esWo:
            wob2 = esWo.enter_context(nc.sbuf_tensor("b_wob2",
                                                     [128, ND * D], BF16))
            gnall = [esWo.enter_context(nc.sbuf_tensor(
                f"b_gnall{i}", [128, ND * 512], BF16)) for i in range(2)]
            pwo = [esWo.enter_context(nc.psum_tensor(f"b_pwo{i}", [128, 512],
                                                     F32)) for i in range(2)]
            pA1 = esWo.enter_context(nc.psum_tensor("b_pA1", [128, 512], F32))
            pB1 = esWo.enter_context(nc.psum_tensor("b_pB1", [128, 512], F32))
            ab1sb = esWo.enter_context(nc.sbuf_tensor("b_ab1", [2, S], F32))
            sel2 = esWo.enter_context(nc.sbuf_tensor("b_sel2", [2, 128], F32))
            ev_ab1 = dS.issue(ab1sb[:], ab1_dram, waits=[P.full("v")])
            evs2 = P.op("vector", lambda e: e.memset(sel2[0:2, :], 1.0))
            evs2 = P.op("vector", lambda e: e.memset(sel2[0:1, :], 0.0),
                        [evs2])
            wo3 = wo_b.rearrange("(c p) n -> p c n", p=128)
            ev_wo = dG.issue(wob2[:].rearrange("p (c n) -> p c n", c=ND),
                             wo3, waits=[P.full("t")])
            ev3 = None
            prev_use = [None, None]
            rd_prev = [None, None]
            for g in range(NQG):
                pi = g % 2
                w0 = [prev_use[pi]] if prev_use[pi] else []
                evl = dS.issue(gnall[pi][:].rearrange("p (c n) -> p c n",
                                                      c=ND),
                               gd3b[:, :, g * 512:(g + 1) * 512], waits=w0)
                evbA = P.op("tensor", lambda e, g=g: e.matmul(
                    pA1[:], ones_row[:], ab1sb[0:1, g * 512:(g + 1) * 512],
                    start=True, stop=True), [ev_ab1, P.full("v")])
                evbB = P.op("tensor", lambda e, g=g: e.matmul(
                    pB1[:], sel2[0:2, :], ab1sb[0:2, g * 512:(g + 1) * 512],
                    start=True, stop=True), [ev_ab1, evs2, P.full("v")])
                evm = None
                for dd in range(ND):
                    bi = dd % 2
                    for c in range(ND):
                        evm = P.op("tensor", lambda e, c=c, dd=dd, bi=bi,
                                   pi=pi: e.matmul(
                            pwo[bi][:], wob2[:, c * D + dd * 128:
                                             c * D + (dd + 1) * 128],
                            gnall[pi][:, c * 512:(c + 1) * 512],
                            start=(c == 0), stop=(c == ND - 1)),
                            [ev_wo, evl] +
                            ([rd_prev[bi]] if c == 0 and rd_prev[bi]
                             else []))
                    ev1 = P.op("vector", lambda e, dd=dd, g=g:
                               e.tensor_tensor(
                        xtmp_b[:], X3[:, dd, g * 512:(g + 1) * 512], pA1[:],
                        ALU.mult), [evbA] + ([ev3] if ev3 else []))
                    ev2 = P.op("vector", lambda e: e.tensor_tensor(
                        xtmp_b[:], xtmp_b[:], pB1[:], ALU.add), [ev1, evbB])
                    ev3 = P.op("vector", lambda e, dd=dd, g=g, bi=bi:
                               e.scalar_tensor_tensor(
                        X3[:, dd, g * 512:(g + 1) * 512], pwo[bi][:],
                        atl("bo", dd), xtmp_b[:],
                        ALU.add, ALU.add), [ev2, evm, ev_atlas])
                    rd_prev[bi] = ev3
                prev_use[pi] = evm
            P.barrier()
            P.flush()
        if upto == "bwo":
            return store_out()

    with ExitStack() as esG:
        def sbG(name, shape, dt=F32):
            return esG.enter_context(nc.sbuf_tensor(name, list(shape), dt))

        ccr2 = sbG("g_cc", [1, S])
        row22 = sbG("g_rows2", [2, S])
        n2sb = sbG("g_n2", [2, D])
        x2ball = sbG("g_x2b", [128, ND * S], BF16)
        XB3 = x2ball[:].rearrange("p (c s) -> p c s", c=ND)
        xtmp_g = sbG("g_xtmp", [128, 512])
        rsc2 = sbG("g_rsc", [1, 768])
        ev_n2 = dS.issue(n2sb[:], n2wb)
        P.op("vector", lambda e: e.memset(row22[0:2, :], 1.0))
        with ExitStack() as esGs:
            psS = esGs.enter_context(nc.psum_tensor("g_psS", [1, 256], F32))
            pAB = esGs.enter_context(nc.psum_tensor("g_pAB", [128, 512], F32))
            pBB = esGs.enter_context(nc.psum_tensor("g_pBB", [128, 512], F32))
            x2f = [esGs.enter_context(nc.sbuf_tensor(f"g_x2f{i}", [128, D],
                                                     F32)) for i in range(2)]
            for tt in range(NT):
                pi = tt % 2
                x_ap = X3[:, :, tt * 128:(tt + 1) * 128]
                evs = P.op("scalar", lambda e, x_ap=x_ap, pi=pi: e.activation(
                    x2f[pi][:].rearrange("p (c n) -> p c n", c=ND), x_ap,
                    AF.Square, bias=0.0, scale=1.0),
                    [P.full("v"), P.full("t")])
                ev_m1 = ev_m2 = None
                for c in range(ND):
                    ev_m1 = P.op("tensor", lambda e, c=c, x_ap=x_ap: e.matmul(
                        psS[0:1, 0:128], ones_col32[:], x_ap[:, c],
                        start=(c == 0), stop=(c == ND - 1)), [P.full("v")])
                for c in range(ND):
                    ev_m2 = P.op("tensor", lambda e, c=c, pi=pi: e.matmul(
                        psS[0:1, 128:256], ones_col32[:],
                        x2f[pi][:, c * 128:(c + 1) * 128],
                        start=(c == 0), stop=(c == ND - 1)), [evs])
                sl = slice(tt * 128, (tt + 1) * 128)
                ev = P.op("vector", lambda e: e.tensor_scalar(
                    rsc2[:, 0:128], psS[0:1, 0:128], 1.0 / D, None,
                    ALU.mult), [ev_m1, ev_m2])
                ev = P.op("vector", lambda e: e.tensor_tensor(
                    rsc2[:, 128:256], rsc2[:, 0:128], rsc2[:, 0:128],
                    ALU.mult), [ev])
                ev = P.op("vector", lambda e: e.scalar_tensor_tensor(
                    rsc2[:, 256:384], psS[0:1, 128:256], 1.0 / D,
                    rsc2[:, 128:256], ALU.mult, ALU.subtract), [ev])
                ev = P.op("vector", lambda e: e.tensor_scalar(
                    rsc2[:, 384:512], rsc2[:, 256:384], EPS, None, ALU.add),
                    [ev])
                ev = P.op("vector", lambda e: e.reciprocal(
                    rsc2[:, 384:512], rsc2[:, 384:512]), [ev])
                evA1 = P.op("scalar", lambda e: e.activation(
                    rsc2[:, 512:640], rsc2[:, 384:512], AF.Sqrt, bias=0.0,
                    scale=1.0), [ev])
                evCC = P.op("vector", lambda e, sl=sl: e.tensor_copy(
                    ccr2[:, sl], rsc2[:, 512:640]), [evA1])
                P.op("vector", lambda e, sl=sl: e.scalar_tensor_tensor(
                    row22[0:1, sl], rsc2[:, 0:128], -1.0, ccr2[:, sl],
                    ALU.mult, ALU.mult), [evCC])
            for p in range(ND):
                for g in range(NQG):
                    evb1 = P.op("tensor", lambda e, p=p, g=g: e.matmul(
                        pAB[:], n2sb[0:1, p * 128:(p + 1) * 128],
                        ccr2[:, g * 512:(g + 1) * 512], start=True,
                        stop=True), [ev_n2, P.full("v")])
                    evb2 = P.op("tensor", lambda e, p=p, g=g: e.matmul(
                        pBB[:], n2sb[0:2, p * 128:(p + 1) * 128],
                        row22[0:2, g * 512:(g + 1) * 512], start=True,
                        stop=True), [P.full("v")])
                    ev1 = P.op("vector", lambda e, p=p, g=g: e.tensor_tensor(
                        xtmp_g[:], X3[:, p, g * 512:(g + 1) * 512], pAB[:],
                        ALU.mult), [evb1, P.full("v")])
                    P.op("vector", lambda e, p=p, g=g: e.tensor_tensor(
                        XB3[:, p, g * 512:(g + 1) * 512], xtmp_g[:], pBB[:],
                        ALU.add), [ev1, evb2])
            P.barrier()
            P.flush()
        gld3 = glu_dram.rearrange("(j p) s -> p j s", p=128)
        with ExitStack() as esGi:
            HHALF = HIDP
            wibuf = esGi.enter_context(nc.sbuf_tensor(
                "g_wibuf", [128, ND * HHALF], BF16))
            wi4 = wibuf[:].rearrange("p (c n) -> p c n", c=ND)
            su = esGi.enter_context(nc.sbuf_tensor("g_su", [128, 512], F32))
            g2v = esGi.enter_context(nc.sbuf_tensor("g_g2v", [128, 512],
                                                    F32))
            glub = esGi.enter_context(nc.sbuf_tensor("g_glub", [128, 512],
                                                     BF16))
            psUr = [esGi.enter_context(nc.psum_tensor(
                f"g_psU{i}", [128, 512], F32)) for i in range(2)]
            psG2r = [esGi.enter_context(nc.psum_tensor(
                f"g_psG2{i}", [128, 512], F32)) for i in range(2)]
            wiT3 = w_inT.rearrange("(c p) n -> p c n", p=128)
            ev_gs = None
            evgl = None
            it2 = 0
            evsu_bank = [None, None]
            evg2_bank = [None, None]
            for wh in range(2):
                ev_wi = dG.issue(wi4, wiT3[:, :, wh * HHALF:(wh + 1) * HHALF],
                                 waits=[P.full("t")])
                for j2 in range(wh * (NF2 // 2), (wh + 1) * (NF2 // 2)):
                  for g in range(NQG):
                    pi = it2 % 2
                    it2 += 1
                    cb = (2 * j2) * 128 - wh * HHALF
                    evmU = evmG = None
                    for c in range(ND):
                        evmU = P.op("tensor", lambda e, c=c, cb=cb, g=g,
                                    pi=pi: e.matmul(
                            psUr[pi][:], wi4[:, c, cb:cb + 128],
                            XB3[:, c, g * 512:(g + 1) * 512],
                            start=(c == 0), stop=(c == ND - 1)),
                            [ev_wi] + ([evsu_bank[pi]]
                                       if c == 0 and evsu_bank[pi] else []))
                    for c in range(ND):
                        evmG = P.op("tensor", lambda e, c=c, cb=cb, g=g,
                                    pi=pi: e.matmul(
                            psG2r[pi][:], wi4[:, c, cb + 128:cb + 256],
                            XB3[:, c, g * 512:(g + 1) * 512],
                            start=(c == 0), stop=(c == ND - 1)),
                            [ev_wi] + ([evg2_bank[pi]]
                                       if c == 0 and evg2_bank[pi] else []))
                    evsu = P.op("scalar", lambda e, j2=j2, pi=pi:
                                e.activation(
                        su[:], psUr[pi][:], AF.Silu, bias=atl("b_in", 2 * j2),
                        scale=1.0), [evmU, ev_atlas] +
                        ([evgl] if evgl else []))
                    evg2 = P.op("vector", lambda e, j2=j2, pi=pi:
                                e.tensor_scalar(
                        g2v[:], psG2r[pi][:], atl("b_in", 2 * j2 + 1), None,
                        ALU.add), [evmG, ev_atlas] +
                        ([evgl] if evgl else []))
                    evgl = P.op("vector", lambda e: e.tensor_tensor(
                        glub[:], su[:], g2v[:], ALU.mult),
                        [evsu, evg2] + ([ev_gs] if ev_gs else []))
                    ev_gs = dS.issue(gld3[:, j2, g * 512:(g + 1) * 512],
                                     glub[:], waits=[evgl])
                    evsu_bank[pi] = evsu
                    evg2_bank[pi] = evgl

            P.wait("vector", ev_gs)
            P.barrier()
            P.flush()
        with ExitStack() as esGo:
            wobuf = esGo.enter_context(nc.sbuf_tensor(
                "g_wobuf", [128, NF2 * D], BF16))
            wo4 = wobuf[:].rearrange("p (j n) -> p j n", j=NF2)
            glblk = [esGo.enter_context(nc.sbuf_tensor(
                f"g_glblk{i}", [128, 512], BF16)) for i in range(2)]
            pyO = [esGo.enter_context(nc.psum_tensor(
                f"g_pyO{i}", [128, 512], F32)) for i in range(ND)]
            ev_wog = dG.issue(wo4,
                              w_outT.rearrange("(j p) n -> p j n", p=128),
                              waits=[P.full("t")])
            evq_prev = [None, None]
            ev_xup = None
            for g in range(NQG):
                for j2 in range(NF2):
                    pi = j2 % 2
                    w0 = [evq_prev[pi]] if evq_prev[pi] else []
                    evl = dS.issue(glblk[pi][:],
                                   gld3[:, j2, g * 512:(g + 1) * 512],
                                   waits=w0)
                    for dd in range(ND):
                        evq_prev[pi] = P.op("tensor",
                                            lambda e, j2=j2, dd=dd, pi=pi:
                                            e.matmul(
                            pyO[dd][:], wo4[:, j2, dd * 128:(dd + 1) * 128],
                            glblk[pi][:], start=(j2 == 0),
                            stop=(j2 == NF2 - 1)), [evl, ev_wog] +
                            ([ev_xup] if j2 == 0 and ev_xup else []))
                for dd in range(ND):
                    ev_xup = P.op("vector", lambda e, dd=dd, g=g:
                                  e.scalar_tensor_tensor(
                        X3[:, dd, g * 512:(g + 1) * 512], pyO[dd][:],
                        atl("b_out", dd), X3[:, dd, g * 512:(g + 1) * 512],
                        ALU.add, ALU.add),
                        [evq_prev[0], evq_prev[1], P.full("v")])
            P.barrier()
            P.flush()

    with ExitStack() as esXa:
        xrb = [esXa.enter_context(nc.sbuf_tensor(f"xr{i}", [128, ND * 512],
                                                 F32)) for i in range(2)]
        xo3 = xpre_dram.rearrange("(c p) s -> p c s", p=128)
        prev_add = [None, None]
        for g in range(NQG):
            pi = g % 2
            w0 = [prev_add[pi]] if prev_add[pi] else []
            evl = dS.issue(xrb[pi][:].rearrange("p (c n) -> p c n", c=ND),
                           xo3[:, :, g * 512:(g + 1) * 512], waits=w0)
            prev_add[pi] = P.op("vector", lambda e, g=g, pi=pi:
                                e.tensor_tensor(
                X3[:, :, g * 512:(g + 1) * 512],
                X3[:, :, g * 512:(g + 1) * 512],
                xrb[pi][:].rearrange("p (c n) -> p c n", c=ND), ALU.add),
                [evl, P.full("v"), P.full("s")])
        P.barrier()
        P.flush()

    if upto == "attn_b":
        return store_out()
    ffn_phase("f2")
    return store_out()


def _stage_inputs(full):
    x = np.ascontiguousarray(full["L0_attn_a_x"])
    B, Sfull, _ = x.shape
    SQ = Sfull // 4
    wqkvT = np.ascontiguousarray(full["L0_attn_a_Wqkv"].T)
    wgateT = np.ascontiguousarray(full["L0_attn_a_Wgate"].T)
    woutT = np.ascontiguousarray(full["L0_attn_a_Wout"].T)
    posb = np.ascontiguousarray(full["L0_attn_a_pos_bias"])
    posb_rep = np.ascontiguousarray(
        np.tile(posb.T.reshape(1, -1), (128, 1)).astype(np.float32))
    w1T = np.ascontiguousarray(full["L0_ffn_w1"].T)
    w2T = np.ascontiguousarray(full["L0_ffn_w2"].T)
    wq_b = np.ascontiguousarray(full["L0_attn_b_wq"])
    wk_b = np.ascontiguousarray(full["L0_attn_b_wk"])
    wv_b = np.ascontiguousarray(full["L0_attn_b_wv"])
    wg_b = np.ascontiguousarray(full["L0_attn_b_wg"])
    wo_b = np.ascontiguousarray(full["L0_attn_b_wo"])
    w_in = full["L0_attn_b_w_in"]
    wu, wg2 = w_in[:, :HID], w_in[:, HID:]
    w_inT = np.zeros((D, 2 * HIDP), np.float32)
    b_in = full["L0_attn_b_b_in"]
    b_in_r = np.zeros(2 * HIDP, np.float32)
    for j in range(NF2):
        lo, hi = j * 128, min((j + 1) * 128, HID)
        w_inT[:, 2 * j * 128:2 * j * 128 + (hi - lo)] = wu[:, lo:hi]
        w_inT[:, (2 * j + 1) * 128:(2 * j + 1) * 128 + (hi - lo)] = wg2[:, lo:hi]
        b_in_r[2 * j * 128:2 * j * 128 + (hi - lo)] = b_in[lo:hi]
        b_in_r[(2 * j + 1) * 128:(2 * j + 1) * 128 + (hi - lo)] = b_in[HID + lo:HID + hi]
    w_out = full["L0_attn_b_w_out"]
    w_outT = np.zeros((HIDP, D), np.float32)
    w_outT[:HID] = w_out
    n2wb = np.stack([full["L0_attn_b_norm2_w"], full["L0_attn_b_norm2_b"]])
    n1wb = np.stack([full["L0_attn_b_norm1_w"], full["L0_attn_b_norm1_b"]])
    gnw = full["L0_attn_b_gn_w"]
    gnwm = np.zeros((1, 2 * D), np.float32)
    for p in range(NPAIR):
        gnwm[0, p * 128:p * 128 + 64] = gnw[p * 128:p * 128 + 64]
        gnwm[0, D + p * 128 + 64:D + (p + 1) * 128] = gnw[p * 128 + 64:
                                                          (p + 1) * 128]
    gnbrow = np.ascontiguousarray(full["L0_attn_b_gn_b"].reshape(1, D))

    def cols(v):
        return np.ascontiguousarray(v.reshape(-1, 128).T)

    atlasv = np.zeros((128, NB_COLS), np.float32)
    atlasv[:, ATL["bqkv"]:ATL["bqkv"] + 24] = np.concatenate(
        [cols(full["L0_attn_a_bqkv"][i * D:(i + 1) * D]) for i in range(3)],
        axis=1)
    atlasv[:, ATL["bout"]:ATL["bout"] + 8] = cols(full["L0_attn_a_bout"])
    atlasv[:, ATL["bgate"]:ATL["bgate"] + 8] = cols(full["L0_attn_a_bgate"])
    atlasv[:, ATL["bq"]:ATL["bq"] + 8] = cols(full["L0_attn_b_bq"])
    atlasv[:, ATL["bk"]:ATL["bk"] + 8] = cols(full["L0_attn_b_bk"])
    atlasv[:, ATL["bv"]:ATL["bv"] + 8] = cols(full["L0_attn_b_bv"])
    atlasv[:, ATL["bg"]:ATL["bg"] + 8] = cols(full["L0_attn_b_bg"])
    atlasv[:, ATL["bo"]:ATL["bo"] + 8] = cols(full["L0_attn_b_bo"])
    atlasv[:, ATL["gnw"]:ATL["gnw"] + 8] = cols(full["L0_attn_b_gn_w"])
    atlasv[:, ATL["gnb"]:ATL["gnb"] + 8] = cols(full["L0_attn_b_gn_b"])
    atlasv[:, ATL["n1w"]:ATL["n1w"] + 8] = cols(full["L0_attn_b_norm1_w"])
    atlasv[:, ATL["n1b"]:ATL["n1b"] + 8] = cols(full["L0_attn_b_norm1_b"])
    atlasv[:, ATL["n2w"]:ATL["n2w"] + 8] = cols(full["L0_attn_b_norm2_w"])
    atlasv[:, ATL["n2b"]:ATL["n2b"] + 8] = cols(full["L0_attn_b_norm2_b"])
    atlasv[:, ATL["b_in"]:ATL["b_in"] + 44] = cols(b_in_r)
    atlasv[:, ATL["b_out"]:ATL["b_out"] + 8] = cols(full["L0_attn_b_b_out"])
    atlasv[:, ATL["b1"]:ATL["b1"] + 32] = cols(full["L0_ffn_b1"])
    atlasv[:, ATL["b2"]:ATL["b2"] + 8] = cols(full["L0_ffn_b2"])

    in_maps = []
    for core in range(8):
        b, qt = divmod(core, 4)
        sl = x[b, qt * SQ:(qt + 1) * SQ]
        x_o = np.ascontiguousarray(sl.T)
        if qt == 0:
            x_h = np.zeros((D, HALO), np.float32)
        else:
            x_h = np.ascontiguousarray(
                x[b, qt * SQ - HALO:qt * SQ].T)
        ind = np.zeros((NOFF, SQ), np.float32)
        if qt == 0:
            n = np.arange(SQ)
            for oi, off in enumerate(OFFSETS):
                ind[oi] = (off > n).astype(np.float32)
        coef = np.zeros((1, 128), np.float32)
        for i in range(4):
            if i < qt:
                for h in range(NH):
                    coef[0, i * NH + h] = GAMMAS[h] ** (SQ * (qt - 1 - i))
        in_maps.append({
            "x_own": x_o, "x_halo": x_h, "wqkvT": wqkvT, "wgateT": wgateT,
            "woutT": woutT, "posb": posb, "posb_rep": posb_rep,
            "ind_bnd": ind,
            "bias_atlas": atlasv, "w1T": w1T, "w2T": w2T,
            "wq_b": wq_b, "wk_b": wk_b, "wv_b": wv_b, "wg_b": wg_b,
            "wo_b": wo_b, "w_inT": w_inT, "w_outT": w_outT,
            "coef_row": coef, "n2wb": np.ascontiguousarray(n2wb),
            "n1wb": np.ascontiguousarray(n1wb), "gnwm": gnwm,
            "gnbrow": gnbrow,
        })
    return in_maps


def kernel(**inputs):
    in_maps = _stage_inputs(inputs)
    nc = bass.Bass("TRN2", target_bir_lowering=False)
    build_program(nc, S=2048, upto="ffn2")
    res = run_bass_kernel_spmd(nc, in_maps, core_ids=list(range(8)))
    x = inputs["L0_attn_a_x"]
    B, Sfull, _ = x.shape
    SQ = Sfull // 4
    out = np.empty((B, Sfull, D), np.float32)
    for core in range(8):
        b, qt = divmod(core, 4)
        out[b, qt * SQ:(qt + 1) * SQ] = res.results[core]["y_out"].T
    return out


# version 13
# speedup vs baseline: 1.0380x; 1.0083x over previous
import numpy as np
import ml_dtypes
from contextlib import ExitStack

import concourse.bass as bass
import concourse.mybir as mybir
from concourse.bass_utils import run_bass_kernel_spmd
from concourse.bass_isa import ReduceOp
from concourse.library_config import attn as ATTN_LIB

F32 = mybir.dt.float32
BF16 = mybir.dt.bfloat16
FP16 = mybir.dt.float16
I32 = mybir.dt.int32
AF = mybir.ActivationFunctionType
ALU = mybir.AluOpType
AX = mybir.AxisListType
BF = ml_dtypes.bfloat16

D = 1024
ND = 8
NH = 16
HD = 64
NPAIR = 8
OFFSETS = sorted(set(range(0, 33)) | {48, 64, 96, 128, 192, 256, 384, 512,
                                      768, 1024, 1536})
NOFF = len(OFFSETS)
HALO = 1536
RELCH = [-12, -8, -6, -4, -3, -2, -1, 0]
NCH = len(RELCH)
EPS = 1e-5
QSC = HD ** (-0.5)
GAMMAS = [1.0 - 2.0 ** (-5.0 - h) for h in range(NH)]
LNG = [float(np.log(g)) for g in GAMMAS]
FF = 4096
NFC = 32
HID = 2730
NF2 = 22
HIDP = NF2 * 128
ELEN = 1792
ELEN2 = 2048
EBASE = 255
REL256 = [-12, -11, -8, -7, -6, -5, -4, -3, -2, -1, 0, 1]
NC256 = len(REL256)

_ATL_SPEC = [("bqkv", 24), ("bout", 8), ("bgate", 8),
             ("bq", 8), ("bk", 8), ("bv", 8), ("bg", 8), ("bo", 8),
             ("gnw", 8), ("gnb", 8), ("n1w", 8), ("n1b", 8),
             ("n2w", 8), ("n2b", 8), ("b_in", 44), ("b_out", 8),
             ("b1", 32), ("b2", 8)]
ATL = {}
_c = 0
for _n, _w in _ATL_SPEC:
    ATL[_n] = _c
    _c += _w
NB_COLS = _c


class Prog:
    ENGINES = ("sync", "scalar", "vector", "tensor", "gpsimd")
    CSEM = {"sync": "qS", "scalar": "s", "vector": "v", "tensor": "t",
            "gpsimd": "g"}

    def __init__(self, nc, es):
        self.nc = nc
        self.es = es
        self.ops = {e: [] for e in self.ENGINES}
        self.semh = {}
        self.count = {}
        self.wmark = {e: {} for e in self.ENGINES}

    def sem(self, name):
        if name not in self.semh:
            self.semh[name] = self.es.enter_context(self.nc.semaphore(name))
            self.count[name] = 0
        return self.semh[name]

    def wait(self, eng, ev):
        if ev is None:
            return
        s, v = ev
        if v <= 0 or self.wmark[eng].get(s, 0) >= v:
            return
        self.wmark[eng][s] = v
        h = self.sem(s)
        self.ops[eng].append(lambda e, h=h, v=v: e.wait_ge(h, v))

    def op(self, eng, fn, waits=(), sem=None, amt=1):
        for ev in waits:
            self.wait(eng, ev)
        if sem is None:
            sem = self.CSEM[eng]
        h = self.sem(sem)
        self.count[sem] += amt
        v = self.count[sem]

        def run(e, fn=fn, h=h, amt=amt):
            fn(e).then_inc(h, amt)

        self.ops[eng].append(run)
        return (sem, v)

    def full(self, sem):
        return (sem, self.count.get(sem, 0))

    def barrier(self):
        evs = [self.full(s) for s in list(self.count)]
        for e in self.ENGINES:
            for ev in evs:
                self.wait(e, ev)

    def flush(self):
        if not any(self.ops[e] for e in self.ENGINES):
            return
        with self.nc.Block() as block:
            for eng in self.ENGINES:
                ops = self.ops[eng]
                if not ops:
                    continue

                def body(e, ops=ops):
                    for f in ops:
                        f(e)

                getattr(block, eng)(body)
                self.ops[eng] = []

    def finalize(self):
        self.flush()


class DmaStream:
    def __init__(self, prog, eng, name, gens=6):
        self.p = prog
        self.eng = eng
        self.names = [f"{name}{i}" for i in range(gens)]
        self.i = 0

    def issue(self, out, in_, waits=(), **kw):
        p, g = self.p, self.names[self.i % len(self.names)]
        self.i += 1
        prev = p.count.get(g, 0)
        if prev:
            p.wait(self.eng, (g, prev))
        return p.op(self.eng, lambda e: e.dma_start(out=out, in_=in_, **kw),
                    waits, sem=g, amt=16)


def build_program(nc, S=2048, upto="ffn2", n_cores=8, npair=NPAIR):
    TT = HALO + S
    NT = S // 128
    NTT = TT // 128
    NTH = HALO // 128
    NQG = S // 512 if S >= 512 else 1
    QG = S // NQG
    NKG = TT // 512

    def din(name, shape, dt=F32):
        return nc.dram_tensor(name, list(shape), dt, kind="ExternalInput").ap()

    x_own = din("x_own", [D, S])
    x_halo = din("x_halo", [D, HALO])
    wqkvT = din("wqkvT", [D, 3 * D])
    wgateT = din("wgateT", [D, D])
    woutT = din("woutT", [D, D])
    posb = din("posb", [NOFF, NH])
    posb_rep = din("posb_rep", [128, NH * NOFF])
    ind_bnd = din("ind_bnd", [NOFF, S])
    bias_atlas = din("bias_atlas", [128, NB_COLS])
    w1T = din("w1T", [D, FF])
    w2T = din("w2T", [FF, D])
    wq_b = din("wq_b", [D, D])
    wk_b = din("wk_b", [D, D])
    wv_b = din("wv_b", [D, D])
    wg_b = din("wg_b", [D, D])
    wo_b = din("wo_b", [D, D])
    w_inT = din("w_inT", [D, 2 * HIDP])
    w_outT = din("w_outT", [HIDP, D])
    coef_row = din("coef_row", [1, 128])
    n2wb = din("n2wb", [2, D])
    n1wb = din("n1wb", [2, D])
    gnwm = din("gnwm", [1, 2 * D])
    gnbrow = din("gnbrow", [1, D])
    y_out = nc.dram_tensor("y_out", [D, S], F32, kind="ExternalOutput").ap()

    e2_dram = nc.dram_tensor("e2_dram", [NH, 128 * (ELEN2 + 1)], BF16,
                             kind="Internal").ap()
    fg_dram = nc.dram_tensor("fg_dram", [ND * 128, S], BF16,
                             kind="Internal").ap()
    g_dram = nc.dram_tensor("g_dram", [NFC * 128, S], FP16, kind="Internal").ap()
    glu_dram = nc.dram_tensor("glu_dram", [NF2 * 128, S], BF16,
                              kind="Internal").ap()
    t1_dram = nc.dram_tensor("t1_dram", [D, FF], BF16, kind="Internal").ap()
    t2_dram = nc.dram_tensor("t2_dram", [FF, D], BF16, kind="Internal").ap()
    cc_in = nc.dram_tensor("cc_in", [NH * HD, HD], BF16, kind="Internal").ap()
    cc_out = nc.dram_tensor("cc_out", [n_cores * NH * HD, HD], BF16,
                            kind="Internal").ap()

    es = ExitStack()
    P = Prog(nc, es)

    def sb(name, shape, dt=F32):
        return es.enter_context(nc.sbuf_tensor(name, list(shape), dt))

    dS = DmaStream(P, "sync", "dS", 8)
    dA = DmaStream(P, "scalar", "dA", 4)
    dG = DmaStream(P, "gpsimd", "dG", 8)

    X = sb("X", [128, ND * S])
    atlas = sb("atlas", [128, NB_COLS])
    ones_col_bf = sb("ones_col_bf", [128, 1], BF16)
    ones_col32 = sb("ones_col32", [128, 1])
    ones_row = sb("ones_row", [1, 128])
    neg1_bf = sb("neg1_bf", [1, 1], BF16)
    identbf = sb("identbf", [128, 128], BF16)
    ident32 = sb("ident32", [128, 128])
    iojp = sb("iojp", [128, 128])
    maskge = sb("maskge", [128, 128])
    iorow = sb("iorow", [1, 128])
    iocol = sb("iocol", [128, 1])
    indrow32 = sb("indrow32", [1, 256])
    iocol127 = sb("iocol127", [128, 1])
    iorow1 = sb("iorow1", [1, 128])

    ev_atlas = dS.issue(atlas[:], bias_atlas)
    ev_io = P.op("gpsimd", lambda e: e.iota(
        iojp[:], [[1, 128]], base=0, channel_multiplier=-1,
        allow_small_or_imprecise_dtypes=True))
    ev_iorow = P.op("gpsimd", lambda e: e.iota(
        iorow[:], [[1, 128]], base=0, channel_multiplier=0,
        allow_small_or_imprecise_dtypes=True))
    ev_iocol = P.op("gpsimd", lambda e: e.iota(
        iocol[:], [[0, 1]], base=0, channel_multiplier=1,
        allow_small_or_imprecise_dtypes=True))
    ev_id = P.op("vector", lambda e: e.tensor_scalar(
        ident32[:], iojp[:], 0.0, None, ALU.is_equal), [ev_io])
    ev_idb = P.op("vector", lambda e: e.tensor_copy(identbf[:], ident32[:]),
                  [ev_id])
    ev_mge = P.op("vector", lambda e: e.tensor_scalar(
        maskge[:], iojp[:], 0.0, None, ALU.is_ge), [ev_io])
    P.op("vector", lambda e: e.memset(ones_col_bf[:], 1.0))
    P.op("vector", lambda e: e.memset(ones_col32[:], 1.0))
    P.op("vector", lambda e: e.memset(ones_row[:], 1.0))
    P.op("vector", lambda e: e.memset(neg1_bf[:], -1.0))
    P.op("vector", lambda e: e.memset(indrow32[:, 0:64], 1.0))
    P.op("vector", lambda e: e.memset(indrow32[:, 64:128], 0.0))
    P.op("vector", lambda e: e.memset(indrow32[:, 128:192], 0.0))
    P.op("vector", lambda e: e.memset(indrow32[:, 192:256], 1.0))
    P.op("vector", lambda e: e.tensor_scalar(
        iocol127[:], iocol[:], -127.0, None, ALU.add), [ev_iocol])
    P.op("vector", lambda e: e.tensor_scalar(
        iorow1[:], iorow[:], 1.0, None, ALU.add), [ev_iorow])

    X3 = X[:].rearrange("p (c s) -> p c s", c=ND)
    ev_X = dS.issue(X3, x_own.rearrange("(c p) s -> p c s", p=128))

    def atl(name, j):
        return atlas[:, ATL[name] + j:ATL[name] + j + 1]

    def store_out():
        ev_y = dS.issue(y_out.rearrange("(c p) s -> p c s", p=128), X3,
                        waits=[P.full("v")])
        P.wait("sync", ev_y)
        P.finalize()
        return P

    P.barrier()
    if upto == "init":
        return store_out()

    with ExitStack() as esA:
        def sbA(name, shape, dt=F32):
            return esA.enter_context(nc.sbuf_tensor(name, list(shape), dt))

        u_bf = sbA("u_bf", [128, ND * TT], BF16)
        U3 = u_bf[:].rearrange("p (c t) -> p c t", c=ND)

        with ExitStack() as esL:
            def sbL(name, shape, dt=F32):
                return esL.enter_context(nc.sbuf_tensor(name, list(shape), dt))

            def psL(name, shape, dt=F32):
                return esL.enter_context(nc.psum_tensor(name, list(shape), dt))

            x2f = [sbL(f"a_x2f{i}", [128, D]) for i in range(2)]
            xh32 = [sbL(f"a_xh{i}", [128, D]) for i in range(2)]
            t32 = [sbL(f"a_t32{i}", [128, D]) for i in range(2)]
            rows = [sbL(f"a_rows{i}", [1, 768]) for i in range(2)]
            psS = [psL(f"a_psS{i}", [1, 256]) for i in range(2)]
            psAB = [psL(f"a_psAB{i}", [128, 128]) for i in range(2)]
            psBB = [psL(f"a_psBB{i}", [128, 128]) for i in range(2)]
            prev_u = [None, None]
            xhalo3 = x_halo.rearrange("(c p) t -> p c t", p=128)
            for tt in range(NTT):
                pi = tt % 2
                w0 = [prev_u[pi]] if prev_u[pi] else []
                if tt < NTH:
                    ev_ld = dS.issue(
                        xh32[pi][:].rearrange("p (c n) -> p c n", c=ND),
                        xhalo3[:, :, tt * 128:(tt + 1) * 128], waits=w0)
                    x_ap = xh32[pi][:].rearrange("p (c n) -> p c n", c=ND)
                    w0 = w0 + [ev_ld]
                else:
                    t0 = (tt - NTH) * 128
                    x_ap = X3[:, :, t0:t0 + 128]
                    w0 = w0 + [ev_X]
                evs = P.op("scalar", lambda e, x_ap=x_ap, pi=pi:
                           e.activation(
                    x2f[pi][:].rearrange("p (c n) -> p c n", c=ND), x_ap,
                    AF.Square, bias=0.0, scale=1.0), w0 + [P.full("v")])
                ev_m1 = ev_m2 = None
                for c in range(ND):
                    ev_m1 = P.op("tensor", lambda e, c=c, x_ap=x_ap, pi=pi:
                                 e.matmul(
                        psS[pi][0:1, 0:128], ones_col32[:],
                        x_ap[:, c], start=(c == 0), stop=(c == ND - 1)),
                        w0 + [P.full("v")])
                for c in range(ND):
                    ev_m2 = P.op("tensor", lambda e, c=c, pi=pi: e.matmul(
                        psS[pi][0:1, 128:256], ones_col32[:],
                        x2f[pi][:, c * 128:(c + 1) * 128],
                        start=(c == 0), stop=(c == ND - 1)), [evs])
                r = rows[pi]
                ev = P.op("vector", lambda e, r=r, pi=pi: e.tensor_scalar(
                    r[:, 0:128], psS[pi][0:1, 0:128], 1.0 / D, None, ALU.mult),
                    [ev_m1, ev_m2])
                ev = P.op("vector", lambda e, r=r: e.tensor_tensor(
                    r[:, 128:256], r[:, 0:128], r[:, 0:128], ALU.mult), [ev])
                ev = P.op("vector", lambda e, r=r, pi=pi:
                          e.scalar_tensor_tensor(
                    r[:, 256:384], psS[pi][0:1, 128:256], 1.0 / D,
                    r[:, 128:256], ALU.mult, ALU.subtract), [ev])
                ev = P.op("vector", lambda e, r=r: e.tensor_scalar(
                    r[:, 640:768], r[:, 256:384], EPS, None, ALU.add), [ev])
                ev = P.op("vector", lambda e, r=r: e.reciprocal(
                    r[:, 256:384], r[:, 640:768]), [ev])
                evA = P.op("scalar", lambda e, r=r: e.activation(
                    r[:, 384:512], r[:, 256:384], AF.Sqrt, bias=0.0,
                    scale=1.0), [ev])
                evN = P.op("vector", lambda e, r=r: e.scalar_tensor_tensor(
                    r[:, 512:640], r[:, 0:128], -1.0, r[:, 384:512],
                    ALU.mult, ALU.mult), [evA])
                evbA = P.op("tensor", lambda e, r=r, pi=pi: e.matmul(
                    psAB[pi][:], ones_row[:], r[:, 384:512],
                    start=True, stop=True), [evA, P.full("v")])
                evbB = P.op("tensor", lambda e, r=r, pi=pi: e.matmul(
                    psBB[pi][:], ones_row[:], r[:, 512:640],
                    start=True, stop=True), [evN])
                abc = psAB[pi][:].unsqueeze(1).broadcast_to([128, ND, 128])
                bbc = psBB[pi][:].unsqueeze(1).broadcast_to([128, ND, 128])
                ev_t = P.op("vector", lambda e, x_ap=x_ap, pi=pi, abc=abc:
                            e.tensor_tensor(
                    t32[pi][:].rearrange("p (c n) -> p c n", c=ND), x_ap, abc,
                    ALU.mult), [evbA] + w0)
                ev_u = P.op("vector", lambda e, tt=tt, pi=pi, bbc=bbc:
                            e.tensor_tensor(
                    U3[:, :, tt * 128:(tt + 1) * 128],
                    t32[pi][:].rearrange("p (c n) -> p c n", c=ND), bbc,
                    ALU.add), [evbB, ev_t])
                prev_u[pi] = ev_u
            P.barrier()
            P.flush()
        if upto == "ln0":
            return store_out()

        pb = sbA("pb", [NOFF, NH])
        expE44 = sbA("expE44", [NOFF, NH], BF16)
        indb = sbA("indb", [NOFF, S], BF16)
        with ExitStack() as esE:
            pbrep = esE.enter_context(nc.sbuf_tensor(
                "pbrep", [128, NH * NOFF], F32))
            erep = esE.enter_context(nc.sbuf_tensor(
                "erep", [128, ELEN2], F32))
            ev_pb = dS.issue(pb[:], posb)
            ev_pbr = dS.issue(pbrep[:], posb_rep)
            ev_e44 = P.op("scalar", lambda e: e.activation(
                expE44[:], pb[:], AF.Exp, bias=0.0, scale=1.0), [ev_pb])
            ev_e128 = P.op("scalar", lambda e: e.activation(
                pbrep[:], pbrep[:], AF.Exp, bias=0.0, scale=1.0), [ev_pbr])
            ev_ed = None
            for h in range(NH):
                evz = P.op("vector", lambda e: e.memset(erep[:], 0.0),
                           [ev_ed] if ev_ed else [])
                evs_ = [P.op("vector", lambda e, h=h: e.tensor_copy(
                    erep[:, EBASE:EBASE + 33],
                    pbrep[:, h * NOFF:h * NOFF + 33]), [evz, ev_e128])]
                for oi in range(33, NOFF):
                    off = OFFSETS[oi]
                    evs_.append(P.op("vector", lambda e, h=h, oi=oi, off=off:
                                     e.tensor_copy(
                        erep[:, EBASE + off:EBASE + 1 + off],
                        pbrep[:, h * NOFF + oi:h * NOFF + oi + 1]),
                        [evz, ev_e128]))
                ev_ed = dG.issue(
                    bass.AP(e2_dram.tensor, h * 128 * (ELEN2 + 1),
                            [[ELEN2 + 1, 128], [1, ELEN2]]),
                    erep[:, 0:ELEN2], waits=evs_)
            ev_indb = dG.issue(indb[:], ind_bnd)
            P.barrier()
            P.flush()
        if upto == "etab":
            return store_out()

        with ExitStack() as esP:
            def sbP(name, shape, dt=F32):
                return esP.enter_context(nc.sbuf_tensor(name, list(shape), dt))

            def psP(name, shape, dt=F32):
                return esP.enter_context(nc.psum_tensor(name, list(shape), dt))

            wsl = sbP("wsl", [128, 4 * ND * 128], BF16)
            q_bf = sbP("q_bf", [128, S], BF16)
            k_bf = sbP("k_bf", [128, TT], BF16)
            v_bf = sbP("v_bf", [128, TT], BF16)
            v_tm = sbP("v_tm", [128, NTT * 128], BF16)
            gate32 = sbP("gate32", [128, S], BF16)
            expM = sbP("expM", [128, 2 * NC256 * 256], BF16)
            ab = sbP("ab", [128, 2 * NC256 * 256], BF16)
            ftmp = sbP("ftmp", [1, 512])
            sufcb = sbP("sufcb", [1, 2 * S], BF16)
            negv = sbP("negv", [1, 128], BF16)
            zrec = sbP("zrec", [1, 512])
            zbb = sbP("zbb", [128, 256], BF16)
            t1b = sbP("t1b", [128, 256], BF16)
            fgp = sbP("fgp", [128, S], BF16)
            pp = psP("pp", [128, 512])
            psT = [psP(f"pst{i}", [128, 512]) for i in range(2)]

            wqkvT3 = wqkvT.rearrange("(c p) n -> p c n", p=128)
            wgateT3 = wgateT.rearrange("(c p) n -> p c n", p=128)
            wsl4 = wsl[:].rearrange("p (w c n) -> p w c n", w=4, c=ND)
            fgd3 = fg_dram.rearrange("(c p) s -> p c s", p=128)

            eve_prev = None
            ev_exp_prev = [None, None]
            ev_fg_prev = None
            ev_zr_prev = None
            ev_t1_prev = None
            last_gate = None
            for p in range(npair):
                tguard = [P.full("t")]
                ev_w = []
                for wi, (src3, c0) in enumerate(
                        [(wqkvT3, 128 * p), (wqkvT3, D + 128 * p),
                         (wqkvT3, 2 * D + 128 * p), (wgateT3, 128 * p)]):
                    ev_w.append(dG.issue(wsl4[:, wi],
                                         src3[:, :, c0:c0 + 128],
                                         waits=tguard))
                if upto == "wsl":
                    P.barrier()
                    P.flush()
                    break

                def proj(dst_ap, wi, tok0, ntok, bias_ap, scale, ev_extra,
                         act=None):
                    evs_out = []
                    ng = ntok // 512 if ntok >= 512 else 1
                    gs = ntok // ng
                    evo = None
                    for g in range(ng):
                        evm = None
                        for c in range(ND):
                            evm = P.op("tensor",
                                       lambda e, c=c, g=g, wi=wi, tok0=tok0,
                                       gs=gs: e.matmul(
                                           pp[:, 0:gs],
                                           wsl4[:, wi, c],
                                           U3[:, c, tok0 + g * gs:
                                              tok0 + (g + 1) * gs],
                                           start=(c == 0), stop=(c == ND - 1)),
                                       [ev_w[wi], P.full("v")] + ev_extra +
                                       ([evo] if evo else []))
                        if act is None:
                            evo = P.op("vector",
                                       lambda e, g=g, gs=gs: e.tensor_scalar(
                                           dst_ap[:, g * gs:(g + 1) * gs],
                                           pp[:, 0:gs], bias_ap, scale,
                                           ALU.add, ALU.mult), [evm])
                        else:
                            evo = P.op("scalar",
                                       lambda e, g=g, gs=gs: e.activation(
                                           dst_ap[:, g * gs:(g + 1) * gs],
                                           pp[:, 0:gs], act, bias=bias_ap,
                                           scale=1.0), [evm])
                        evs_out.append(evo)
                    return evs_out

                ev_q = proj(q_bf[:], 0, HALO, S, atl("bqkv", p), QSC,
                            [last_gate] if last_gate else [])
                if upto == "projq":
                    P.barrier()
                    P.flush()
                    break
                ev_k = proj(k_bf[:], 1, 0, TT, atl("bqkv", 8 + p), 1.0, [])
                ev_v = proj(v_bf[:], 2, 0, TT, atl("bqkv", 16 + p), 1.0, [])
                if upto == "projv":
                    P.barrier()
                    P.flush()
                    break
                ev_g = proj(gate32[:], 3, HALO, S, atl("bgate", p), 1.0, [],
                            act=AF.Sigmoid)
                last_gate = ev_g[-1]
                if upto == "proj":
                    P.barrier()
                    P.flush()
                    break

                esPre = ExitStack()
                ptr = esPre.enter_context(
                    nc.psum_tensor(f"ptr{p}", [128, 128], BF16))
                psuf = esPre.enter_context(
                    nc.psum_tensor(f"psuf{p}", [1, 1024], F32))
                ev_vt = []
                for a in range(NTT):
                    evt = P.op("tensor", lambda e, a=a, ptr=ptr: e.transpose(
                        ptr[:], v_bf[:, a * 128:(a + 1) * 128], identbf[:]),
                        [ev_v[min(a * 128 // (TT // len(ev_v)),
                                  len(ev_v) - 1)], ev_idb, P.full("v")])
                    ev_vt.append(P.op("vector",
                                      lambda e, a=a, ptr=ptr: e.tensor_copy(
                        v_tm[:, a * 128:(a + 1) * 128], ptr[:]), [evt]))
                ev_vt_all = P.full("v")
                if upto == "vtr":
                    P.barrier()
                    P.flush()
                    esPre.close()
                    break

                ev_eM = []
                for hb in (0, 1):
                    h = 2 * p + hb
                    for ci, rel in enumerate(REL256):
                        off0 = h * 128 * (ELEN2 + 1) + EBASE - 128 * rel
                        sl_m = (hb * NC256 + ci) * 256
                        ev_eM.append(dA.issue(
                            expM[:, sl_m:sl_m + 256],
                            bass.AP(e2_dram.tensor, off0,
                                    [[ELEN2, 128], [1, 256]]),
                            waits=[P.full("v")]))
                if upto == "expm":
                    P.barrier()
                    P.flush()
                    esPre.close()
                    break

                ev_sf = []
                for hb in (0, 1):
                    h = 2 * p + hb
                    ho = hb * 64
                    for g in range(NQG):
                        gs = QG
                        evm1 = P.op("tensor", lambda e, h=h, g=g, gs=gs, psuf=psuf:
                                    e.matmul(
                            psuf[0:1, 0:gs], expE44[:, h:h + 1],
                            indb[:, g * gs:(g + 1) * gs],
                            start=True, stop=True),
                            [ev_e44, ev_indb, P.full("v")] +
                            ([eve_prev] if eve_prev else []))
                        evm2 = P.op("tensor", lambda e, ho=ho, g=g, gs=gs, psuf=psuf:
                                    e.matmul(
                            psuf[0:1, 512:512 + gs],
                            k_bf[ho:ho + 64, 0:1],
                            q_bf[ho:ho + 64, g * gs:(g + 1) * gs],
                            start=True, stop=True,
                            tile_position=(ho, 0)),
                            ev_k + ev_q +
                            ([eve_prev] if eve_prev else []))
                        eve = P.op("scalar", lambda e, gs=gs, psuf=psuf:
                                   e.activation(
                            ftmp[0:1, 0:gs], psuf[0:1, 512:512 + gs], AF.Exp,
                            bias=0.0, scale=1.0), [evm2, P.full("v")])
                        eve_prev = eve
                        ev_sf.append(P.op("vector", lambda e, g=g, gs=gs, psuf=psuf,
                                          hb=hb: e.tensor_tensor(
                            sufcb[0:1, hb * S + g * gs:hb * S + (g + 1) * gs],
                            ftmp[0:1, 0:gs], psuf[0:1, 0:gs], ALU.mult),
                            [eve, evm1]))
                ev_negv = P.op("vector", lambda e: e.tensor_scalar(
                    negv[0:1, :], v_tm[0:1, 0:128], -1.0, None, ALU.mult),
                    [ev_vt[0]])
                P.flush()
                esPre.close()
                if upto == "bnd":
                    break
                esTile = ExitStack()
                pav = esTile.enter_context(
                    nc.psum_tensor(f"pav{p}", [128, 256], F32))
                zps = esTile.enter_context(
                    nc.psum_tensor(f"zps{p}", [1, 512], F32))
                pzb = esTile.enter_context(
                    nc.psum_tensor(f"pzb{p}", [128, 256], F32))

                for tb in range(NT // 2):
                    q0 = tb * 256
                    ev_zmm2 = [None, None]
                    ev_av = [None, None]
                    for hb in (0, 1):
                        ho = hb * 64
                        ev_ab_c = [None] * NC256
                        for cp in range(NC256 // 2):
                            bi = cp % 2
                            pb_ = psT[bi]
                            evm = []
                            for k2 in (0, 1):
                                ci = 2 * cp + k2
                                a = NTH + 2 * tb + REL256[ci]
                                evm.append(P.op("tensor",
                                                lambda e, ci=ci, a=a, hb=hb,
                                                ho=ho, q0=q0, k2=k2, pb_=pb_:
                                                e.matmul(
                                    pb_[:, k2 * 256:(k2 + 1) * 256],
                                    k_bf[ho:ho + 64, a * 128:(a + 1) * 128],
                                    q_bf[ho:ho + 64, q0:q0 + 256],
                                    start=True, stop=True,
                                    tile_position=(ho, 0)),
                                    ev_k + ev_q +
                                    ([ev_exp_prev[bi]] if k2 == 0 and
                                     ev_exp_prev[bi] else [])))
                            sl_a = (hb * NC256 + 2 * cp) * 256
                            ev_exp = P.op("scalar", lambda e, sl_a=sl_a,
                                          pb_=pb_: e.activation(
                                ab[:, sl_a:sl_a + 512], pb_[:],
                                AF.Exp, bias=0.0, scale=1.0), evm +
                                ([ev_fg_prev] if ev_fg_prev else []))
                            ev_exp_prev[bi] = ev_exp
                            ev_ab = P.op("vector", lambda e, sl_a=sl_a:
                                         e.tensor_tensor(
                                ab[:, sl_a:sl_a + 512],
                                ab[:, sl_a:sl_a + 512],
                                expM[:, sl_a:sl_a + 512], ALU.mult),
                                [ev_exp] + ev_eM)
                            ev_ab_c[2 * cp] = ev_ab
                            ev_ab_c[2 * cp + 1] = ev_ab
                        for ci in range(NC256):
                            sl_c = (hb * NC256 + ci) * 256
                            P.op("tensor", lambda e, ci=ci, hb=hb, sl_c=sl_c,
                                 zps=zps: e.matmul(
                                zps[0:1, hb * 256:(hb + 1) * 256],
                                ones_col_bf[:], ab[:, sl_c:sl_c + 256],
                                start=(ci == 0), stop=False),
                                [ev_ab_c[ci]] +
                                ([ev_zr_prev] if ci == 0 and ev_zr_prev
                                 else []))
                        ev_zmm = P.op("tensor", lambda e, hb=hb, q0=q0, zps=zps:
                                      e.matmul(
                            zps[0:1, hb * 256:(hb + 1) * 256], neg1_bf[:],
                            sufcb[0:1, hb * S + q0:hb * S + q0 + 256],
                            start=False, stop=True), ev_sf)
                        ev_zmm2[hb] = ev_zmm
                        for ci in range(NC256):
                            a = NTH + 2 * tb + REL256[ci]
                            sl_c = (hb * NC256 + ci) * 256
                            P.op("tensor", lambda e, ci=ci, a=a, ho=ho,
                                 sl_c=sl_c, pav=pav: e.matmul(
                                pav[ho:ho + 64, :],
                                v_tm[:, a * 128 + ho:a * 128 + ho + 64],
                                ab[:, sl_c:sl_c + 256],
                                start=(ci == 0), stop=False,
                                tile_position=(0, ho)),
                                [ev_ab_c[ci], ev_vt_all] +
                                ([ev_t1_prev] if ci == 0 and ev_t1_prev
                                 else []))
                        ev_av[hb] = P.op("tensor", lambda e, ho=ho, hb=hb,
                                         q0=q0, pav=pav: e.matmul(
                            pav[ho:ho + 64, :], negv[0:1, ho:ho + 64],
                            sufcb[0:1, hb * S + q0:hb * S + q0 + 256],
                            start=False, stop=True, tile_position=(0, ho)),
                            [ev_negv] + ev_sf)
                    ev_zr_all = P.op("vector", lambda e, zps=zps:
                                     e.tensor_copy(
                        zrec[0:1, 0:512], zps[0:1, 0:512]),
                        [ev_zmm2[0], ev_zmm2[1]])
                    ev_zr_prev = ev_zr_all
                    ev_zb = P.op("tensor", lambda e, pzb=pzb: e.matmul(
                        pzb[:], indrow32[0:1, 0:128], zrec[0:1, 0:256],
                        start=True, stop=False),
                        [ev_zr_all] +
                        ([ev_fg_prev] if ev_fg_prev else []))
                    ev_zb = P.op("tensor", lambda e, pzb=pzb: e.matmul(
                        pzb[:], indrow32[0:1, 128:256], zrec[0:1, 256:512],
                        start=False, stop=True), [])
                    def _recip_lp(e, pzb=pzb):
                        with nc.allow_low_precision(reason="1/Z bcast bf16"):
                            return e.reciprocal(zbb[:], pzb[:])
                    ev_zbr = P.op("vector", _recip_lp, [ev_zb] +
                                  ([ev_fg_prev] if ev_fg_prev else []))
                    ev_t1 = P.op("vector", lambda e, q0=q0, pav=pav: e.tensor_tensor(
                        t1b[:], pav[:], gate32[:, q0:q0 + 256],
                        ALU.mult), [ev_av[0], ev_av[1]] + ev_g)
                    ev_t1_prev = ev_t1
                    ev_fg_prev = P.op("vector", lambda e, q0=q0:
                                      e.tensor_tensor(
                        fgp[:, q0:q0 + 256], t1b[:], zbb[:],
                        ALU.mult), [ev_t1, ev_zbr])
                    if upto == "tile1":
                        break
                ev_fgs = dS.issue(fgd3[:, p], fgp[:], waits=[P.full("v")])
                P.wait("vector", ev_fgs)
                P.flush()
                esTile.close()
            P.barrier()

            P.barrier()
            P.flush()
        if upto in ("pairs", "wsl", "proj", "projq", "projv", "vtr", "expm", "bnd", "tile1"):
            return store_out()

        with ExitStack() as esW:
            wob = esW.enter_context(nc.sbuf_tensor("wob", [128, ND * D],
                                                   BF16))
            fgall = [esW.enter_context(nc.sbuf_tensor(
                f"fgall{i}", [128, ND * 512], BF16)) for i in range(2)]
            pwoA = [esW.enter_context(nc.psum_tensor(f"pwoA{i}", [128, 512],
                                                     F32)) for i in range(2)]
            fgd3 = fg_dram.rearrange("(c p) s -> p c s", p=128)
            wob4 = wob[:].rearrange("p (c n) -> p c n", c=ND)
            ev_wo = dG.issue(wob4, woutT.rearrange("(c p) n -> p c n", p=128))
            prev_use = [None, None]
            rd_prev = [None, None]
            for g in range(NQG):
                pi = g % 2
                w0 = [prev_use[pi]] if prev_use[pi] else []
                evl = dS.issue(fgall[pi][:].rearrange("p (c n) -> p c n",
                                                      c=ND),
                               fgd3[:, :, g * QG:(g + 1) * QG], waits=w0)
                evm = None
                for dd in range(ND):
                    bi = dd % 2
                    for c in range(ND):
                        evm = P.op("tensor", lambda e, c=c, dd=dd, g=g, pi=pi,
                                   bi=bi: e.matmul(
                            pwoA[bi][:],
                            wob4[:, c, dd * 128:(dd + 1) * 128],
                            fgall[pi][:, c * 512:(c + 1) * 512],
                            start=(c == 0), stop=(c == ND - 1)),
                            [ev_wo, evl] +
                            ([rd_prev[bi]] if c == 0 and rd_prev[bi]
                             else []))
                    rd_prev[bi] = P.op("vector", lambda e, dd=dd, g=g, bi=bi:
                                       e.scalar_tensor_tensor(
                        X3[:, dd, g * QG:(g + 1) * QG], pwoA[bi][:],
                        atl("bout", dd), X3[:, dd, g * QG:(g + 1) * QG],
                        ALU.add, ALU.add), [evm, ev_atlas])
                prev_use[pi] = evm
            P.barrier()
            P.flush()

    if upto == "attn_a":
        return store_out()

    mwsb = sb("mwsb", [128, 2])

    def ternarize_weights():
        with ExitStack() as esT:
            def sbT(name, shape, dt=F32):
                return esT.enter_context(nc.sbuf_tensor(name, list(shape), dt))

            wch = [sbT(f"t_wch{i}", [128, FF]) for i in range(2)]
            gtb = sbT("t_gtb", [128, FF], BF16)
            ltb = sbT("t_ltb", [128, FF], BF16)
            ternb = sbT("t_ternb", [128, FF], BF16)
            asum = sbT("t_asum", [128, 40])
            wsi = sbT("t_wsi", [128, 1])
            pts1 = esT.enter_context(nc.psum_tensor("t_pts1", [1, 128], F32))
            pts2 = esT.enter_context(nc.psum_tensor("t_pts2", [128, 128],
                                                    F32))
            for wi, (wsrc, tdst, C, NR) in enumerate(
                    [(w1T, t1_dram, FF, ND), (w2T, t2_dram, D, NFC)]):
                src3 = wsrc.rearrange("(r p) c -> p r c", p=128)
                dst3 = tdst.rearrange("(r p) c -> p r c", p=128)
                prev = [None, None]
                for r in range(NR):
                    pi = r % 2
                    w0 = [prev[pi]] if prev[pi] else []
                    evl = dS.issue(wch[pi][:, 0:C], src3[:, r], waits=w0)
                    prev[pi] = P.op("vector", lambda e, pi=pi, r=r, C=C:
                                    e.tensor_reduce(
                        asum[:, r:r + 1], wch[pi][:, 0:C], AX.X, ALU.add,
                        apply_absolute_value=True), [evl])
                ev = P.op("vector", lambda e, NR=NR: e.tensor_reduce(
                    asum[:, 32:33], asum[:, 0:NR], AX.X, ALU.add),
                    [prev[0], prev[1]])
                ev = P.op("tensor", lambda e: e.matmul(
                    pts1[0:1, 0:1], ones_col32[:], asum[:, 32:33],
                    start=True, stop=True), [ev, P.full("v")])
                ev = P.op("vector", lambda e: e.tensor_copy(
                    asum[0:1, 33:34], pts1[0:1, 0:1]), [ev])
                ev = P.op("tensor", lambda e: e.matmul(
                    pts2[:, 0:1], ones_row[:], asum[0:1, 33:34],
                    start=True, stop=True), [ev])
                ev = P.op("vector", lambda e, C=C, NR=NR, wi=wi:
                          e.tensor_scalar(
                    mwsb[:, wi:wi + 1], pts2[:, 0:1], 1.0 / (128 * NR * C),
                    1e-5, ALU.mult, ALU.max), [ev])
                ev_wsi = P.op("vector", lambda e, wi=wi: e.reciprocal(
                    wsi[:], mwsb[:, wi:wi + 1]), [ev])
                prev_st = [None, None]
                ev_tst = None
                for r in range(NR):
                    pi = r % 2
                    w0 = [prev_st[pi]] if prev_st[pi] else []
                    evl = dS.issue(wch[pi][:, 0:C], src3[:, r],
                                   waits=w0 + [ev_wsi])
                    eg = P.op("vector", lambda e, pi=pi, C=C: e.tensor_scalar(
                        gtb[:, 0:C], wch[pi][:, 0:C], wsi[:], 0.5,
                        ALU.mult, ALU.is_gt), [evl])
                    el = P.op("vector", lambda e, pi=pi, C=C: e.tensor_scalar(
                        ltb[:, 0:C], wch[pi][:, 0:C], wsi[:], -0.5,
                        ALU.mult, ALU.is_lt), [evl])
                    prev_st[pi] = el
                    et = P.op("vector", lambda e, C=C: e.tensor_tensor(
                        ternb[:, 0:C], gtb[:, 0:C], ltb[:, 0:C],
                        ALU.subtract), [eg, el] +
                        ([ev_tst] if ev_tst else []))
                    ev_tst = dS.issue(dst3[:, r], ternb[:, 0:C], waits=[et])
            P.barrier()
            P.flush()

    def ffn_phase(tag):
        with ExitStack() as esF:
            def sbF(name, shape, dt=F32):
                return esF.enter_context(nc.sbuf_tensor(name, list(shape), dt))

            sgrow = sbF(f"{tag}_sg", [1, S])
            am2row = sbF(f"{tag}_am2", [1, S])
            rowB = sbF(f"{tag}_rowB", [1, S])
            gd3 = g_dram.rearrange("(f p) s -> p f s", p=128)

            with ExitStack() as esA1:
                def sbA1(name, shape, dt=F32):
                    return esA1.enter_context(nc.sbuf_tensor(name,
                                                             list(shape), dt))
                qg = sbA1(f"{tag}_qg", [128, ND * 512], BF16)
                Qg3 = qg[:].rearrange("p (c n) -> p c n", c=ND)
                c1bg = sbA1(f"{tag}_c1bg", [128, 512])
                rg = sbA1(f"{tag}_rg", [1, 3 * 512])
                rsc = sbA1(f"{tag}_rsc", [1, 768])
                x2f = [sbA1(f"{tag}_x2f{i}", [128, D]) for i in range(2)]
                racc = sbA1(f"{tag}_racc", [128, 128])
                rtmp = sbA1(f"{tag}_rtmp", [128, 128])
                h32 = sbA1(f"{tag}_h32", [128, 512])
                gbf = sbA1(f"{tag}_gbf", [128, 512], FP16)
                g2b = sbA1(f"{tag}_g2b", [128, 512], FP16)
                acc2 = sbA1(f"{tag}_acc2", [128, 512])
                acc2n = sbA1(f"{tag}_acc2n", [128, 512])
                tg = [sbA1(f"{tag}_tg{i}", [128, 512]) for i in range(2)]
                qi32 = [sbA1(f"{tag}_qi{i}", [128, 512], I32)
                        for i in range(2)]
                t1b = esA1.enter_context(
                    nc.sbuf_tensor(f"{tag}_t1b", [128, ND * FF], BF16))
                t13 = t1b[:].rearrange("p (c f) -> p c f", c=ND)
                ev_t1 = dS.issue(t13,
                                 t1_dram.rearrange("(c p) f -> p c f", p=128))
                psS = esA1.enter_context(nc.psum_tensor(f"{tag}_psS",
                                                        [1, 256], F32))
                pmB = esA1.enter_context(nc.psum_tensor(f"{tag}_pmB",
                                                        [128, 128], F32))
                pBB = esA1.enter_context(nc.psum_tensor(f"{tag}_pBB",
                                                        [128, 512], F32))
                psHr = [esA1.enter_context(nc.psum_tensor(
                    f"{tag}_psH{i}", [128, 512], F32)) for i in range(2)]
                pG = esA1.enter_context(nc.psum_tensor(f"{tag}_pG",
                                                       [1, 512], F32))
                pTT = esA1.enter_context(nc.psum_tensor(f"{tag}_pTT",
                                                        [128, 128], F32))
                pTr = esA1.enter_context(nc.psum_tensor(f"{tag}_pTr",
                                                        [1, 128], F32))
                ev_gst = None
                ev_pgrd = None
                ev_epi_bank = [None, None]
                ev_epi = None
                ev_gg = None
                ev_sq = None
                ev_sq2 = None
                ev_accT = None
                for g in range(NQG):
                    for it in range(4):
                        tt = g * 4 + it
                        pi = it % 2
                        x_ap = X3[:, :, tt * 128:(tt + 1) * 128]
                        evs = P.op("scalar", lambda e, x_ap=x_ap, pi=pi:
                                   e.activation(
                            x2f[pi][:].rearrange("p (c n) -> p c n", c=ND),
                            x_ap, AF.Square, bias=0.0, scale=1.0),
                            [P.full("v"), P.full("t")])
                        ev_m1 = ev_m2 = None
                        for c in range(ND):
                            ev_m1 = P.op("tensor", lambda e, c=c, x_ap=x_ap:
                                         e.matmul(
                                psS[0:1, 0:128], ones_col32[:],
                                x_ap[:, c], start=(c == 0),
                                stop=(c == ND - 1)), [P.full("v")])
                        for c in range(ND):
                            ev_m2 = P.op("tensor", lambda e, c=c, pi=pi:
                                         e.matmul(
                                psS[0:1, 128:256], ones_col32[:],
                                x2f[pi][:, c * 128:(c + 1) * 128],
                                start=(c == 0), stop=(c == ND - 1)), [evs])
                        ev = P.op("vector", lambda e: e.tensor_scalar(
                            rsc[:, 0:128], psS[0:1, 0:128], 1.0 / D, None,
                            ALU.mult), [ev_m1, ev_m2])
                        ev = P.op("vector", lambda e: e.tensor_tensor(
                            rsc[:, 128:256], rsc[:, 0:128], rsc[:, 0:128],
                            ALU.mult), [ev])
                        ev = P.op("vector", lambda e: e.scalar_tensor_tensor(
                            rsc[:, 256:384], psS[0:1, 128:256], 1.0 / D,
                            rsc[:, 128:256], ALU.mult, ALU.subtract), [ev])
                        ev = P.op("vector", lambda e: e.tensor_scalar(
                            rsc[:, 384:512], rsc[:, 256:384], EPS, None,
                            ALU.add), [ev])
                        ev = P.op("vector", lambda e: e.reciprocal(
                            rsc[:, 384:512], rsc[:, 384:512]), [ev])
                        evA1 = P.op("scalar", lambda e: e.activation(
                            rsc[:, 512:640], rsc[:, 384:512], AF.Sqrt,
                            bias=0.0, scale=1.0), [ev])
                        ev = P.op("vector", lambda e: e.tensor_tensor(
                            rsc[:, 128:256], rsc[:, 256:384], rsc[:, 384:512],
                            ALU.mult), [evA1])
                        ev = P.op("vector", lambda e: e.tensor_scalar(
                            rsc[:, 128:256], rsc[:, 128:256], EPS, None,
                            ALU.add), [ev])
                        ev = P.op("vector", lambda e: e.reciprocal(
                            rsc[:, 128:256], rsc[:, 128:256]), [ev])
                        evA2 = P.op("scalar", lambda e: e.activation(
                            rsc[:, 128:256], rsc[:, 128:256], AF.Sqrt,
                            bias=0.0, scale=1.0), [ev])
                        evCC = P.op("vector", lambda e: e.tensor_tensor(
                            rsc[:, 512:640], rsc[:, 512:640], rsc[:, 128:256],
                            ALU.mult), [evA2])
                        evMC = P.op("vector", lambda e: e.scalar_tensor_tensor(
                            rsc[:, 384:512], rsc[:, 0:128], -1.0,
                            rsc[:, 512:640], ALU.mult, ALU.mult), [evCC])
                        eva = evb_ = None
                        for c in range(ND):
                            if c == 0:
                                eva = P.op("vector", lambda e, x_ap=x_ap:
                                           e.tensor_copy(
                                    racc[:], x_ap[:, 0]), [P.full("t")])
                                evb_ = P.op("vector", lambda e, x_ap=x_ap:
                                            e.tensor_copy(
                                    rtmp[:], x_ap[:, 0]), [P.full("t")])
                            else:
                                eva = P.op("vector", lambda e, c=c,
                                           x_ap=x_ap: e.tensor_tensor(
                                    racc[:], racc[:], x_ap[:, c], ALU.max),
                                    [eva])
                                evb_ = P.op("vector", lambda e, c=c,
                                            x_ap=x_ap: e.tensor_tensor(
                                    rtmp[:], rtmp[:], x_ap[:, c], ALU.min),
                                    [evb_])
                        evT = P.op("tensor", lambda e: e.transpose(
                            pTT[:], racc[:], ident32[:]),
                            [eva, ev_id, P.full("v")])
                        evR = P.op("vector", lambda e: e.tensor_reduce(
                            racc[:, 0:1], pTT[:], AX.X, ALU.max), [evT])
                        evT2 = P.op("tensor", lambda e: e.transpose(
                            pTr[0:1, 0:128], racc[:, 0:1], ident32[:]),
                            [evR, P.full("v")])
                        evmx = P.op("vector", lambda e: e.scalar_tensor_tensor(
                            rsc[:, 640:768], rsc[:, 0:128], -1.0,
                            pTr[0:1, 0:128], ALU.mult, ALU.add), [evT2, ev])
                        evT = P.op("tensor", lambda e: e.transpose(
                            pTT[:], rtmp[:], ident32[:]),
                            [evb_, P.full("v")])
                        evR = P.op("vector", lambda e: e.tensor_reduce(
                            rtmp[:, 0:1], pTT[:], AX.X, ALU.min), [evT])
                        evT2 = P.op("tensor", lambda e: e.transpose(
                            pTr[0:1, 0:128], rtmp[:, 0:1], ident32[:]),
                            [evR, P.full("v")])
                        evmn = P.op("vector", lambda e: e.scalar_tensor_tensor(
                            rsc[:, 0:128], pTr[0:1, 0:128], -1.0,
                            rsc[:, 0:128], ALU.mult, ALU.add), [evT2, evmx])
                        ev = P.op("vector", lambda e: e.tensor_tensor(
                            rsc[:, 640:768], rsc[:, 640:768], rsc[:, 0:128],
                            ALU.max), [evmn])
                        ev = P.op("vector", lambda e: e.tensor_tensor(
                            rsc[:, 640:768], rsc[:, 640:768], rsc[:, 512:640],
                            ALU.mult), [ev, evCC])
                        ev = P.op("vector", lambda e: e.tensor_scalar(
                            rsc[:, 0:128], rsc[:, 640:768], 1e-5, 1.0 / 127.0,
                            ALU.max, ALU.mult), [ev])
                        ev_s = P.op("vector", lambda e: e.reciprocal(
                            rsc[:, 640:768], rsc[:, 0:128]), [ev])
                        sl = slice(it * 128, (it + 1) * 128)
                        P.op("vector", lambda e, sl=sl: e.tensor_tensor(
                            rg[:, 0:512][:, sl], rsc[:, 512:640],
                            rsc[:, 640:768], ALU.mult), [ev_s])
                        evsh = P.op("vector", lambda e, sl=sl: e.tensor_tensor(
                            rg[:, 512:1024][:, sl], rsc[:, 384:512],
                            rsc[:, 640:768], ALU.mult), [ev_s, evMC])
                        P.op("vector", lambda e, sl=sl: e.tensor_scalar(
                            rg[:, 1024:1536][:, sl], rsc[:, 0:128],
                            mwsb[0:1, 0:1], None, ALU.mult),
                            [ev_s, P.full("g")])
                    evb_sb = P.op("tensor", lambda e: e.matmul(
                        pBB[:], ones_row[:], rg[:, 0:512],
                        start=True, stop=True), [P.full("v")])
                    evb_sh = P.op("tensor", lambda e: e.matmul(
                        psHr[0][:], ones_row[:], rg[:, 512:1024],
                        start=True, stop=True), [P.full("v")])
                    evc1 = None
                    for q4 in range(4):
                        evbq = P.op("tensor", lambda e, q4=q4: e.matmul(
                            pmB[:], ones_row[:],
                            rg[:, 1024 + q4 * 128:1024 + (q4 + 1) * 128],
                            start=True, stop=True),
                            [P.full("v")] + ([evc1] if evc1 else []))
                        evc1 = P.op("vector", lambda e, q4=q4: e.tensor_copy(
                            c1bg[:, q4 * 128:(q4 + 1) * 128], pmB[:]), [evbq])
                    ev_qgw = []
                    for c in range(ND):
                        pi = c % 2
                        ev1 = P.op("vector", lambda e, c=c, g=g, pi=pi:
                                   e.tensor_tensor(
                            tg[pi][:], X3[:, c, g * 512:(g + 1) * 512],
                            pBB[:], ALU.mult), [evb_sb, P.full("v")])
                        ev3 = P.op("vector", lambda e, pi=pi: e.tensor_tensor(
                            qi32[pi][:], tg[pi][:], psHr[0][:], ALU.add),
                            [ev1, evb_sh])
                        ev_qgw.append(P.op("vector", lambda e, c=c, pi=pi:
                                           e.tensor_copy(
                            Qg3[:, c], qi32[pi][:]), [ev3]))
                    ev_ac = None
                    ev_epi_bank = [ev_qgw[ND - 1], ev_epi_bank[1]]
                    pend_pg = None
                    for f in range(NFC):
                        pb2 = f % 2
                        ph = psHr[pb2]
                        evm = None
                        for c in range(ND):
                            evm = P.op("tensor", lambda e, c=c, f=f, ph=ph:
                                       e.matmul(
                                ph[:], t13[:, c, f * 128:(f + 1) * 128],
                                Qg3[:, c], start=(c == 0),
                                stop=(c == ND - 1)),
                                [ev_t1, ev_qgw[c]] +
                                ([ev_epi_bank[pb2]]
                                 if c == 0 and ev_epi_bank[pb2] else []))
                        if pend_pg is not None:
                            ev_sq = P.op("tensor", *pend_pg)
                            pend_pg = None
                        ev_epi = P.op("vector", lambda e, ph=ph:
                                      e.tensor_tensor(
                            h32[:], ph[:], c1bg[:], ALU.mult),
                            [evm] + ([ev_epi] if ev_epi else []) +
                            ([ev_gg] if ev_gg else []))
                        ev_epi_bank[pb2] = ev_epi
                        ev_gg = P.op("scalar", lambda e, f=f: e.activation(
                            gbf[:], h32[:], AF.Gelu, bias=atl("b1", f),
                            scale=1.0), [ev_epi, ev_atlas] +
                            ([ev_gst] if ev_gst else []) +
                            ([ev_sq2] if ev_sq2 else []))
                        ev_gst = dS.issue(gd3[:, f, g * 512:(g + 1) * 512],
                                          gbf[:], waits=[ev_gg])
                        ev_sq2 = P.op("scalar", lambda e: e.activation(
                            g2b[:], gbf[:], AF.Square, bias=0.0, scale=1.0),
                            [ev_gg] + ([ev_sq] if ev_sq else []))
                        pend_pg = (
                            lambda e, f=f: e.matmul(
                                pG[0:1, :], ones_col_bf[:], g2b[:],
                                start=(f == 0), stop=(f == NFC - 1)),
                            [ev_sq2] + ([ev_pgrd] if f == 0 and ev_pgrd
                                        else []))
                        if f == 0:
                            ev_ac = P.op("vector", lambda e: e.tensor_copy(
                                acc2[:], gbf[:]), [ev_gg] +
                                ([ev_accT] if ev_accT else []))
                            ev_ac2 = P.op("vector", lambda e: e.tensor_copy(
                                acc2n[:], gbf[:]), [ev_gg] +
                                ([ev_accT] if ev_accT else []))
                        else:
                            ev_ac = P.op("vector", lambda e: e.tensor_tensor(
                                acc2[:], acc2[:], gbf[:], ALU.max),
                                [ev_gg, ev_ac])
                            ev_ac2 = P.op("vector", lambda e: e.tensor_tensor(
                                acc2n[:], acc2n[:], gbf[:], ALU.min),
                                [ev_gg, ev_ac2])
                    if pend_pg is not None:
                        ev_sq = P.op("tensor", *pend_pg)
                        pend_pg = None
                    ev_cmb = P.op("vector", lambda e: e.scalar_tensor_tensor(
                        acc2[:], acc2n[:], -1.0, acc2[:], ALU.mult, ALU.max),
                        [ev_ac, ev_ac2])
                    for b4 in range(4):
                        evT = P.op("tensor", lambda e, b4=b4: e.transpose(
                            pTT[:], acc2[:, b4 * 128:(b4 + 1) * 128],
                            ident32[:]), [ev_cmb, ev_id, P.full("v")])
                        ev_accT = evT
                        evR = P.op("vector", lambda e: e.tensor_reduce(
                            rtmp[:, 0:1], pTT[:], AX.X, ALU.max), [evT])
                        evT2 = P.op("tensor", lambda e: e.transpose(
                            pTr[0:1, 0:128], rtmp[:, 0:1], ident32[:]),
                            [evR, P.full("v")])
                        P.op("vector", lambda e, g=g, b4=b4: e.tensor_copy(
                            am2row[:, g * 512 + b4 * 128:
                                   g * 512 + (b4 + 1) * 128],
                            pTr[0:1, 0:128]), [evT2])
                    ev_pgrd = P.op("vector", lambda e, g=g: e.tensor_copy(
                        sgrow[:, g * 512:(g + 1) * 512], pG[0:1, :]), [ev_sq])
                P.barrier()
                P.flush()

            ev = P.op("vector", lambda e: e.tensor_scalar(
                sgrow[:], sgrow[:], 1.0 / FF, EPS, ALU.mult, ALU.add),
                [P.full("v"), P.full("g")])
            ev = P.op("vector", lambda e: e.reciprocal(sgrow[:], sgrow[:]),
                      [ev])
            evA3 = P.op("scalar", lambda e: e.activation(
                sgrow[:], sgrow[:], AF.Sqrt, bias=0.0, scale=1.0), [ev])
            ev = P.op("vector", lambda e: e.tensor_tensor(
                am2row[:], am2row[:], sgrow[:], ALU.mult), [evA3])
            ev = P.op("vector", lambda e: e.tensor_scalar(
                am2row[:], am2row[:], 1e-5, 1.0 / 127.0, ALU.max, ALU.mult),
                [ev])
            ev_s2 = P.op("vector", lambda e: e.reciprocal(
                rowB[:], am2row[:]), [ev])
            ev_sb2 = P.op("vector", lambda e: e.tensor_tensor(
                sgrow[:], sgrow[:], rowB[:], ALU.mult), [ev_s2])
            ev_c2 = P.op("vector", lambda e: e.tensor_scalar(
                am2row[:], am2row[:], mwsb[0:1, 1:2], None, ALU.mult),
                [ev_s2, P.full("g")])

            with ExitStack() as esB:
                t2b = esB.enter_context(
                    nc.sbuf_tensor(f"{tag}_t2b", [128, NFC * D], BF16))
                t23 = t2b[:].rearrange("p (f d) -> p f d", f=NFC)
                ev_t2 = dS.issue(t23,
                                 t2_dram.rearrange("(f p) d -> p f d", p=128),
                                 waits=[P.full("t")])
                sb2b = esB.enter_context(
                    nc.sbuf_tensor(f"{tag}_sb2b", [128, S], F32))
                c2b = esB.enter_context(
                    nc.sbuf_tensor(f"{tag}_c2b", [128, S], F32))
                with ExitStack() as esB1:
                    pBB2 = esB1.enter_context(
                        nc.psum_tensor(f"{tag}_pBB2", [128, 512], F32))
                    evp = None
                    for g in range(NQG):
                        evb = P.op("tensor", lambda e, g=g: e.matmul(
                            pBB2[:], ones_row[:],
                            sgrow[:, g * 512:(g + 1) * 512],
                            start=True, stop=True),
                            [ev_sb2] + ([evp] if evp else []))
                        evp = P.op("vector", lambda e, g=g: e.tensor_copy(
                            sb2b[:, g * 512:(g + 1) * 512], pBB2[:]), [evb])
                        evb = P.op("tensor", lambda e, g=g: e.matmul(
                            pBB2[:], ones_row[:],
                            am2row[:, g * 512:(g + 1) * 512],
                            start=True, stop=True), [ev_c2, evp])
                        evp = P.op("vector", lambda e, g=g: e.tensor_copy(
                            c2b[:, g * 512:(g + 1) * 512], pBB2[:]), [evb])
                    P.barrier()
                    P.flush()
                with ExitStack() as esB2:
                    pyB = [esB2.enter_context(nc.psum_tensor(
                        f"{tag}_pyB{i}", [128, 512], F32)) for i in range(ND)]
                    gblk = [esB2.enter_context(nc.sbuf_tensor(
                        f"{tag}_gblk{i}", [128, 512], FP16)) for i in range(2)]
                    tg2 = [esB2.enter_context(nc.sbuf_tensor(
                        f"{tag}_tg2{i}", [128, 512], F32)) for i in range(2)]
                    qi2 = [esB2.enter_context(nc.sbuf_tensor(
                        f"{tag}_qi2{i}", [128, 512], I32)) for i in range(2)]
                    q2b = [esB2.enter_context(nc.sbuf_tensor(
                        f"{tag}_q2b{i}", [128, 512], BF16)) for i in range(2)]
                    evq_prev = [None, None]
                    for g in range(NQG):
                        for fc in range(NFC):
                            pi = fc % 2
                            w0 = [evq_prev[pi]] if evq_prev[pi] else []
                            evl = dS.issue(gblk[pi][:],
                                           gd3[:, fc, g * 512:(g + 1) * 512],
                                           waits=w0)
                            ev2 = P.op("vector", lambda e, pi=pi, g=g:
                                       e.tensor_tensor(
                                qi2[pi][:], gblk[pi][:],
                                sb2b[:, g * 512:(g + 1) * 512], ALU.mult),
                                [evl] + w0 +
                                ([P.full("v")] if fc < 2 else []))
                            ev3 = P.op("vector", lambda e, pi=pi:
                                       e.tensor_copy(
                                q2b[pi][:], qi2[pi][:]), [ev2])
                            for dd in range(ND):
                                evq_prev[pi] = P.op(
                                    "tensor",
                                    lambda e, fc=fc, dd=dd, pi=pi: e.matmul(
                                        pyB[dd][:],
                                        t23[:, fc, dd * 128:(dd + 1) * 128],
                                        q2b[pi][:], start=(fc == 0),
                                        stop=(fc == NFC - 1)), [ev3, ev_t2])
                        for dd in range(ND):
                            ev1 = P.op("vector", lambda e, dd=dd, g=g:
                                       e.tensor_tensor(
                                tg2[0][:], pyB[dd][:],
                                c2b[:, g * 512:(g + 1) * 512], ALU.mult),
                                [evq_prev[0], evq_prev[1], P.full("v")])
                            P.op("vector", lambda e, dd=dd, g=g:
                                 e.scalar_tensor_tensor(
                                X3[:, dd, g * 512:(g + 1) * 512], tg2[0][:],
                                atl("b2", dd),
                                X3[:, dd, g * 512:(g + 1) * 512],
                                ALU.add, ALU.add), [ev1, ev_atlas])
                    P.barrier()
                    P.flush()

    ternarize_weights()
    ffn_phase("f1")
    if upto == "ffn1":
        return store_out()

    NC16 = S // 128
    k_dram = nc.dram_tensor("k_dram", [NPAIR * 128, S], BF16,
                            kind="Internal").ap()
    vt_dram = nc.dram_tensor("vt_dram", [NPAIR * 128, S], BF16,
                             kind="Internal").ap()
    gn_dram = nc.dram_tensor("gn_dram", [NPAIR * 128, S], BF16,
                             kind="Internal").ap()
    ab1_dram = nc.dram_tensor("ab1_dram", [2, S], F32, kind="Internal").ap()
    xpre_dram = nc.dram_tensor("xpre_dram", [D, S], F32,
                               kind="Internal").ap()
    kd3 = k_dram.rearrange("(r p) s -> p r s", r=NPAIR)
    vd3 = vt_dram.rearrange("(r p) s -> p r s", r=NPAIR)
    gd3b = gn_dram.rearrange("(r p) s -> p r s", r=NPAIR)

    with ExitStack() as esR:
        def sbR(name, shape, dt=F32):
            return esR.enter_context(nc.sbuf_tensor(name, list(shape), dt))

        ccrow = sbR("b_cc", [1, S])
        rows2 = sbR("b_rows2", [2, S])
        onesS = sbR("b_onesS", [1, 512])
        n1sb = sbR("b_n1", [2, D])
        gnwsb = sbR("b_gnw", [1, 2 * D])
        gnbsb = sbR("b_gnb", [1, D])
        coefsb = sbR("b_coef", [128, 128])
        indcol = sbR("b_indcol", [128, 2], BF16)
        snap = sbR("b_snap", [128, NPAIR * (NC16 + 1) * HD], BF16)
        SN4 = snap[:].rearrange("p (r c d) -> p r c d", r=NPAIR, c=NC16 + 1)
        sinall = sbR("b_sin", [128, NPAIR * HD])
        SI3 = sinall[:].rearrange("p (r d) -> p r d", r=NPAIR)
        xnp = sbR("b_xnp", [128, ND * S], BF16)
        XN3 = xnp[:].rearrange("p (c s) -> p c s", c=ND)
        xtmp_b = sbR("b_xtmp", [128, 512])
        wbuf = sbR("b_wbuf", [128, ND * 128], BF16)
        rsc = sbR("b_rsc", [1, 768])

        ev_ld = [dS.issue(n1sb[:], n1wb), dS.issue(gnwsb[:], gnwm),
                 dS.issue(gnbsb[:], gnbrow)]
        ev_xpre = dS.issue(xpre_dram.rearrange("(c p) s -> p c s", p=128),
                           X3, waits=[P.full("v")])
        P.wait("vector", ev_xpre)
        P.op("vector", lambda e: e.memset(onesS[:], 1.0))
        P.op("vector", lambda e: e.memset(rows2[0:2, :], 1.0))
        P.op("vector", lambda e: e.memset(indcol[0:64, 0:1], 1.0))
        P.op("vector", lambda e: e.memset(indcol[64:128, 0:1], 0.0))
        P.op("vector", lambda e: e.memset(indcol[0:64, 1:2], 0.0))
        P.op("vector", lambda e: e.memset(indcol[64:128, 1:2], 1.0))

        with ExitStack() as esC:
            pcf = esC.enter_context(nc.psum_tensor("b_pcf", [128, 128], F32))
            psS = esC.enter_context(nc.psum_tensor("b_psS", [1, 256], F32))
            x2f = [esC.enter_context(nc.sbuf_tensor(f"b_x2f{i}", [128, D],
                                                    F32)) for i in range(2)]
            a1row = esC.enter_context(nc.sbuf_tensor("b_a1", [1, S], F32))
            b1row = esC.enter_context(nc.sbuf_tensor("b_b1", [1, S], F32))
            ev_cf = dS.issue(coefsb[0:1, :], coef_row)
            evb = P.op("tensor", lambda e: e.matmul(
                pcf[:], ones_row[:], coefsb[0:1, :], start=True, stop=True),
                [ev_cf])
            P.op("vector", lambda e: e.tensor_copy(coefsb[:], pcf[:]), [evb])
            for tt in range(NT):
                pi = tt % 2
                x_ap = X3[:, :, tt * 128:(tt + 1) * 128]
                evs = P.op("scalar", lambda e, x_ap=x_ap, pi=pi: e.activation(
                    x2f[pi][:].rearrange("p (c n) -> p c n", c=ND), x_ap,
                    AF.Square, bias=0.0, scale=1.0),
                    [P.full("v"), P.full("t")])
                ev_m1 = ev_m2 = None
                for c in range(ND):
                    ev_m1 = P.op("tensor", lambda e, c=c, x_ap=x_ap: e.matmul(
                        psS[0:1, 0:128], ones_col32[:], x_ap[:, c],
                        start=(c == 0), stop=(c == ND - 1)), [P.full("v")])
                for c in range(ND):
                    ev_m2 = P.op("tensor", lambda e, c=c, pi=pi: e.matmul(
                        psS[0:1, 128:256], ones_col32[:],
                        x2f[pi][:, c * 128:(c + 1) * 128],
                        start=(c == 0), stop=(c == ND - 1)), [evs])
                sl = slice(tt * 128, (tt + 1) * 128)
                ev = P.op("vector", lambda e: e.tensor_scalar(
                    rsc[:, 0:128], psS[0:1, 0:128], 1.0 / D, None, ALU.mult),
                    [ev_m1, ev_m2, P.full("v")])
                ev = P.op("vector", lambda e: e.tensor_tensor(
                    rsc[:, 128:256], rsc[:, 0:128], rsc[:, 0:128], ALU.mult),
                    [ev])
                ev = P.op("vector", lambda e: e.scalar_tensor_tensor(
                    rsc[:, 256:384], psS[0:1, 128:256], 1.0 / D,
                    rsc[:, 128:256], ALU.mult, ALU.subtract), [ev])
                ev = P.op("vector", lambda e: e.tensor_scalar(
                    rsc[:, 384:512], rsc[:, 256:384], EPS, None, ALU.add),
                    [ev])
                ev = P.op("vector", lambda e: e.reciprocal(
                    rsc[:, 384:512], rsc[:, 384:512]), [ev])
                evA1 = P.op("scalar", lambda e: e.activation(
                    rsc[:, 512:640], rsc[:, 384:512], AF.Sqrt, bias=0.0,
                    scale=1.0), [ev, P.full("v")])
                P.op("vector", lambda e, sl=sl: e.tensor_copy(
                    a1row[:, sl], rsc[:, 512:640]), [evA1])
                P.op("vector", lambda e, sl=sl: e.scalar_tensor_tensor(
                    b1row[:, sl], rsc[:, 0:128], -1.0, rsc[:, 512:640],
                    ALU.mult, ALU.mult), [evA1])
                ev = P.op("vector", lambda e: e.tensor_tensor(
                    rsc[:, 128:256], rsc[:, 256:384], rsc[:, 384:512],
                    ALU.mult), [evA1])
                ev = P.op("vector", lambda e: e.tensor_scalar(
                    rsc[:, 128:256], rsc[:, 128:256], EPS, None, ALU.add),
                    [ev])
                ev = P.op("vector", lambda e: e.reciprocal(
                    rsc[:, 128:256], rsc[:, 128:256]), [ev])
                evA2 = P.op("scalar", lambda e: e.activation(
                    rsc[:, 128:256], rsc[:, 128:256], AF.Sqrt, bias=0.0,
                    scale=1.0), [ev])
                evCC = P.op("vector", lambda e, sl=sl: e.tensor_tensor(
                    ccrow[:, sl], rsc[:, 512:640], rsc[:, 128:256],
                    ALU.mult), [evA2])
                P.op("vector", lambda e, sl=sl: e.scalar_tensor_tensor(
                    rows2[0:1, sl], rsc[:, 0:128], -1.0, ccrow[:, sl],
                    ALU.mult, ALU.mult), [evCC])
            dS.issue(ab1_dram[0:1, :], a1row[:], waits=[P.full("v")])
            dS.issue(ab1_dram[1:2, :], b1row[:], waits=[P.full("v")])
            P.barrier()
            P.flush()

        def xn_all(pABx, pBBx, nrows):
            evs = []
            for c in range(ND):
                for g in range(NQG):
                    evb1 = P.op("tensor", lambda e, c=c, g=g, nrows=nrows:
                                e.matmul(
                        pABx[:], nrows[0:1, c * 128:(c + 1) * 128],
                        ccrow[:, g * 512:(g + 1) * 512], start=True,
                        stop=True), ev_ld + [P.full("v")])
                    evb2 = P.op("tensor", lambda e, c=c, g=g, nrows=nrows:
                                e.matmul(
                        pBBx[:], nrows[0:2, c * 128:(c + 1) * 128],
                        rows2[0:2, g * 512:(g + 1) * 512], start=True,
                        stop=True), [P.full("v")])
                    ev1 = P.op("vector", lambda e, c=c, g=g, pABx=pABx:
                               e.tensor_tensor(
                        xtmp_b[:], X3[:, c, g * 512:(g + 1) * 512], pABx[:],
                        ALU.mult), [evb1, P.full("v")])
                    evs.append(P.op("vector", lambda e, c=c, g=g, pBBx=pBBx:
                                    e.tensor_tensor(
                        XN3[:, c, g * 512:(g + 1) * 512], xtmp_b[:],
                        pBBx[:], ALU.add), [ev1, evb2]))
            return evs

        def projp(wsrc, col0, bias_ap, scale, dst_ap, ppb, wl=(), act=None,
                  eguard=(), bank_guard=()):
            w3 = wsrc.rearrange("(c p) n -> p c n", p=128)
            evw = dG.issue(wbuf[:].rearrange("p (c n) -> p c n", c=ND),
                           w3[:, :, col0:col0 + 128], waits=[P.full("t")])
            evs = []
            rot = ppb[0] is not ppb[1]
            nb = 2 if rot else 1
            for g in range(NQG):
                pp_ = ppb[g % 2]
                evm = None
                for c in range(ND):
                    evm = P.op("tensor", lambda e, c=c, g=g, pp_=pp_:
                               e.matmul(
                        pp_[:], wbuf[:, c * 128:(c + 1) * 128],
                        XN3[:, c, g * 512:(g + 1) * 512],
                        start=(c == 0), stop=(c == ND - 1)),
                        [evw, P.full("v")] + list(wl) +
                        ([evs[-nb]] if len(evs) >= nb else []) +
                        (list(bank_guard) if c == 0 and g < nb else []))
                if act is None:
                    evs.append(P.op("vector", lambda e, g=g, pp_=pp_:
                                    e.tensor_scalar(
                        dst_ap[:, g * 512:(g + 1) * 512], pp_[:], bias_ap,
                        scale, ALU.add, ALU.mult), [evm] + list(eguard)))
                else:
                    evs.append(P.op("scalar", lambda e, g=g, pp_=pp_:
                                    e.activation(
                        dst_ap[:, g * 512:(g + 1) * 512], pp_[:], act,
                        bias=bias_ap, scale=1.0), [evm] + list(eguard)))
            return evs

        with ExitStack() as esB1:
            def sbB1(name, shape, dt=F32):
                return esB1.enter_context(nc.sbuf_tensor(name, list(shape),
                                                         dt))
            kpair = sbB1("b_kpair", [128, S], BF16)
            vpair = sbB1("b_vpair", [128, S], BF16)
            vtm = sbB1("b_vtm", [128, S], BF16)
            ktm = sbB1("b_ktm", [128, S], BF16)
            kgt = sbB1("b_kgt", [128, 2 * HD], BF16)
            scur = sbB1("b_scur", [128, HD])
            gpcol = sbB1("b_gpcol", [128, 2])
            ppb = [esB1.enter_context(nc.psum_tensor(f"b_ppb{i}",
                                                      [128, 512], F32))
                   for i in range(2)]
            pABx = esB1.enter_context(nc.psum_tensor("b_pABx", [128, 512],
                                                     F32))
            pBBx = esB1.enter_context(nc.psum_tensor("b_pBBx", [128, 512],
                                                     F32))
            ptrb = esB1.enter_context(nc.psum_tensor("b_ptrb", [128, 128],
                                                     BF16))
            pU = esB1.enter_context(nc.psum_tensor("b_pU", [128, HD], F32))
            prev_store = []
            evxn = xn_all(pABx, pBBx, n1sb)
            for p in range(NPAIR):
                ev_k = projp(wk_b, 128 * p, atl("bk", p), 1.0, kpair[:], ppb,
                             eguard=prev_store)
                ev_v = projp(wv_b, 128 * p, atl("bv", p), 1.0, vpair[:], ppb,
                             wl=ev_k)
                ev_ks = dS.issue(kd3[:, p], kpair[:], waits=ev_k)
                ev_t = []
                for a in range(NC16):
                    evt = P.op("tensor", lambda e, a=a: e.transpose(
                        ptrb[:], vpair[:, a * 128:(a + 1) * 128], identbf[:]),
                        ev_v + [P.full("v")])
                    ev_t.append(P.op("vector", lambda e, a=a: e.tensor_copy(
                        vtm[:, a * 128:(a + 1) * 128], ptrb[:]),
                        [evt] + prev_store))
                for a in range(NC16):
                    evt = P.op("tensor", lambda e, a=a: e.transpose(
                        ptrb[:], kpair[:, a * 128:(a + 1) * 128], identbf[:]),
                        ev_k + [P.full("v")])
                    ev_t.append(P.op("vector", lambda e, a=a: e.tensor_copy(
                        ktm[:, a * 128:(a + 1) * 128], ptrb[:]), [evt]))
                ev_vs = dS.issue(vd3[:, p], vtm[:], waits=[P.full("v")])
                prev_store = [ev_ks, ev_vs]
                evg = []
                for hb in (0, 1):
                    h = 2 * p + hb
                    evg.append(P.op("scalar", lambda e, hb=hb, h=h:
                                    e.activation(
                        gpcol[:, hb:hb + 1], iocol127[:], AF.Exp,
                        bias=0.0, scale=-LNG[h]),
                        [ev_iocol, P.full("v")]))
                ev_sn0 = P.op("vector", lambda e, p=p: e.memset(
                    SN4[:, p, 0], 0.0), [P.full("t")])
                ev_sc0 = P.op("vector", lambda e: e.memset(scur[:], 0.0),
                              [P.full("t")])
                ev_prev = ev_sc0
                for c in range(NC16):
                    ev_kg = P.op("vector", lambda e, c=c: e.tensor_scalar(
                        kgt[:, 0:HD],
                        ktm[:, c * 128:(c + 1) * 128]
                        .rearrange("p (hb d) -> p hb d", hb=2)[:, 0],
                        gpcol[:, 0:1], None, ALU.mult),
                        ev_t + evg + [ev_prev])
                    ev_kg2 = P.op("vector", lambda e, c=c: e.tensor_scalar(
                        kgt[:, HD:2 * HD],
                        ktm[:, c * 128:(c + 1) * 128]
                        .rearrange("p (hb d) -> p hb d", hb=2)[:, 1],
                        gpcol[:, 1:2], None, ALU.mult), [ev_prev])
                    evu = []
                    for hb in (0, 1):
                        evu.append(P.op("tensor",
                                        lambda e, c=c, hb=hb: e.matmul(
                            pU[hb * HD:(hb + 1) * HD, :],
                            kgt[:, hb * HD:(hb + 1) * HD],
                            vtm[:, c * 128 + hb * HD:c * 128 + hb * HD + HD],
                            start=True, stop=True,
                            tile_position=(0, hb * HD)),
                            [ev_kg, ev_kg2, P.full("v")]))
                    ev_prev = None
                    for hb in (0, 1):
                        g128 = GAMMAS[2 * p + hb] ** 128
                        evs_ = P.op("vector",
                                    lambda e, hb=hb, g128=g128:
                                    e.scalar_tensor_tensor(
                            scur[hb * HD:(hb + 1) * HD, :],
                            scur[hb * HD:(hb + 1) * HD, :], g128,
                            pU[hb * HD:(hb + 1) * HD, :], ALU.mult, ALU.add),
                            evu + ([ev_prev] if ev_prev else []))
                        ev_prev = P.op("vector",
                                       lambda e, c=c, hb=hb, p=p:
                                       e.tensor_copy(
                            SN4[:, p, c + 1][hb * HD:(hb + 1) * HD, :],
                            scur[hb * HD:(hb + 1) * HD, :]), [evs_])
            P.barrier()
            P.flush()
        if upto == "bB1":
            return store_out()

        ev_ag_in = dS.issue(
            cc_in.rearrange("(pr hb d1) d2 -> (hb d1) pr d2", pr=NPAIR, hb=2),
            SN4[:, :, NC16], waits=[P.full("v")])
        ev_cc = P.op("gpsimd", lambda e: e.collective_compute(
            "AllGather", ALU.bypass, replica_groups=[list(range(n_cores))],
            ins=[cc_in], outs=[cc_out]), [ev_ag_in], sem="cc")
        with ExitStack() as esAG:
            agall = esAG.enter_context(nc.sbuf_tensor(
                "b_agall", [128, n_cores * NPAIR * HD], BF16))
            AG4 = agall[:].rearrange("p (r pr d) -> p r pr d", r=n_cores,
                                     pr=NPAIR)
            ev_agl = dS.issue(
                AG4, cc_out.rearrange("(r pr hb d1) d2 -> (hb d1) r pr d2",
                                      r=n_cores, pr=NPAIR, hb=2),
                waits=[ev_cc])
            for p in range(NPAIR):
                for hb in (0, 1):
                    h = 2 * p + hb
                    ev = None
                    for r in range(n_cores):
                        if r == 0:
                            ev = P.op("vector",
                                      lambda e, p=p, hb=hb, h=h, r=r:
                                      e.tensor_scalar(
                                SI3[:, p][hb * HD:(hb + 1) * HD, :],
                                AG4[:, r, p][hb * HD:(hb + 1) * HD, :],
                                coefsb[hb * HD:(hb + 1) * HD,
                                       r * NH + h:r * NH + h + 1],
                                None, ALU.mult), [ev_agl, P.full("t")])
                        else:
                            ev = P.op("vector",
                                      lambda e, p=p, hb=hb, h=h, r=r:
                                      e.scalar_tensor_tensor(
                                SI3[:, p][hb * HD:(hb + 1) * HD, :],
                                AG4[:, r, p][hb * HD:(hb + 1) * HD, :],
                                coefsb[hb * HD:(hb + 1) * HD,
                                       r * NH + h:r * NH + h + 1],
                                SI3[:, p][hb * HD:(hb + 1) * HD, :],
                                ALU.mult, ALU.add), [ev])
            P.barrier()
            P.flush()

        with ExitStack() as esB2:
            def sbB2(name, shape, dt=F32):
                return esB2.enter_context(nc.sbuf_tensor(name, list(shape),
                                                         dt))
            qp = sbB2("b_qp", [128, S], BF16)
            qgam = sbB2("b_qgam", [128, S], BF16)
            sg = sbB2("b_sg", [128, S])
            kpair2 = sbB2("b_kpair2", [128, S], BF16)
            vtm2 = sbB2("b_vtm2", [128, S], BF16)
            NHV = 2 if NQG >= 2 else 1
            rp = sbB2("b_rp", [128, S // NHV])
            rpb = sbB2("b_rpb", [128, 512], BF16)
            rp2b = sbB2("b_rp2b", [128, 512], BF16)
            gnp = sbB2("b_gnp", [128, S // NHV], BF16)
            stot = sbB2("b_stot", [128, HD], BF16)
            ab2 = sbB2("b_ab2", [128, 256], BF16)
            dm = sbB2("b_dm", [128, 256])
            grow = sbB2("b_grow", [1, 256])
            rgn = sbB2("b_rgn", [1, 4 * 512])
            gB = sbB2("b_gB", [128, 128])
            with ExitStack() as esPs:
                ppb2s = esPs.enter_context(nc.psum_tensor(
                    "b_ppb2", [128, 512], F32))
                ppb2 = [ppb2s, ppb2s]
                pABx2 = esPs.enter_context(nc.psum_tensor("b_pABx2",
                                                          [128, 512], F32))
                pBBx2 = esPs.enter_context(nc.psum_tensor("b_pBBx2",
                                                          [128, 512], F32))
                psCe = esPs.enter_context(nc.psum_tensor("b_psCe",
                                                          [128, 128], F32))
                psCo = esPs.enter_context(nc.psum_tensor("b_psCo",
                                                          [128, 128], F32))
                pret = esPs.enter_context(nc.psum_tensor("b_pret", [128, 128],
                                                         F32))
                pgs = esPs.enter_context(nc.psum_tensor("b_pgs", [1, 1024],
                                                        F32))
                ev_g_prev = []
                for p in range(NPAIR):
                    ev_q = projp(wq_b, 128 * p, atl("bq", p), QSC, qp[:],
                                 ppb2, bank_guard=ev_g_prev[-2:])
                    ev_g = projp(wg_b, 128 * p, atl("bg", p), 1.0, sg[:],
                                 ppb2, wl=ev_q, act=AF.Silu)
                    ev_g_prev = ev_g
                    ev_kl = dS.issue(kpair2[:], kd3[:, p],
                                     waits=[P.full("t")])
                    ev_vl = dS.issue(vtm2[:], vd3[:, p], waits=[P.full("t")])
                    evgr = []
                    for hb in (0, 1):
                        h = 2 * p + hb
                        evgr.append(P.op("scalar", lambda e, hb=hb, h=h:
                                         e.activation(
                            grow[0:1, hb * 128:(hb + 1) * 128], iorow1[:],
                            AF.Exp, bias=0.0, scale=LNG[h]),
                            [ev_iorow, P.full("v")]))
                    evgb = P.op("tensor", lambda e: e.matmul(
                        pABx2[0:128, 0:128], indrow32[0:1, 0:128],
                        grow[0:1, 0:128], start=True, stop=False),
                        evgr + [P.full("v")])
                    evgb = P.op("tensor", lambda e: e.matmul(
                        pABx2[0:128, 0:128], indrow32[0:1, 128:256],
                        grow[0:1, 128:256], start=False, stop=True), [])
                    evgbc = P.op("vector", lambda e: e.tensor_copy(
                        gB[:], pABx2[0:128, 0:128]), [evgb])
                    ev_qg = P.op("vector", lambda e: e.tensor_tensor(
                        qgam[:].rearrange("p (a n) -> p a n", n=128),
                        qp[:].rearrange("p (a n) -> p a n", n=128),
                        gB[:].unsqueeze(1).broadcast_to([128, NC16, 128]),
                        ALU.mult), ev_q + [evgbc])
                    evdm = []
                    for hb in (0, 1):
                        h = 2 * p + hb
                        eve = P.op("scalar", lambda e, hb=hb, h=h:
                                   e.activation(
                            dm[:, hb * 128:(hb + 1) * 128], iojp[:], AF.Exp,
                            bias=0.0, scale=LNG[h]), [ev_io, P.full("v")])
                        evdm.append(P.op("vector", lambda e, hb=hb:
                                         e.tensor_tensor(
                            dm[:, hb * 128:(hb + 1) * 128],
                            dm[:, hb * 128:(hb + 1) * 128], maskge[:],
                            ALU.mult), [eve, ev_mge]))
                    ev_rp_prev = None
                    ev_at_prev = None
                    ev_gn_st = None
                    for half in range(NHV):
                     for c in range(half * (NC16 // NHV),
                                    (half + 1) * (NC16 // NHV)):
                         evst = []
                         for hb in (0, 1):
                             g128c = GAMMAS[2 * p + hb] ** (128 * c)
                             evst.append(P.op("vector",
                                              lambda e, c=c, hb=hb, p=p,
                                              g128c=g128c:
                                              e.scalar_tensor_tensor(
                                 stot[hb * HD:(hb + 1) * HD, :],
                                 SI3[:, p][hb * HD:(hb + 1) * HD, :], g128c,
                                 SN4[:, p, c][hb * HD:(hb + 1) * HD, :],
                                 ALU.mult, ALU.add), [P.full("t")]))
                         evC = []
                         for hb, psC in ((0, psCe), (1, psCo)):
                             evC.append(P.op("tensor",
                                             lambda e, c=c, hb=hb, psC=psC,
                                             p=p: e.matmul(
                                 psC[:], kpair2[hb * HD:(hb + 1) * HD,
                                                c * 128:(c + 1) * 128],
                                 qp[hb * HD:(hb + 1) * HD,
                                    c * 128:(c + 1) * 128],
                                 start=True, stop=True,
                                 tile_position=(hb * HD, 0)),
                                 [ev_kl] + ev_q +
                                 ([ev_at_prev] if ev_at_prev else [])))
                         ev_at = []
                         for hb, psC in ((0, psCe), (1, psCo)):
                             ev_at.append(P.op("vector",
                                               lambda e, hb=hb, psC=psC:
                                               e.tensor_tensor(
                                 ab2[:, hb * 128:(hb + 1) * 128], psC[:],
                                 dm[:, hb * 128:(hb + 1) * 128], ALU.mult),
                                 [evC[hb]] + evdm))
                         ev_at_prev = ev_at[1]
                         evR = []
                         for hb in (0, 1):
                             evR.append(P.op("tensor",
                                             lambda e, c=c, hb=hb: e.matmul(
                                 pret[hb * HD:(hb + 1) * HD, :],
                                 vtm2[:, c * 128 + hb * HD:
                                      c * 128 + hb * HD + HD],
                                 ab2[:, hb * 128:(hb + 1) * 128],
                                 start=True, stop=False,
                                 tile_position=(0, hb * HD)),
                                 [ev_vl, ev_at[hb]] +
                                 ([ev_rp_prev] if ev_rp_prev else [])))
                             evR.append(P.op("tensor",
                                             lambda e, c=c, hb=hb: e.matmul(
                                 pret[hb * HD:(hb + 1) * HD, :],
                                 stot[hb * HD:(hb + 1) * HD, :],
                                 qgam[hb * HD:(hb + 1) * HD,
                                      c * 128:(c + 1) * 128],
                                 start=False, stop=True,
                                 tile_position=(hb * HD, hb * HD)),
                                 [evst[hb], ev_qg]))
                         ev_rp_prev = P.op("vector", lambda e, c=c:
                                           e.tensor_copy(
                             rp[:, (c % (NC16 // NHV)) * 128:(c % (NC16 // NHV) + 1) * 128], pret[:]), evR)
                     for g in range(half * (NQG // NHV),
                                    (half + 1) * (NQG // NHV)):
                         sl2 = slice(g * 512, (g + 1) * 512)
                         slh = slice((g % (NQG // NHV)) * 512,
                                     (g % (NQG // NHV) + 1) * 512)
                         evcp = P.op("vector", lambda e, slh=slh:
                                     e.tensor_copy(
                             rpb[:], rp[:, slh]), [ev_rp_prev, P.full("s")])
                         evsq = P.op("scalar", lambda e: e.activation(
                             rp2b[:], rpb[:], AF.Square, bias=0.0, scale=1.0),
                             [evcp])
                         evm = []
                         for hb in (0, 1):
                             evm.append(P.op("tensor", lambda e, hb=hb:
                                             e.matmul(
                                 pgs[0:1, hb * 512:(hb + 1) * 512],
                                 indcol[:, hb:hb + 1], rpb[:],
                                 start=True, stop=True),
                                 [evcp, P.full("v")]))
                         ev1 = P.op("vector", lambda e: e.tensor_scalar(
                             rgn[:, 0:1024], pgs[0:1, 0:1024], 1.0 / HD, None,
                             ALU.mult), evm)
                         evm2 = []
                         for hb in (0, 1):
                             evm2.append(P.op("tensor", lambda e, hb=hb:
                                              e.matmul(
                                 pgs[0:1, hb * 512:(hb + 1) * 512],
                                 indcol[:, hb:hb + 1], rp2b[:],
                                 start=True, stop=True), [evsq, ev1]))
                         ev2 = P.op("vector", lambda e: e.tensor_tensor(
                             rgn[:, 1024:2048], rgn[:, 0:1024],
                             rgn[:, 0:1024], ALU.mult), [ev1])
                         ev3 = P.op("vector", lambda e: e.scalar_tensor_tensor(
                             rgn[:, 1024:2048], pgs[0:1, 0:1024], 1.0 / HD,
                             rgn[:, 1024:2048], ALU.mult, ALU.subtract),
                             evm2 + [ev2])
                         ev4 = P.op("vector", lambda e: e.tensor_scalar(
                             rgn[:, 1024:2048], rgn[:, 1024:2048], EPS, None,
                             ALU.add), [ev3])
                         ev5 = P.op("vector", lambda e: e.reciprocal(
                             rgn[:, 1024:2048], rgn[:, 1024:2048]), [ev4])
                         ev6 = P.op("scalar", lambda e: e.activation(
                             rgn[:, 1024:2048], rgn[:, 1024:2048], AF.Sqrt,
                             bias=0.0, scale=1.0), [ev5])
                         ev7 = P.op("vector", lambda e: e.scalar_tensor_tensor(
                             rgn[:, 0:1024], rgn[:, 0:1024], -1.0,
                             rgn[:, 1024:2048], ALU.mult, ALU.mult), [ev6])
                         evb1 = P.op("tensor", lambda e, p=p: e.matmul(
                             pABx2[:], gnwsb[0:1, p * 128:(p + 1) * 128],
                             rgn[:, 1024:1536], start=True, stop=False),
                             [ev6, P.full("v")])
                         evb1 = P.op("tensor", lambda e, p=p: e.matmul(
                             pABx2[:], gnwsb[0:1, D + p * 128:
                                             D + (p + 1) * 128],
                             rgn[:, 1536:2048], start=False, stop=True), [])
                         evb2 = P.op("tensor", lambda e, p=p: e.matmul(
                             pBBx2[:], gnwsb[0:1, p * 128:(p + 1) * 128],
                             rgn[:, 0:512], start=True, stop=False),
                             [ev7, P.full("v")])
                         evb2 = P.op("tensor", lambda e, p=p: e.matmul(
                             pBBx2[:], gnwsb[0:1, D + p * 128:
                                             D + (p + 1) * 128],
                             rgn[:, 512:1024], start=False, stop=False), [])
                         evb2 = P.op("tensor", lambda e, p=p: e.matmul(
                             pBBx2[:], gnbsb[0:1, p * 128:(p + 1) * 128],
                             onesS[:], start=False, stop=True), [])
                         evap = P.op("vector", lambda e, slh=slh:
                                     e.tensor_tensor(
                             xtmp_b[:], rp[:, slh], pABx2[:], ALU.mult),
                             [evb1, P.full("v")])
                         evap = P.op("vector", lambda e: e.tensor_tensor(
                             xtmp_b[:], xtmp_b[:], pBBx2[:], ALU.add),
                             [evap, evb2])
                         evap = P.op("vector", lambda e, sl2=sl2, slh=slh:
                                     e.tensor_tensor(
                             gnp[:, slh], xtmp_b[:], sg[:, sl2], ALU.mult),
                             [evap] + ev_g)
                         ev_gn_st = dS.issue(gd3b[:, p, sl2], gnp[:, slh],
                                             waits=[evap] +
                                             ([ev_gn_st] if ev_gn_st else []))
                     P.wait("vector", ev_gn_st)
                P.barrier()
                P.flush()
        if upto == "bB2":
            return store_out()

        with ExitStack() as esWo:
            wob2 = esWo.enter_context(nc.sbuf_tensor("b_wob2",
                                                     [128, ND * D], BF16))
            gnall = [esWo.enter_context(nc.sbuf_tensor(
                f"b_gnall{i}", [128, ND * 512], BF16)) for i in range(2)]
            pwo = [esWo.enter_context(nc.psum_tensor(f"b_pwo{i}", [128, 512],
                                                     F32)) for i in range(2)]
            pA1 = esWo.enter_context(nc.psum_tensor("b_pA1", [128, 512], F32))
            pB1 = esWo.enter_context(nc.psum_tensor("b_pB1", [128, 512], F32))
            ab1sb = esWo.enter_context(nc.sbuf_tensor("b_ab1", [2, S], F32))
            sel2 = esWo.enter_context(nc.sbuf_tensor("b_sel2", [2, 128], F32))
            ev_ab1 = dS.issue(ab1sb[:], ab1_dram, waits=[P.full("v")])
            evs2 = P.op("vector", lambda e: e.memset(sel2[0:2, :], 1.0))
            evs2 = P.op("vector", lambda e: e.memset(sel2[0:1, :], 0.0),
                        [evs2])
            wo3 = wo_b.rearrange("(c p) n -> p c n", p=128)
            ev_wo = dG.issue(wob2[:].rearrange("p (c n) -> p c n", c=ND),
                             wo3, waits=[P.full("t")])
            ev3 = None
            prev_use = [None, None]
            rd_prev = [None, None]
            for g in range(NQG):
                pi = g % 2
                w0 = [prev_use[pi]] if prev_use[pi] else []
                evl = dS.issue(gnall[pi][:].rearrange("p (c n) -> p c n",
                                                      c=ND),
                               gd3b[:, :, g * 512:(g + 1) * 512], waits=w0)
                evbA = P.op("tensor", lambda e, g=g: e.matmul(
                    pA1[:], ones_row[:], ab1sb[0:1, g * 512:(g + 1) * 512],
                    start=True, stop=True), [ev_ab1, P.full("v")])
                evbB = P.op("tensor", lambda e, g=g: e.matmul(
                    pB1[:], sel2[0:2, :], ab1sb[0:2, g * 512:(g + 1) * 512],
                    start=True, stop=True), [ev_ab1, evs2, P.full("v")])
                evm = None
                for dd in range(ND):
                    bi = dd % 2
                    for c in range(ND):
                        evm = P.op("tensor", lambda e, c=c, dd=dd, bi=bi,
                                   pi=pi: e.matmul(
                            pwo[bi][:], wob2[:, c * D + dd * 128:
                                             c * D + (dd + 1) * 128],
                            gnall[pi][:, c * 512:(c + 1) * 512],
                            start=(c == 0), stop=(c == ND - 1)),
                            [ev_wo, evl] +
                            ([rd_prev[bi]] if c == 0 and rd_prev[bi]
                             else []))
                    ev1 = P.op("vector", lambda e, dd=dd, g=g:
                               e.tensor_tensor(
                        xtmp_b[:], X3[:, dd, g * 512:(g + 1) * 512], pA1[:],
                        ALU.mult), [evbA] + ([ev3] if ev3 else []))
                    ev2 = P.op("vector", lambda e: e.tensor_tensor(
                        xtmp_b[:], xtmp_b[:], pB1[:], ALU.add), [ev1, evbB])
                    ev3 = P.op("vector", lambda e, dd=dd, g=g, bi=bi:
                               e.scalar_tensor_tensor(
                        X3[:, dd, g * 512:(g + 1) * 512], pwo[bi][:],
                        atl("bo", dd), xtmp_b[:],
                        ALU.add, ALU.add), [ev2, evm, ev_atlas])
                    rd_prev[bi] = ev3
                prev_use[pi] = evm
            P.barrier()
            P.flush()
        if upto == "bwo":
            return store_out()

    with ExitStack() as esG:
        def sbG(name, shape, dt=F32):
            return esG.enter_context(nc.sbuf_tensor(name, list(shape), dt))

        ccr2 = sbG("g_cc", [1, S])
        row22 = sbG("g_rows2", [2, S])
        n2sb = sbG("g_n2", [2, D])
        x2ball = sbG("g_x2b", [128, ND * S], BF16)
        XB3 = x2ball[:].rearrange("p (c s) -> p c s", c=ND)
        xtmp_g = sbG("g_xtmp", [128, 512])
        rsc2 = sbG("g_rsc", [1, 768])
        ev_n2 = dS.issue(n2sb[:], n2wb)
        P.op("vector", lambda e: e.memset(row22[0:2, :], 1.0))
        with ExitStack() as esGs:
            psS = esGs.enter_context(nc.psum_tensor("g_psS", [1, 256], F32))
            pAB = esGs.enter_context(nc.psum_tensor("g_pAB", [128, 512], F32))
            pBB = esGs.enter_context(nc.psum_tensor("g_pBB", [128, 512], F32))
            x2f = [esGs.enter_context(nc.sbuf_tensor(f"g_x2f{i}", [128, D],
                                                     F32)) for i in range(2)]
            for tt in range(NT):
                pi = tt % 2
                x_ap = X3[:, :, tt * 128:(tt + 1) * 128]
                evs = P.op("scalar", lambda e, x_ap=x_ap, pi=pi: e.activation(
                    x2f[pi][:].rearrange("p (c n) -> p c n", c=ND), x_ap,
                    AF.Square, bias=0.0, scale=1.0),
                    [P.full("v"), P.full("t")])
                ev_m1 = ev_m2 = None
                for c in range(ND):
                    ev_m1 = P.op("tensor", lambda e, c=c, x_ap=x_ap: e.matmul(
                        psS[0:1, 0:128], ones_col32[:], x_ap[:, c],
                        start=(c == 0), stop=(c == ND - 1)), [P.full("v")])
                for c in range(ND):
                    ev_m2 = P.op("tensor", lambda e, c=c, pi=pi: e.matmul(
                        psS[0:1, 128:256], ones_col32[:],
                        x2f[pi][:, c * 128:(c + 1) * 128],
                        start=(c == 0), stop=(c == ND - 1)), [evs])
                sl = slice(tt * 128, (tt + 1) * 128)
                ev = P.op("vector", lambda e: e.tensor_scalar(
                    rsc2[:, 0:128], psS[0:1, 0:128], 1.0 / D, None,
                    ALU.mult), [ev_m1, ev_m2])
                ev = P.op("vector", lambda e: e.tensor_tensor(
                    rsc2[:, 128:256], rsc2[:, 0:128], rsc2[:, 0:128],
                    ALU.mult), [ev])
                ev = P.op("vector", lambda e: e.scalar_tensor_tensor(
                    rsc2[:, 256:384], psS[0:1, 128:256], 1.0 / D,
                    rsc2[:, 128:256], ALU.mult, ALU.subtract), [ev])
                ev = P.op("vector", lambda e: e.tensor_scalar(
                    rsc2[:, 384:512], rsc2[:, 256:384], EPS, None, ALU.add),
                    [ev])
                ev = P.op("vector", lambda e: e.reciprocal(
                    rsc2[:, 384:512], rsc2[:, 384:512]), [ev])
                evA1 = P.op("scalar", lambda e: e.activation(
                    rsc2[:, 512:640], rsc2[:, 384:512], AF.Sqrt, bias=0.0,
                    scale=1.0), [ev])
                evCC = P.op("vector", lambda e, sl=sl: e.tensor_copy(
                    ccr2[:, sl], rsc2[:, 512:640]), [evA1])
                P.op("vector", lambda e, sl=sl: e.scalar_tensor_tensor(
                    row22[0:1, sl], rsc2[:, 0:128], -1.0, ccr2[:, sl],
                    ALU.mult, ALU.mult), [evCC])
            for p in range(ND):
                for g in range(NQG):
                    evb1 = P.op("tensor", lambda e, p=p, g=g: e.matmul(
                        pAB[:], n2sb[0:1, p * 128:(p + 1) * 128],
                        ccr2[:, g * 512:(g + 1) * 512], start=True,
                        stop=True), [ev_n2, P.full("v")])
                    evb2 = P.op("tensor", lambda e, p=p, g=g: e.matmul(
                        pBB[:], n2sb[0:2, p * 128:(p + 1) * 128],
                        row22[0:2, g * 512:(g + 1) * 512], start=True,
                        stop=True), [P.full("v")])
                    ev1 = P.op("vector", lambda e, p=p, g=g: e.tensor_tensor(
                        xtmp_g[:], X3[:, p, g * 512:(g + 1) * 512], pAB[:],
                        ALU.mult), [evb1, P.full("v")])
                    P.op("vector", lambda e, p=p, g=g: e.tensor_tensor(
                        XB3[:, p, g * 512:(g + 1) * 512], xtmp_g[:], pBB[:],
                        ALU.add), [ev1, evb2])
            P.barrier()
            P.flush()
        gld3 = glu_dram.rearrange("(j p) s -> p j s", p=128)
        with ExitStack() as esGi:
            HHALF = HIDP
            wibuf = esGi.enter_context(nc.sbuf_tensor(
                "g_wibuf", [128, ND * HHALF], BF16))
            wi4 = wibuf[:].rearrange("p (c n) -> p c n", c=ND)
            su = esGi.enter_context(nc.sbuf_tensor("g_su", [128, 512], F32))
            g2v = esGi.enter_context(nc.sbuf_tensor("g_g2v", [128, 512],
                                                    F32))
            glub = esGi.enter_context(nc.sbuf_tensor("g_glub", [128, 512],
                                                     BF16))
            psUr = [esGi.enter_context(nc.psum_tensor(
                f"g_psU{i}", [128, 512], F32)) for i in range(2)]
            psG2r = [esGi.enter_context(nc.psum_tensor(
                f"g_psG2{i}", [128, 512], F32)) for i in range(2)]
            wiT3 = w_inT.rearrange("(c p) n -> p c n", p=128)
            ev_gs = None
            evgl = None
            it2 = 0
            evsu_bank = [None, None]
            evg2_bank = [None, None]
            for wh in range(2):
                ev_wi = dG.issue(wi4, wiT3[:, :, wh * HHALF:(wh + 1) * HHALF],
                                 waits=[P.full("t")])
                for j2 in range(wh * (NF2 // 2), (wh + 1) * (NF2 // 2)):
                  for g in range(NQG):
                    pi = it2 % 2
                    it2 += 1
                    cb = (2 * j2) * 128 - wh * HHALF
                    evmU = evmG = None
                    for c in range(ND):
                        evmU = P.op("tensor", lambda e, c=c, cb=cb, g=g,
                                    pi=pi: e.matmul(
                            psUr[pi][:], wi4[:, c, cb:cb + 128],
                            XB3[:, c, g * 512:(g + 1) * 512],
                            start=(c == 0), stop=(c == ND - 1)),
                            [ev_wi] + ([evsu_bank[pi]]
                                       if c == 0 and evsu_bank[pi] else []))
                    for c in range(ND):
                        evmG = P.op("tensor", lambda e, c=c, cb=cb, g=g,
                                    pi=pi: e.matmul(
                            psG2r[pi][:], wi4[:, c, cb + 128:cb + 256],
                            XB3[:, c, g * 512:(g + 1) * 512],
                            start=(c == 0), stop=(c == ND - 1)),
                            [ev_wi] + ([evg2_bank[pi]]
                                       if c == 0 and evg2_bank[pi] else []))
                    evsu = P.op("scalar", lambda e, j2=j2, pi=pi:
                                e.activation(
                        su[:], psUr[pi][:], AF.Silu, bias=atl("b_in", 2 * j2),
                        scale=1.0), [evmU, ev_atlas] +
                        ([evgl] if evgl else []))
                    evg2 = P.op("vector", lambda e, j2=j2, pi=pi:
                                e.tensor_scalar(
                        g2v[:], psG2r[pi][:], atl("b_in", 2 * j2 + 1), None,
                        ALU.add), [evmG, ev_atlas] +
                        ([evgl] if evgl else []))
                    evgl = P.op("vector", lambda e: e.tensor_tensor(
                        glub[:], su[:], g2v[:], ALU.mult),
                        [evsu, evg2] + ([ev_gs] if ev_gs else []))
                    ev_gs = dS.issue(gld3[:, j2, g * 512:(g + 1) * 512],
                                     glub[:], waits=[evgl])
                    evsu_bank[pi] = evsu
                    evg2_bank[pi] = evgl

            P.wait("vector", ev_gs)
            P.barrier()
            P.flush()
        with ExitStack() as esGo:
            wobuf = esGo.enter_context(nc.sbuf_tensor(
                "g_wobuf", [128, NF2 * D], BF16))
            wo4 = wobuf[:].rearrange("p (j n) -> p j n", j=NF2)
            glblk = [esGo.enter_context(nc.sbuf_tensor(
                f"g_glblk{i}", [128, 512], BF16)) for i in range(2)]
            pyO = [esGo.enter_context(nc.psum_tensor(
                f"g_pyO{i}", [128, 512], F32)) for i in range(ND)]
            ev_wog = dG.issue(wo4,
                              w_outT.rearrange("(j p) n -> p j n", p=128),
                              waits=[P.full("t")])
            evq_prev = [None, None]
            ev_xup = None
            for g in range(NQG):
                for j2 in range(NF2):
                    pi = j2 % 2
                    w0 = [evq_prev[pi]] if evq_prev[pi] else []
                    evl = dS.issue(glblk[pi][:],
                                   gld3[:, j2, g * 512:(g + 1) * 512],
                                   waits=w0)
                    for dd in range(ND):
                        evq_prev[pi] = P.op("tensor",
                                            lambda e, j2=j2, dd=dd, pi=pi:
                                            e.matmul(
                            pyO[dd][:], wo4[:, j2, dd * 128:(dd + 1) * 128],
                            glblk[pi][:], start=(j2 == 0),
                            stop=(j2 == NF2 - 1)), [evl, ev_wog] +
                            ([ev_xup] if j2 == 0 and ev_xup else []))
                for dd in range(ND):
                    ev_xup = P.op("vector", lambda e, dd=dd, g=g:
                                  e.scalar_tensor_tensor(
                        X3[:, dd, g * 512:(g + 1) * 512], pyO[dd][:],
                        atl("b_out", dd), X3[:, dd, g * 512:(g + 1) * 512],
                        ALU.add, ALU.add),
                        [evq_prev[0], evq_prev[1], P.full("v")])
            P.barrier()
            P.flush()

    with ExitStack() as esXa:
        xrb = [esXa.enter_context(nc.sbuf_tensor(f"xr{i}", [128, ND * 512],
                                                 F32)) for i in range(2)]
        xo3 = xpre_dram.rearrange("(c p) s -> p c s", p=128)
        prev_add = [None, None]
        for g in range(NQG):
            pi = g % 2
            w0 = [prev_add[pi]] if prev_add[pi] else []
            evl = dS.issue(xrb[pi][:].rearrange("p (c n) -> p c n", c=ND),
                           xo3[:, :, g * 512:(g + 1) * 512], waits=w0)
            prev_add[pi] = P.op("vector", lambda e, g=g, pi=pi:
                                e.tensor_tensor(
                X3[:, :, g * 512:(g + 1) * 512],
                X3[:, :, g * 512:(g + 1) * 512],
                xrb[pi][:].rearrange("p (c n) -> p c n", c=ND), ALU.add),
                [evl, P.full("v"), P.full("s")])
        P.barrier()
        P.flush()

    if upto == "attn_b":
        return store_out()
    ffn_phase("f2")
    return store_out()


def _stage_inputs(full):
    x = np.ascontiguousarray(full["L0_attn_a_x"])
    B, Sfull, _ = x.shape
    SQ = Sfull // 4
    wqkvT = np.ascontiguousarray(full["L0_attn_a_Wqkv"].T)
    wgateT = np.ascontiguousarray(full["L0_attn_a_Wgate"].T)
    woutT = np.ascontiguousarray(full["L0_attn_a_Wout"].T)
    posb = np.ascontiguousarray(full["L0_attn_a_pos_bias"])
    posb_rep = np.ascontiguousarray(
        np.tile(posb.T.reshape(1, -1), (128, 1)).astype(np.float32))
    w1T = np.ascontiguousarray(full["L0_ffn_w1"].T)
    w2T = np.ascontiguousarray(full["L0_ffn_w2"].T)
    wq_b = np.ascontiguousarray(full["L0_attn_b_wq"])
    wk_b = np.ascontiguousarray(full["L0_attn_b_wk"])
    wv_b = np.ascontiguousarray(full["L0_attn_b_wv"])
    wg_b = np.ascontiguousarray(full["L0_attn_b_wg"])
    wo_b = np.ascontiguousarray(full["L0_attn_b_wo"])
    w_in = full["L0_attn_b_w_in"]
    wu, wg2 = w_in[:, :HID], w_in[:, HID:]
    w_inT = np.zeros((D, 2 * HIDP), np.float32)
    b_in = full["L0_attn_b_b_in"]
    b_in_r = np.zeros(2 * HIDP, np.float32)
    for j in range(NF2):
        lo, hi = j * 128, min((j + 1) * 128, HID)
        w_inT[:, 2 * j * 128:2 * j * 128 + (hi - lo)] = wu[:, lo:hi]
        w_inT[:, (2 * j + 1) * 128:(2 * j + 1) * 128 + (hi - lo)] = wg2[:, lo:hi]
        b_in_r[2 * j * 128:2 * j * 128 + (hi - lo)] = b_in[lo:hi]
        b_in_r[(2 * j + 1) * 128:(2 * j + 1) * 128 + (hi - lo)] = b_in[HID + lo:HID + hi]
    w_out = full["L0_attn_b_w_out"]
    w_outT = np.zeros((HIDP, D), np.float32)
    w_outT[:HID] = w_out
    n2wb = np.stack([full["L0_attn_b_norm2_w"], full["L0_attn_b_norm2_b"]])
    n1wb = np.stack([full["L0_attn_b_norm1_w"], full["L0_attn_b_norm1_b"]])
    gnw = full["L0_attn_b_gn_w"]
    gnwm = np.zeros((1, 2 * D), np.float32)
    for p in range(NPAIR):
        gnwm[0, p * 128:p * 128 + 64] = gnw[p * 128:p * 128 + 64]
        gnwm[0, D + p * 128 + 64:D + (p + 1) * 128] = gnw[p * 128 + 64:
                                                          (p + 1) * 128]
    gnbrow = np.ascontiguousarray(full["L0_attn_b_gn_b"].reshape(1, D))

    def cols(v):
        return np.ascontiguousarray(v.reshape(-1, 128).T)

    atlasv = np.zeros((128, NB_COLS), np.float32)
    atlasv[:, ATL["bqkv"]:ATL["bqkv"] + 24] = np.concatenate(
        [cols(full["L0_attn_a_bqkv"][i * D:(i + 1) * D]) for i in range(3)],
        axis=1)
    atlasv[:, ATL["bout"]:ATL["bout"] + 8] = cols(full["L0_attn_a_bout"])
    atlasv[:, ATL["bgate"]:ATL["bgate"] + 8] = cols(full["L0_attn_a_bgate"])
    atlasv[:, ATL["bq"]:ATL["bq"] + 8] = cols(full["L0_attn_b_bq"])
    atlasv[:, ATL["bk"]:ATL["bk"] + 8] = cols(full["L0_attn_b_bk"])
    atlasv[:, ATL["bv"]:ATL["bv"] + 8] = cols(full["L0_attn_b_bv"])
    atlasv[:, ATL["bg"]:ATL["bg"] + 8] = cols(full["L0_attn_b_bg"])
    atlasv[:, ATL["bo"]:ATL["bo"] + 8] = cols(full["L0_attn_b_bo"])
    atlasv[:, ATL["gnw"]:ATL["gnw"] + 8] = cols(full["L0_attn_b_gn_w"])
    atlasv[:, ATL["gnb"]:ATL["gnb"] + 8] = cols(full["L0_attn_b_gn_b"])
    atlasv[:, ATL["n1w"]:ATL["n1w"] + 8] = cols(full["L0_attn_b_norm1_w"])
    atlasv[:, ATL["n1b"]:ATL["n1b"] + 8] = cols(full["L0_attn_b_norm1_b"])
    atlasv[:, ATL["n2w"]:ATL["n2w"] + 8] = cols(full["L0_attn_b_norm2_w"])
    atlasv[:, ATL["n2b"]:ATL["n2b"] + 8] = cols(full["L0_attn_b_norm2_b"])
    atlasv[:, ATL["b_in"]:ATL["b_in"] + 44] = cols(b_in_r)
    atlasv[:, ATL["b_out"]:ATL["b_out"] + 8] = cols(full["L0_attn_b_b_out"])
    atlasv[:, ATL["b1"]:ATL["b1"] + 32] = cols(full["L0_ffn_b1"])
    atlasv[:, ATL["b2"]:ATL["b2"] + 8] = cols(full["L0_ffn_b2"])

    in_maps = []
    for core in range(8):
        b, qt = divmod(core, 4)
        sl = x[b, qt * SQ:(qt + 1) * SQ]
        x_o = np.ascontiguousarray(sl.T)
        if qt == 0:
            x_h = np.zeros((D, HALO), np.float32)
        else:
            x_h = np.ascontiguousarray(
                x[b, qt * SQ - HALO:qt * SQ].T)
        ind = np.zeros((NOFF, SQ), np.float32)
        if qt == 0:
            n = np.arange(SQ)
            for oi, off in enumerate(OFFSETS):
                ind[oi] = (off > n).astype(np.float32)
        coef = np.zeros((1, 128), np.float32)
        for i in range(4):
            if i < qt:
                for h in range(NH):
                    coef[0, i * NH + h] = GAMMAS[h] ** (SQ * (qt - 1 - i))
        in_maps.append({
            "x_own": x_o, "x_halo": x_h, "wqkvT": wqkvT, "wgateT": wgateT,
            "woutT": woutT, "posb": posb, "posb_rep": posb_rep,
            "ind_bnd": ind,
            "bias_atlas": atlasv, "w1T": w1T, "w2T": w2T,
            "wq_b": wq_b, "wk_b": wk_b, "wv_b": wv_b, "wg_b": wg_b,
            "wo_b": wo_b, "w_inT": w_inT, "w_outT": w_outT,
            "coef_row": coef, "n2wb": np.ascontiguousarray(n2wb),
            "n1wb": np.ascontiguousarray(n1wb), "gnwm": gnwm,
            "gnbrow": gnbrow,
        })
    return in_maps


def kernel(**inputs):
    in_maps = _stage_inputs(inputs)
    nc = bass.Bass("TRN2", target_bir_lowering=False)
    build_program(nc, S=2048, upto="ffn2")
    res = run_bass_kernel_spmd(nc, in_maps, core_ids=list(range(8)))
    x = inputs["L0_attn_a_x"]
    B, Sfull, _ = x.shape
    SQ = Sfull // 4
    out = np.empty((B, Sfull, D), np.float32)
    for core in range(8):
        b, qt = divmod(core, 4)
        out[b, qt * SQ:(qt + 1) * SQ] = res.results[core]["y_out"].T
    return out
